# version 12
# baseline (speedup 1.0000x reference)
import numpy as np
import ml_dtypes
import concourse.bass as bass
import concourse.mybir as mybir
from concourse.bass_utils import run_bass_kernel_spmd

F32 = mybir.dt.float32
BF16 = mybir.dt.bfloat16
I16 = mybir.dt.int16
ALU = mybir.AluOpType
AF = mybir.ActivationFunctionType
AX = mybir.AxisListType
bfloat16 = ml_dtypes.bfloat16

B, S, D, H, HD = 4, 2048, 2048, 16, 128
TOK = 1024
KC = D // 128
GU, HF = 16384, 8192
M_THR = 0.501953125
EPS8 = float(np.float32(bfloat16(1e-8)))
LN_EPS = 1e-5
RMS_EPS = float(np.float32(np.finfo(np.float32).eps))
NEG_BIG = -1.0e30
SCL_ATT = 1.0 / float(np.sqrt(HD))
W_QT = [512, 1024, 1024, 1536, 1536, 2048, 2048, 2048]
MSK_W = [W_QT[i] - 256 * i for i in range(8)]
MSK_OFF = np.cumsum([0] + MSK_W).tolist()
MSK_TOT = MSK_OFF[-1]

_DBG_OUTS = ()
_DBG_PHASES = None
_DBG_QSTORE = None
_DBG_QLIMIT = None


class Prog:
    ENGS = ("pe", "dve", "act", "gp", "sync")
    NROT = 4
    DCLS = ("dS", "dG", "dC", "dO", "dQ")
    SEMS = ("sP", "sV", "sA", "sG") + tuple(
        f"{c}{r}" for c in ("dS", "dG", "dC", "dO", "dQ") for r in range(4))

    def __init__(self):
        self.recs = {e: [] for e in self.ENGS}
        self.cnt = {s: 0 for s in self.SEMS}
        self.rr = {c: 0 for c in self.DCLS}
        self.consts_ok = []

    def snap(self, cls):
        return [(f"{cls}{r}", self.cnt[f"{cls}{r}"]) for r in range(self.NROT)
                if self.cnt[f"{cls}{r}"] > 0]

    def emit(self, eng, fn, waits=(), inc=None, tag=""):
        waits = list(waits)
        if inc is not None and inc in self.DCLS:
            phys = f"{inc}{self.rr[inc] % self.NROT}"
            self.rr[inc] += 1
            if self.cnt[phys] > 0:
                waits.append((phys, self.cnt[phys]))
            inc = phys
        for (sem, v) in waits:
            assert v <= self.cnt[sem], f"wait on future {sem}={v}>{self.cnt[sem]} at {tag}"
        post = None
        if inc is not None:
            self.cnt[inc] += 16 if inc.startswith("d") else 1
            post = self.cnt[inc]
        self.recs[eng].append((tuple(waits), fn, inc, post, tag))
        return (inc, post) if inc is not None and inc.startswith("d") else post

    def render(self, block, sems):
        def rnd(eng_obj, recs, own_sem=None):
            last_own = 0
            for waits, fn, inc, post, tag in recs:
                if own_sem is not None and last_own > 0 and fn is not None:
                    eng_obj.wait_ge(sems[own_sem], last_own)
                for (sem, v) in waits:
                    if v > 0:
                        eng_obj.wait_ge(sems[sem], v)
                if fn is not None:
                    ins = fn(eng_obj)
                    if inc is not None:
                        ins.then_inc(sems[inc], 16 if inc.startswith("d") else 1)
                if inc == own_sem and post is not None:
                    last_own = post

        @block.tensor
        def _(t):
            rnd(t, self.recs["pe"])

        @block.vector
        def _(v):
            rnd(v, self.recs["dve"], own_sem="sV")

        @block.scalar
        def _(s):
            rnd(s, self.recs["act"], own_sem="sA")

        @block.gpsimd
        def _(g):
            rnd(g, self.recs["gp"], own_sem="sG")

        @block.sync
        def _(s):
            rnd(s, self.recs["sync"])


def _alloc(nc):
    T, DR = {}, {}
    DR["xT"] = nc.dram_tensor("xT", [D, S], F32, kind="ExternalInput").ap()
    for w in ("wqT", "wkT", "wvT", "woT"):
        DR[w] = nc.dram_tensor(w, [D, D], F32, kind="ExternalInput").ap()
    DR["wguT"] = nc.dram_tensor("wguT", [D, GU], F32, kind="ExternalInput").ap()
    DR["wpT"] = nc.dram_tensor("wpT", [HF, D], F32, kind="ExternalInput").ap()
    DR["qmask"] = nc.dram_tensor("qmask", [128, MSK_TOT], F32, kind="ExternalInput").ap()
    DR["b2"] = nc.dram_tensor("b2", [128, 2], BF16, kind="ExternalInput").ap()
    DR["c2"] = nc.dram_tensor("c2", [2, 128], BF16, kind="ExternalInput").ap()
    DR["ones1"] = nc.dram_tensor("ones1", [128, 1], BF16, kind="ExternalInput").ap()
    DR["ones1f"] = nc.dram_tensor("ones1f", [1, 128], F32, kind="ExternalInput").ap()
    DR["ident"] = nc.dram_tensor("ident", [128, 128], F32, kind="ExternalInput").ap()
    DR["out"] = nc.dram_tensor("out", [D, TOK], F32, kind="ExternalOutput").ap()
    for nm, shape, dt_ in (("wgu_hat", [D, GU], F32), ("wp_hat", [HF, D], F32),
                           ("k_scr", [D, S], F32), ("v_scr", [S, D], F32),
                           ("q_scr", [D, TOK], F32), ("x1_scr", [D, TOK], F32)):
        if nm in _DBG_OUTS:
            DR[nm] = nc.dram_tensor(nm, shape, dt_, kind="ExternalOutput").ap()
        else:
            DR[nm] = nc.dram_tensor(nm, shape, dt_).ap()

    al = nc.alloc_sbuf_tensor
    T["b2"] = al("b2_t", [128, 2], BF16)
    T["c2"] = al("c2_t", [2, 128], BF16)
    T["ones1"] = al("ones1_t", [128, 1], BF16)
    T["ones1f"] = al("ones1f_t", [1, 128], F32)
    T["negmax"] = al("negmax", [128, 1], F32)
    T["bias1"] = al("bias1", [128, 1], F32)
    T["rowsum"] = al("rowsum", [128, 1], F32)
    T["recip"] = al("recip", [128, 1], F32)

    BASE = 18432
    at = lambda nm, shape, dt_, offset: nc.alloc_sbuf_tensor_at(
        nm, shape, dt_, offset=BASE + offset)
    T["QW"] = at("QW", [128, 8192], BF16, 0)
    T["QABS"] = at("QABS", [128, 4096], BF16, 16384)
    T["QSG"] = at("QSG", [128, 4096], BF16, 24576)
    T["QT"] = at("QT", [128, 2048], BF16, 32768)
    T["QBB"] = at("QBB", [128, 2560], F32, 36864)
    T["QSCL"] = at("QSCL", [2, 4096], BF16, 47104)
    T["QLND"] = at("QLND", [128, 4096], I16, 55296)
    T["AR_A"] = at("AR_A", [128, 16384], F32, 0)
    T["AR_B"] = at("AR_B", [128, 12288], F32, 65536)
    T["AR_C"] = at("AR_C", [128, 8192], F32, 114688)
    T["AR_D"] = at("AR_D", [128, 8192], F32, 147456)
    T["AR_W"] = at("AR_W", [128, 4096], F32, 180224)
    T["LNXB"] = at("LNXB", [128, 1024], BF16, 196608)
    T["stg"] = at("stg", [128, 1024], F32, 198656)
    T["tmp"] = at("tmp", [128, 512], F32, 202752)
    T["ptb"] = at("ptb", [128, 256], F32, 204800)
    T["ident"] = at("ident_t", [128, 128], F32, 205824)
    T["stat1"] = at("stat1", [1, 512], F32, 206336)
    T["stat2"] = at("stat2", [1, 512], F32, 208384)
    T["HH"] = at("HH", [128, 16384], BF16, 0)
    T["HL"] = at("HL", [128, 16384], BF16, 32768)
    T["LN2H"] = at("LN2H", [128, 4096], BF16, 65536)
    T["LN2L"] = at("LN2L", [128, 4096], BF16, 65536 + 8192)
    T["AR_WB"] = at("AR_WB", [128, 8192], BF16, 180224)
    T["FSLAB"] = at("FSLAB", [128, 8192], F32, 65536)

    T["pbE"] = nc.alloc_psum_tensor("pbE", [128, 1024], F32)
    T["pbF"] = nc.alloc_psum_tensor("pbF", [128, 1024], F32)
    T["pbG"] = nc.alloc_psum_tensor("pbG", [128, 1024], F32)
    T["pbH"] = nc.alloc_psum_tensor("pbH", [128, 1024], F32)
    return T, DR


def _qtiles(T, i):
    s4 = slice((i % 4) * 2048, (i % 4) * 2048 + 2048)
    s2 = slice((i % 2) * 2048, (i % 2) * 2048 + 2048)
    return (T["QW"].ap()[:, s4], T["QABS"].ap()[:, s2], T["QSG"].ap()[:, s2],
            T["QT"].ap()[:, 0:2048], T["QSCL"].ap()[:, s2],
            T["QLND"].ap().bitcast(BF16)[:, s2])


def _quant_gen(P_, T, DR, consts_ok):
    FQ = 2048
    jobs = []
    for d0 in range(0, D, 128):
        for h0 in range(0, GU, FQ):
            jobs.append((DR["wguT"][d0:d0 + 128, h0:h0 + FQ],
                         DR["wgu_hat"][d0:d0 + 128, h0:h0 + FQ]))
    for h0 in range(0, HF, 128):
        for d0 in range(0, D, FQ):
            jobs.append((DR["wpT"][h0:h0 + 128, d0:d0 + FQ],
                         DR["wp_hat"][h0:h0 + 128, d0:d0 + FQ]))
    if _DBG_QLIMIT:
        jobs = jobs[:_DBG_QLIMIT]
    N = len(jobs)
    pbE, pbF = T["pbE"], T["pbF"]
    bcreg = [pbF[:, 0:512], pbF[:, 512:1024], pbE[:, 512:1024]]
    qbb = T["QBB"].ap()
    thrb = qbb[:, 0:512]
    qbct = [qbb[:, 512 + q * 512:1024 + q * 512] for q in range(4)]
    ld_tok, ab_tok, sg_tok = {}, {}, {}
    sc_tok, ev_tok, t_tok, tsg_tok = {}, {}, {}, {}
    d_tok, wf_tok, st_tok, bc3_tok = {}, {}, {}, {}

    def stage_a(i):
        w, absw, sg, tt, scl, laund = _qtiles(T, i)
        ws = [("sV", t_tok[(i - 2, 3)])] if (i - 2, 3) in t_tok else []
        ab_tok[i] = P_.emit("act", lambda e: e.activation(
            out=absw, in_=w, func=AF.Abs), waits=[ld_tok[i]] + ws,
            inc="sA", tag=f"q_ab{i}")
        ws = [("sG", wf_tok[i - 2])] if (i - 2) in wf_tok else []
        sg_tok[i] = P_.emit("act", lambda e: e.activation(
            out=sg, in_=w, func=AF.Sign), waits=ws, inc="sA", tag=f"q_sg{i}")
        for q in range(4):
            base = (0, 32, 64, 0)[q]
            ws = [("sA", ab_tok[i])] if q == 0 else []
            if i == 0 and q == 0:
                ws += list(consts_ok)
            prevq = (i, 0) if q == 3 else (i - 1, (3, 1, 2)[q])
            if prevq in sc_tok:
                ws.append(("sV", sc_tok[prevq]))
            mm = P_.emit("pe", lambda e, q=q, base=base: e.matmul(
                pbE[base:base + 2, 0:512], T["b2"][:],
                absw[:, q * 512:(q + 1) * 512], start=True, stop=True),
                waits=ws, inc="sP", tag=f"q_sum{i}.{q}")
            ws2 = [("sP", mm)]
            if (i - 2) in bc3_tok:
                ws2.append(("sP", bc3_tok[i - 2]))
            sc_tok[(i, q)] = P_.emit("dve", lambda e, q=q, base=base:
                e.tensor_scalar(
                    out=scl[:, q * 512:(q + 1) * 512],
                    in0=pbE[base:base + 2, 0:512],
                    scalar1=1.0 / 64, scalar2=EPS8, op0=ALU.mult, op1=ALU.max),
                waits=ws2, inc="sV", tag=f"q_sc{i}.{q}")

    def stage_b(i, src, dst):
        w, absw, sg, tt, scl, laund = _qtiles(T, i)
        for q in range(4):
            ws = [("sV", sc_tok[(i, q)])]
            if i == 0 and q == 0:
                ws += list(consts_ok)
            prev = (i, 0) if q == 3 else (i - 1, [3, 1, 2][q])
            if prev in ev_tok:
                ws.append(("sA", ev_tok[prev]))
            bc = P_.emit("pe", lambda e, q=q: e.matmul(
                bcreg[q % 3][:], T["c2"][:], scl[:, q * 512:(q + 1) * 512],
                start=True, stop=True), waits=ws, inc="sP", tag=f"q_bc{i}.{q}")
            if q == 3:
                bc3_tok[i] = bc
            ws = [("sP", bc)]
            if (i - 1, q) in tsg_tok:
                ws.append(("sV", tsg_tok[(i - 1, q)]))
            ev_tok[(i, q)] = P_.emit("act", lambda e, q=q: e.activation(
                out=qbct[q], in_=bcreg[q % 3][:], func=AF.Copy),
                waits=ws, inc="sA", tag=f"q_ev{i}.{q}")
        for q in range(4):
            th = P_.emit("dve", lambda e, q=q: e.tensor_scalar(
                out=thrb, in0=qbct[q], scalar1=M_THR, scalar2=None,
                op0=ALU.mult), waits=[("sA", ev_tok[(i, q)])],
                inc="sV", tag=f"q_th{i}.{q}")
            tw = [("sG", d_tok[i - 1])] if (q == 0 and (i - 1) in d_tok) \
                else []
            t_tok[(i, q)] = P_.emit("dve", lambda e, q=q: e.tensor_tensor(
                out=tt[:, q * 512:(q + 1) * 512],
                in0=absw[:, q * 512:(q + 1) * 512], in1=thrb, op=ALU.is_gt),
                waits=tw, inc="sV", tag=f"q_t{i}.{q}")
            g1w = [("sA", sg_tok[i])] if q == 0 else []
            P_.emit("dve", lambda e, q=q: e.tensor_tensor(
                out=tt[:, q * 512:(q + 1) * 512],
                in0=tt[:, q * 512:(q + 1) * 512],
                in1=sg[:, q * 512:(q + 1) * 512], op=ALU.mult),
                waits=g1w, inc="sV", tag=f"q_tsg{i}.{q}")
            tsg_tok[(i, q)] = P_.emit("dve", lambda e, q=q: e.tensor_tensor(
                out=tt[:, q * 512:(q + 1) * 512],
                in0=tt[:, q * 512:(q + 1) * 512],
                in1=qbct[q], op=ALU.mult),
                inc="sV", tag=f"q_wh{i}.{q}")
        d_tok[i] = P_.emit("gp", lambda e: e.tensor_tensor(
            out=sg, in0=tt, in1=w, op=ALU.subtract),
            waits=[("sV", tsg_tok[(i, 3)])], inc="sG", tag=f"q_d{i}")
        ws = [st_tok[i - 2]] if (i - 2) in st_tok else []
        wf = P_.emit("gp", lambda e: e.tensor_tensor(
            out=laund, in0=w, in1=sg, op=ALU.add),
            waits=ws, inc="sG", tag=f"q_wf{i}")
        wf_tok[i] = wf
        st_tok[i] = P_.emit("gp", lambda e: e.dma_start(
            out=dst, in_=laund), waits=[("sG", wf)], inc="dQ", tag=f"q_st{i}")

    def do_ld(j):
        if j >= N:
            return
        w = _qtiles(T, j)[0]
        ld_tok[j] = P_.emit("gp", lambda e: e.dma_start(
            out=w, in_=jobs[j][0]), inc="dG", tag=f"q_ld{j}")

    for i in range(N + 1):
        if i == 0:
            do_ld(0)
            do_ld(1)
        do_ld(i + 2)
        if i < N:
            stage_a(i)
        if i >= 1:
            stage_b(i - 1, *jobs[i - 1])
        if i < N:
            yield


def _ln_block(P_, T, src, ntok, eps, tagp, first_waits=()):
    assert ntok <= 512
    pbE = T["pbE"]
    mm1 = mm2 = None
    for k in range(KC):
        ws = list(first_waits) if k == 0 else [("sP", mm2)]
        if k == 0:
            ws.append(("sP", P_.cnt["sP"]))
            ws.extend(P_.consts_ok)
        cb = P_.emit("dve", lambda e, k=k: e.tensor_copy(
            out=T["LNXB"].ap()[:, 0:ntok], in_=src(k)), waits=ws,
            inc="sV", tag=f"{tagp}cb{k}")
        sq = P_.emit("act", lambda e, k=k: e.activation(
            out=T["LNXB"].ap()[:, 512:512 + ntok], in_=src(k), func=AF.Square),
            waits=(list(first_waits) if k == 0 else [("sP", mm2)]),
            inc="sA", tag=f"{tagp}sq{k}")
        mm1 = P_.emit("pe", lambda e, k=k: e.matmul(
            pbE[0:1, 0:ntok], T["ones1"][:], T["LNXB"].ap()[:, 0:ntok],
            start=(k == 0), stop=(k == KC - 1)),
            waits=[("sV", cb)], inc="sP", tag=f"{tagp}m1{k}")
        mm2 = P_.emit("pe", lambda e, k=k: e.matmul(
            pbE[0:1, 512:512 + ntok], T["ones1"][:],
            T["LNXB"].ap()[:, 512:512 + ntok],
            start=(k == 0), stop=(k == KC - 1)),
            waits=[("sA", sq)], inc="sP", tag=f"{tagp}m2{k}")
    a1 = P_.emit("dve", lambda e: e.tensor_scalar(
        out=T["stat1"].ap()[:, 0:ntok], in0=pbE[0:1, 0:ntok],
        scalar1=1.0 / D, scalar2=None, op0=ALU.mult),
        waits=[("sP", mm2)], inc="sV", tag=f"{tagp}mean")
    a2 = P_.emit("dve", lambda e: e.tensor_scalar(
        out=T["stat2"].ap()[:, 0:ntok], in0=pbE[0:1, 512:512 + ntok],
        scalar1=1.0 / D, scalar2=None, op0=ALU.mult),
        inc="sV", tag=f"{tagp}msq")
    m2 = P_.emit("act", lambda e: e.activation(
        out=pbE[0:1, 0:ntok], in_=T["stat1"].ap()[:, 0:ntok], func=AF.Square),
        waits=[("sV", a1)], inc="sA", tag=f"{tagp}m2")
    vr = P_.emit("dve", lambda e, eps=eps: e.scalar_tensor_tensor(
        out=T["stat2"].ap()[:, 0:ntok], in0=T["stat2"].ap()[:, 0:ntok],
        scalar=eps, in1=pbE[0:1, 0:ntok], op0=ALU.add, op1=ALU.subtract),
        waits=[("sA", m2), ("sV", a2)], inc="sV", tag=f"{tagp}var")
    sd = P_.emit("act", lambda e: e.activation(
        out=T["stat2"].ap()[:, 0:ntok], in_=T["stat2"].ap()[:, 0:ntok],
        func=AF.Sqrt), waits=[("sV", vr)], inc="sA", tag=f"{tagp}sd")
    rs = P_.emit("dve", lambda e: e.reciprocal(
        out=T["stat2"].ap()[:, 0:ntok], in_=T["stat2"].ap()[:, 0:ntok]),
        waits=[("sA", sd)], inc="sV", tag=f"{tagp}rstd")
    mr = P_.emit("dve", lambda e: e.tensor_tensor(
        out=T["stat1"].ap()[:, 0:ntok], in0=T["stat1"].ap()[:, 0:ntok],
        in1=T["stat2"].ap()[:, 0:ntok], op=ALU.mult),
        inc="sV", tag=f"{tagp}mr")
    b1 = P_.emit("pe", lambda e: e.matmul(
        pbE[:, 0:ntok], T["ones1f"][:], T["stat2"].ap()[:, 0:ntok],
        start=True, stop=True), waits=[("sV", rs), ("sV", vr)], inc="sP",
        tag=f"{tagp}bc1")
    b2v = P_.emit("pe", lambda e: e.matmul(
        pbE[:, 512:512 + ntok], T["ones1f"][:], T["stat1"].ap()[:, 0:ntok],
        start=True, stop=True), waits=[("sV", mr)], inc="sP", tag=f"{tagp}bc2")
    last = None
    for k in range(KC):
        P_.emit("dve", lambda e, k=k: e.tensor_tensor(
            out=src(k), in0=src(k), in1=pbE[:, 0:ntok], op=ALU.mult),
            waits=[("sP", b2v)] if k == 0 else [], inc="sV", tag=f"{tagp}ap1.{k}")
        last = P_.emit("dve", lambda e, k=k: e.tensor_tensor(
            out=src(k), in0=src(k), in1=pbE[:, 512:512 + ntok], op=ALU.subtract),
            inc="sV", tag=f"{tagp}ap2.{k}")
    return last


def _slab(P_, T, src_cols, half, waits, tag):
    return P_.emit("sync", lambda e: e.dma_start(
        out=T["AR_W"].ap()[:, half * 2048:(half + 1) * 2048].rearrange(
            "p (c t) -> p c t", c=16),
        in_=src_cols.rearrange("(c p) m -> p c m", p=128)),
        waits=waits, inc="dS", tag=tag)


def _attn1(P_, T, DR, qstep):
    pbG, pbH = T["pbG"], T["pbH"]
    lnC = lambda k: T["AR_C"].ap()[:, k * 512:(k + 1) * 512]
    slab_free = {0: 0, 1: 0}
    for tb in range(4):
        xl = P_.emit("sync", lambda e, tb=tb: e.dma_start(
            out=T["AR_C"].ap().rearrange("p (c t) -> p c t", c=16),
            in_=DR["xT"].rearrange("(c p) t -> p c t", p=128)[
                :, :, tb * 512:(tb + 1) * 512]),
            waits=[("sP", P_.cnt["sP"]), ("sV", P_.cnt["sV"])],
            inc="dS", tag=f"a1_x{tb}")
        ln_done = _ln_block(P_, T, lnC, 512, LN_EPS, f"a1ln{tb}_",
                            first_waits=[xl])
        ev_prev = {}
        st_prev = {}
        vst_prev = {}
        for kind in ("k", "v", "q"):
            wname = {"k": "wkT", "v": "wvT", "q": "wqT"}[kind]
            for m in range(16):
                half = m % 2
                sl = _slab(P_, T, DR[wname][:, m * 128:(m + 1) * 128], half,
                           waits=[("sP", slab_free[half])], tag=f"a1w{kind}{tb}.{m}")
                bank = {"k": pbG[:, 0:512], "v": pbG[:, 512:1024],
                        "q": pbH[:, 0:256]}[kind]
                mm = None
                for k in range(KC):
                    ws = [sl, ("sV", ln_done)] if k == 0 else []
                    if k == 0 and kind in ev_prev:
                        ws.append(("sA", ev_prev[kind]))
                    if kind == "q":
                        rhs = (T["AR_C"].ap()[:, k * 512:(k + 1) * 512]
                               .rearrange("p (a b t) -> p a b t", a=2, b=2)
                               [:, :, 0:1, :])
                    else:
                        rhs = T["AR_C"].ap()[:, k * 512:(k + 1) * 512]
                    mm = P_.emit("pe", lambda e, bank=bank, half=half, k=k, rhs=rhs:
                                 e.matmul(bank,
                                          T["AR_W"].ap()[:, half * 2048 + k * 128:
                                                         half * 2048 + (k + 1) * 128],
                                          rhs, start=(k == 0), stop=(k == KC - 1)),
                                 waits=ws, inc="sP", tag=f"a1{kind}{tb}.{m}.{k}")
                slab_free[half] = mm
                if kind in ("k", "q"):
                    sh, n = (0, 512) if kind == "k" else (1, 256)
                    evw = [("sP", mm)]
                    if kind in st_prev:
                        evw.append(st_prev[kind])
                    ev = P_.emit("act", lambda e, bank=bank, sh=sh, n=n: e.activation(
                        out=T["stg"].ap()[:, sh * 512:sh * 512 + n], in_=bank,
                        func=AF.Copy), waits=evw, inc="sA",
                        tag=f"a1ev{kind}{tb}.{m}")
                    dst = (DR["k_scr"][m * 128:(m + 1) * 128, tb * 512:(tb + 1) * 512]
                           if kind == "k" else
                           DR["q_scr"][m * 128:(m + 1) * 128, tb * 256:(tb + 1) * 256])
                    stv = P_.emit("act", lambda e, dst=dst, sh=sh, n=n: e.dma_start(
                        out=dst, in_=T["stg"].ap()[:, sh * 512:sh * 512 + n]),
                        waits=[], inc="dC", tag=f"a1st{kind}{tb}.{m}")
                    st_prev[kind] = stv
                    ev_prev[kind] = ev
                else:
                    evw = [("sP", mm)]
                    if "v_tr" in st_prev:
                        evw.append(("sP", st_prev["v_tr"]))
                    ev = P_.emit("act", lambda e, bank=bank: e.activation(
                        out=T["stg"].ap()[:, 512:1024], in_=bank, func=AF.Copy),
                        waits=evw, inc="sA", tag=f"a1evv{tb}.{m}")
                    ev_prev[kind] = ev
                    tr_prev = None
                    for kt in range(4):
                        trb = pbH[:, 512:640]
                        ws = [("sA", ev)] + list(P_.consts_ok)
                        if tr_prev is not None:
                            ws.append(("sV", tr_prev))
                        elif vst_prev:
                            ws.append(("sV", vst_prev["last_te"]))
                        tr = P_.emit("pe", lambda e, trb=trb, kt=kt: e.transpose(
                            trb, T["stg"].ap()[:, 512 + kt * 128:512 + (kt + 1) * 128],
                            T["ident"][:]), waits=ws, inc="sP",
                            tag=f"a1tr{tb}.{m}.{kt}")
                        pcol = 128 * (kt % 2)
                        tvw = [("sP", tr)]
                        if (m, kt - 2) in vst_prev:
                            tvw.append(vst_prev[(m, kt - 2)])
                        elif (m - 1, kt + 2) in vst_prev:
                            tvw.append(vst_prev[(m - 1, kt + 2)])
                        tvv = P_.emit("dve", lambda e, trb=trb, pcol=pcol:
                                      e.tensor_copy(
                                          out=T["ptb"].ap()[:, pcol:pcol + 128],
                                          in_=trb),
                                      waits=tvw, inc="sV", tag=f"a1te{tb}.{m}.{kt}")
                        tr_prev = tvv
                        stv = P_.emit(
                            "act", lambda e, tb=tb, m=m, kt=kt, pcol=pcol:
                            e.dma_start(
                                out=DR["v_scr"][tb * 512 + kt * 128:
                                                tb * 512 + (kt + 1) * 128,
                                                m * 128:(m + 1) * 128],
                                in_=T["ptb"].ap()[:, pcol:pcol + 128]),
                            waits=[("sV", tvv)], inc="dC", tag=f"a1sv{tb}.{m}.{kt}")
                        vst_prev[(m, kt)] = stv
                        vst_prev["last_st"] = stv
                        vst_prev["last_te"] = tvv
                    st_prev["v_tr"] = tr
                qstep()


def _attn2(P_, T, DR):
    BAP = T["AR_B"].ap()
    kh = BAP[:, 0:2048]
    vT = BAP[:, 2048:4096]
    qh = BAP[:, 4096:5120]
    prob = BAP[:, 5120:7168]
    msk = BAP[:, 7168:7168 + MSK_TOT]
    pbE, pbF, pbG, pbH = T["pbE"], T["pbF"], T["pbG"], T["pbH"]

    a1_done = P_.snap("dC")
    quant_bar = P_.snap("dQ") + P_.snap("dG") + [("sG", P_.cnt["sG"]),
                                                 ("sV", P_.cnt["sV"]),
                                                 ("sA", P_.cnt["sA"])]
    ml = P_.emit("sync", lambda e: e.dma_start(out=msk, in_=DR["qmask"]),
                 waits=list(quant_bar), inc="dS", tag="a2_msk")
    last_mm_on = {"kh": 0, "vT": 0, "qh": 0}
    for h in range(H):
        ws = (list(a1_done) + [ml] + list(quant_bar)) if h == 0 else []
        lk = P_.emit("sync", lambda e, h=h: e.dma_start(
            out=kh, in_=DR["k_scr"][h * 128:(h + 1) * 128, :]),
            waits=ws + [("sP", last_mm_on["kh"])], inc="dS", tag=f"a2lk{h}")
        lv = P_.emit("sync", lambda e, h=h: e.dma_start(
            out=vT.rearrange("p (c t) -> p c t", c=16),
            in_=DR["v_scr"].rearrange("(c p) d -> p c d", p=128)[
                :, :, h * 128:(h + 1) * 128]),
            waits=[("sP", last_mm_on["vT"])] if h else list(a1_done),
            inc="dS", tag=f"a2lv{h}")
        lq = P_.emit("sync", lambda e, h=h: e.dma_start(
            out=qh, in_=DR["q_scr"][h * 128:(h + 1) * 128, :]),
            waits=[("sP", last_mm_on["qh"])] if h else list(a1_done),
            inc="dS", tag=f"a2lq{h}")
        for qt in range(8):
            W = W_QT[qt]
            scb = [(pbE, 0), (pbE, 512), (pbF, 0), (pbF, 512)]
            mm = None
            for kb in range(W // 512):
                t_, c_ = scb[kb]
                ws = []
                if kb == 0:
                    ws = [lk, lq, ("sA", P_.cnt["sA"])]
                mm = P_.emit("pe", lambda e, t_=t_, c_=c_, kb=kb, qt=qt: e.matmul(
                    t_[:, c_:c_ + 512],
                    qh[:, qt * 128:(qt + 1) * 128],
                    kh[:, kb * 512:(kb + 1) * 512],
                    start=True, stop=True), waits=ws, inc="sP",
                    tag=f"a2sc{h}.{qt}.{kb}")
            lo, hi = 256 * qt, W
            mvs = []
            off = MSK_OFF[qt]
            seg = lo
            while seg < hi:
                segend = min(hi, (seg // 1024 + 1) * 1024)
                t_ = pbE if seg < 1024 else pbF
                c_ = seg % 1024
                n = segend - seg
                moff = off + (seg - lo)
                mvs.append(P_.emit("dve", lambda e, t_=t_, c_=c_, n=n, moff=moff:
                                   e.tensor_tensor(
                                       out=t_[:, c_:c_ + n], in0=t_[:, c_:c_ + n],
                                       in1=msk[:, moff:moff + n], op=ALU.add),
                                   waits=[("sP", mm)], inc="sV",
                                   tag=f"a2mk{h}.{qt}.{seg}"))
                seg = segend
            if W <= 1024:
                rmx = P_.emit("dve", lambda e, W=W: e.tensor_reduce(
                    out=T["negmax"][:], in_=pbE[:, 0:W], axis=AX.X, op=ALU.max,
                    negate=True), waits=[("sV", mvs[-1])], inc="sV",
                    tag=f"a2mx{h}.{qt}")
            else:
                P_.emit("dve", lambda e: e.tensor_reduce(
                    out=T["negmax"][:], in_=pbE[:, 0:1024], axis=AX.X, op=ALU.max,
                    negate=True), waits=[("sV", mvs[-1])], inc="sV",
                    tag=f"a2mxa{h}.{qt}")
                P_.emit("dve", lambda e, W=W: e.tensor_reduce(
                    out=T["bias1"][:], in_=pbF[:, 0:W - 1024], axis=AX.X, op=ALU.max,
                    negate=True), inc="sV", tag=f"a2mxb{h}.{qt}")
                rmx = P_.emit("dve", lambda e: e.tensor_tensor(
                    out=T["negmax"][:], in0=T["negmax"][:], in1=T["bias1"][:],
                    op=ALU.min), inc="sV", tag=f"a2mxc{h}.{qt}")
            bs = P_.emit("dve", lambda e: e.tensor_scalar(
                out=T["bias1"][:], in0=T["negmax"][:], scalar1=SCL_ATT,
                scalar2=None, op0=ALU.mult), waits=[("sV", rmx)], inc="sV",
                tag=f"a2bs{h}.{qt}")
            if W <= 1024:
                ex = P_.emit("act", lambda e, W=W: e.activation(
                    out=prob[:, 0:W], in_=pbE[:, 0:W], func=AF.Exp,
                    bias=T["bias1"][:], scale=SCL_ATT, accum_out=T["rowsum"][:]),
                    waits=[("sV", bs)], inc="sA", tag=f"a2ex{h}.{qt}")
                rsv = P_.emit("dve", lambda e: e.reciprocal(
                    out=T["recip"][:], in_=T["rowsum"][:]),
                    waits=[("sA", ex)], inc="sV", tag=f"a2rc{h}.{qt}")
            else:
                P_.emit("act", lambda e: e.activation(
                    out=prob[:, 0:1024], in_=pbE[:, 0:1024], func=AF.Exp,
                    bias=T["bias1"][:], scale=SCL_ATT, accum_out=T["rowsum"][:]),
                    waits=[("sV", bs)], inc="sA", tag=f"a2exa{h}.{qt}")
                ex = P_.emit("act", lambda e, W=W: e.activation(
                    out=prob[:, 1024:W], in_=pbF[:, 0:W - 1024], func=AF.Exp,
                    bias=T["bias1"][:], scale=SCL_ATT, accum_out=T["negmax"][:]),
                    inc="sA", tag=f"a2exb{h}.{qt}")
                sm = P_.emit("dve", lambda e: e.tensor_tensor(
                    out=T["rowsum"][:], in0=T["rowsum"][:], in1=T["negmax"][:],
                    op=ALU.add), waits=[("sA", ex)], inc="sV", tag=f"a2sm{h}.{qt}")
                rsv = P_.emit("dve", lambda e: e.reciprocal(
                    out=T["recip"][:], in_=T["rowsum"][:]), inc="sV",
                    tag=f"a2rc{h}.{qt}")
            pn = P_.emit("dve", lambda e, W=W: e.tensor_scalar(
                out=prob[:, 0:W], in0=prob[:, 0:W], scalar1=T["recip"][:],
                scalar2=None, op0=ALU.mult), waits=[("sV", rsv)], inc="sV",
                tag=f"a2pn{h}.{qt}")
            abank = pbH[:, 0:128] if qt % 2 == 0 else pbH[:, 512:640]
            te_prev = {}
            av_prev = {}
            mm_av = None
            for kc in range(W // 128):
                trb = pbG[:, 0:128] if kc % 2 == 0 else pbG[:, 512:640]
                ws = [("sV", pn)] + list(P_.consts_ok)
                if kc >= 2:
                    ws.append(("sV", te_prev[kc % 2]))
                tr = P_.emit("pe", lambda e, trb=trb, kc=kc: e.transpose(
                    trb, prob[:, kc * 128:(kc + 1) * 128], T["ident"][:]),
                    waits=ws, inc="sP", tag=f"a2tr{h}.{qt}.{kc}")
                pcol = 128 * (kc % 2)
                tew = [("sP", tr)]
                if kc - 2 in av_prev:
                    tew.append(("sP", av_prev[kc - 2]))
                te = P_.emit("dve", lambda e, trb=trb, pcol=pcol: e.tensor_copy(
                    out=T["ptb"].ap()[:, pcol:pcol + 128], in_=trb),
                    waits=tew, inc="sV", tag=f"a2te{h}.{qt}.{kc}")
                te_prev[kc % 2] = te
                ws2 = [("sV", te), lv]
                mm_av = P_.emit("pe", lambda e, kc=kc, pcol=pcol, abank=abank, W=W:
                                e.matmul(abank, vT[:, kc * 128:(kc + 1) * 128],
                                         T["ptb"].ap()[:, pcol:pcol + 128],
                                         start=(kc == 0), stop=(kc == W // 128 - 1)),
                                waits=ws2, inc="sP", tag=f"a2av{h}.{qt}.{kc}")
                av_prev[kc] = mm_av
            at_ = T["AR_C"] if h < 8 else T["AR_D"]
            hc = h % 8
            P_.emit("act", lambda e, hc=hc, qt=qt, abank=abank, at_=at_:
                    e.activation(
                        out=at_.ap()[:, hc * 1024 + qt * 128:
                                     hc * 1024 + (qt + 1) * 128],
                        in_=abank, func=AF.Copy), waits=[("sP", mm_av)], inc="sA",
                    tag=f"a2ae{h}.{qt}")
        last_mm_on = {"kh": P_.cnt["sP"], "vT": P_.cnt["sP"], "qh": P_.cnt["sP"]}


def _attn3(P_, T, DR):
    pbG = T["pbG"]
    oB = lambda m: T["AR_B"].ap()[:, m * 512:(m + 1) * 512]

    def a_ap(k, tb):
        at_ = T["AR_C"] if k < 8 else T["AR_D"]
        return at_.ap()[:, (k % 8) * 1024 + tb * 512:(k % 8) * 1024 + (tb + 1) * 512]

    slab_free = {0: P_.cnt["sP"], 1: P_.cnt["sP"]}
    for tb in range(2):
        a2_done = [("sA", P_.cnt["sA"])] if tb == 0 else P_.snap("dC")
        ev_prev = {}
        for m in range(16):
            half = m % 2
            sl = _slab(P_, T, DR["woT"][:, m * 128:(m + 1) * 128], half,
                       waits=[("sP", slab_free[half])], tag=f"a3w{tb}.{m}")
            bank = (pbG[:, 0:512], pbG[:, 512:1024])[m % 2]
            mm = None
            for k in range(KC):
                ws = ([sl] + (list(a2_done) if m == 0 else [])) if k == 0 else []
                if k == 0 and (m % 2) in ev_prev:
                    ws.append(("sA", ev_prev[m % 2]))
                if k == 0 and m == 0 and tb == 1:
                    ws.extend(P_.snap("dC"))
                mm = P_.emit("pe", lambda e, bank=bank, half=half, k=k, tb=tb:
                             e.matmul(bank,
                                      T["AR_W"].ap()[:, half * 2048 + k * 128:
                                                     half * 2048 + (k + 1) * 128],
                                      a_ap(k, tb),
                                      start=(k == 0), stop=(k == KC - 1)),
                             waits=ws, inc="sP", tag=f"a3o{tb}.{m}.{k}")
            slab_free[half] = mm
            ev_prev[m % 2] = P_.emit("act", lambda e, bank=bank, m=m: e.activation(
                out=oB(m), in_=bank, func=AF.Copy), waits=[("sP", mm)], inc="sA",
                tag=f"a3ev{tb}.{m}")
        ln_done = _ln_block(P_, T, oB, 512, LN_EPS, f"a3ln{tb}_",
                            first_waits=[("sA", P_.cnt["sA"])])
        ad_prev = {}
        for k in range(KC):
            sh = k % 2
            xlw = ([("sV", P_.cnt["sV"])] + P_.snap("dC")) if k < 2 else \
                  [("sV", ad_prev[k - 2])]
            xl = P_.emit("sync", lambda e, k=k, tb=tb, sh=sh: e.dma_start(
                out=T["stg"].ap()[:, sh * 512:(sh + 1) * 512].rearrange(
                    "p (a b t) -> p a b t", a=4, b=1),
                in_=DR["xT"][k * 128:(k + 1) * 128, :].rearrange(
                    "p (a b t) -> p a b t", a=8, b=2)[:, 4 * tb:4 * tb + 4, 0:1, :]),
                waits=xlw, inc="dS", tag=f"a3x{tb}.{k}")
            ad = P_.emit("dve", lambda e, k=k, sh=sh: e.tensor_tensor(
                out=oB(k), in0=oB(k), in1=T["stg"].ap()[:, sh * 512:(sh + 1) * 512],
                op=ALU.add),
                waits=[xl, ("sV", ln_done)], inc="sV", tag=f"a3x1{tb}.{k}")
            ad_prev[k] = ad
            P_.emit("act", lambda e, k=k, tb=tb: e.dma_start(
                out=DR["x1_scr"][k * 128:(k + 1) * 128, tb * 512:(tb + 1) * 512],
                in_=oB(k)), waits=[("sV", ad)], inc="dC", tag=f"a3s1{tb}.{k}")


def _ffn(P_, T, DR):
    pbE, pbF, pbG, pbH = T["pbE"], T["pbF"], T["pbG"], T["pbH"]
    lnD = lambda k: T["AR_D"].ap()[:, k * 256:(k + 1) * 256]
    lnY = lambda k: T["AR_D"].ap()[:, 4096 + k * 256: 4096 + (k + 1) * 256]
    hf32 = lambda c: T["AR_A"].ap()[:, c * 256:(c + 1) * 256]
    fsl = lambda q: T["FSLAB"].ap()[:, q * 2048:(q + 1) * 2048]

    x1_ok = P_.snap("dC") + P_.snap("dQ")
    for fb in range(4):
        xl = P_.emit("sync", lambda e, fb=fb: e.dma_start(
            out=T["AR_D"].ap()[:, 0:4096].rearrange("p (c t) -> p c t", c=16),
            in_=DR["x1_scr"].rearrange("(c p) t -> p c t", p=128)[
                :, :, fb * 256:(fb + 1) * 256]),
            waits=list(x1_ok) + [("sV", P_.cnt["sV"]), ("sP", P_.cnt["sP"])]
                  + P_.snap("dO"),
            inc="dS", tag=f"f_x1{fb}")
        ln2_done = _ln_block(P_, T, lnD, 256, LN_EPS, f"fln{fb}_",
                             first_waits=[xl])
        slabs = {}
        hsq_mm = None
        for c in range(64):
            r = c % 2
            gbank = (pbG[:, 0:256], pbG[:, 512:768])[r]
            ubank = (pbH[:, 0:256], pbH[:, 512:768])[r]
            wsl = [("sP", slabs[r])] if r in slabs else                 [("sP", P_.cnt["sP"]), ("sV", P_.cnt["sV"]),
                 ("sA", P_.cnt["sA"])] + list(x1_ok)
            sl_g = P_.emit("sync", lambda e, c=c, r=r: e.dma_start(
                out=fsl(2 * r).rearrange("p (kk t) -> p kk t", kk=16),
                in_=DR["wgu_hat"].rearrange("(kk p) m -> p kk m", p=128)[
                    :, :, c * 128:(c + 1) * 128]),
                waits=wsl, inc="dS", tag=f"fwg{fb}.{c}")
            sl_u = P_.emit("sync", lambda e, c=c, r=r: e.dma_start(
                out=fsl(2 * r + 1).rearrange("p (kk t) -> p kk t", kk=16),
                in_=DR["wgu_hat"].rearrange("(kk p) m -> p kk m", p=128)[
                    :, :, 8192 + c * 128:8192 + (c + 1) * 128]),
                inc="dS", tag=f"fwu{fb}.{c}")
            mmu = None
            for k in range(KC):
                ws = [sl_g, sl_u, ("sV", ln2_done)] if k == 0 else []
                P_.emit("pe", lambda e, gbank=gbank, r=r, k=k:
                        e.matmul(gbank, fsl(2 * r)[:, k * 128:(k + 1) * 128],
                                 lnD(k), start=(k == 0), stop=(k == KC - 1)),
                        waits=ws, inc="sP", tag=f"fg{fb}.{c}.{k}")
                mmu = P_.emit("pe", lambda e, ubank=ubank, r=r, k=k: e.matmul(
                    ubank, fsl(2 * r + 1)[:, k * 128:(k + 1) * 128],
                    lnD(k), start=(k == 0), stop=(k == KC - 1)),
                    inc="sP", tag=f"fu{fb}.{c}.{k}")
            slabs[r] = mmu
            sv = P_.emit("act", lambda e, gbank=gbank: e.activation(
                out=T["tmp"].ap()[:, 0:256], in_=gbank, func=AF.Silu),
                waits=[("sP", mmu)], inc="sA", tag=f"fsilu{fb}.{c}")
            hprev = [("sP", hsq_mm)] if c >= 1 else []
            ht = P_.emit("dve", lambda e, ubank=ubank, c=c: e.tensor_tensor(
                out=hf32(c), in0=T["tmp"].ap()[:, 0:256],
                in1=ubank, op=ALU.mult),
                waits=[("sA", sv)], inc="sV", tag=f"fht{fb}.{c}")
            hq = P_.emit("act", lambda e, c=c: e.activation(
                out=T["LNXB"].ap()[:, 0:256], in_=hf32(c), func=AF.Square),
                waits=[("sV", ht)] + hprev, inc="sA", tag=f"fhsq{fb}.{c}")
            hsq_mm = P_.emit("pe", lambda e, c=c: e.matmul(
                pbE[0:1, 0:256], T["ones1"][:], T["LNXB"].ap()[:, 0:256],
                start=(c == 0), stop=(c == 63)),
                waits=[("sA", hq)], inc="sP", tag=f"fhs{fb}.{c}")
        r1 = P_.emit("dve", lambda e: e.tensor_scalar(
            out=T["stat2"].ap()[:, 0:256], in0=pbE[0:1, 0:256], scalar1=1.0 / HF,
            scalar2=RMS_EPS, op0=ALU.mult, op1=ALU.add),
            waits=[("sP", hsq_mm)], inc="sV", tag=f"fr1{fb}")
        r2 = P_.emit("act", lambda e: e.activation(
            out=T["stat2"].ap()[:, 0:256], in_=T["stat2"].ap()[:, 0:256],
            func=AF.Sqrt), waits=[("sV", r1)], inc="sA", tag=f"fr2{fb}")
        r3 = P_.emit("dve", lambda e: e.reciprocal(
            out=T["stat2"].ap()[:, 0:256], in_=T["stat2"].ap()[:, 0:256]),
            waits=[("sA", r2)], inc="sV", tag=f"fr3{fb}")
        r4 = P_.emit("pe", lambda e: e.matmul(
            pbE[:, 512:768], T["ones1f"][:], T["stat2"].ap()[:, 0:256],
            start=True, stop=True), waits=[("sV", r3)], inc="sP", tag=f"fr4{fb}")
        r5 = P_.emit("dve", lambda e: e.tensor_copy(
            out=T["tmp"].ap()[:, 0:256], in_=pbE[:, 512:768]),
            waits=[("sP", r4)], inc="sV", tag=f"fr5{fb}")
        banks = [pbG[:, 0:256], pbG[:, 512:768], pbH[:, 0:256], pbH[:, 512:768],
                 pbE[:, 0:256], pbE[:, 512:768], pbF[:, 0:256], pbF[:, 512:768]]
        yv = {}
        for sw in range(2):
            for dm in range(8):
                dc = sw * 8 + dm
                bank = banks[dm]
                mm = None
                for ss in range(4):
                    half = ss % 2
                    wsl = [("sP", P_.cnt["sP"] if (sw or dm or ss) else 0)]
                    if dc == 0 and ss == 0:
                        wsl = [("sV", r5), ("sP", P_.cnt["sP"])]
                    sl = P_.emit("sync", lambda e, dc=dc, ss=ss, half=half:
                                 e.dma_start(
                                     out=fsl(half).rearrange(
                                         "p (kk t) -> p kk t", kk=16),
                                     in_=DR["wp_hat"].rearrange(
                                         "(kk p) m -> p kk m", p=128)[
                                         :, ss * 16:(ss + 1) * 16,
                                         dc * 128:(dc + 1) * 128]),
                                 waits=wsl, inc="dS", tag=f"fpw{fb}.{dc}.{ss}")
                    for kk in range(16):
                        hk = ss * 16 + kk
                        ws = [sl] if kk == 0 else []
                        if dm == 0 and ss == 0 and kk == 0 and sw == 1:
                            ws.append(("sV", yv[7]))
                        mm = P_.emit("pe", lambda e, bank=bank, half=half,
                                     kk=kk, hk=hk: e.matmul(
                                         bank, fsl(half)[:, kk * 128:
                                                         (kk + 1) * 128],
                                         hf32(hk), start=(hk == 0),
                                         stop=(hk == 63)),
                                     waits=ws, inc="sP",
                                     tag=f"fp{fb}.{dc}.{hk}")
                yv[dm] = P_.emit("dve", lambda e, bank=bank, dc=dc: e.tensor_tensor(
                    out=lnY(dc), in0=bank, in1=T["tmp"].ap()[:, 0:256],
                    op=ALU.mult), waits=[("sP", mm)], inc="sV", tag=f"fy{fb}.{dc}")
        ln_done = _ln_block(P_, T, lnY, 256, LN_EPS, f"fyln{fb}_",
                            first_waits=[("sV", yv[7])])
        ad_prev = {}
        for k in range(KC):
            sh = k % 2
            xlw = ([("sV", P_.cnt["sV"])] + P_.snap("dC")) if k < 2 else \
                  [("sV", ad_prev[k - 2])]
            xl = P_.emit("sync", lambda e, k=k, fb=fb, sh=sh: e.dma_start(
                out=T["stg"].ap()[:, sh * 512:sh * 512 + 256],
                in_=DR["x1_scr"][k * 128:(k + 1) * 128, fb * 256:(fb + 1) * 256]),
                waits=xlw, inc="dS", tag=f"fx1{fb}.{k}")
            ad = P_.emit("dve", lambda e, k=k, sh=sh: e.tensor_tensor(
                out=lnY(k), in0=lnY(k), in1=T["stg"].ap()[:, sh * 512:sh * 512 + 256],
                op=ALU.add), waits=[xl, ("sV", ln_done)], inc="sV",
                tag=f"fx2{fb}.{k}")
            ad_prev[k] = ad
            P_.emit("sync", lambda e, k=k, fb=fb: e.dma_start(
                out=DR["out"][k * 128:(k + 1) * 128, fb * 256:(fb + 1) * 256],
                in_=lnY(k)), waits=[("sV", ad)], inc="dO", tag=f"fo{fb}.{k}")


def build_program():
    nc = bass.Bass("TRN2", target_bir_lowering=False)
    T, DR = _alloc(nc)
    P_ = Prog()
    phases = _DBG_PHASES or ("quant", "attn1", "attn2", "attn3", "ffn")

    import contextlib
    with contextlib.ExitStack() as stack:
        sems = {nm: stack.enter_context(nc.semaphore(nm)) for nm in Prog.SEMS}
        block = stack.enter_context(nc.Block())
        for nm in ("b2", "c2", "ones1", "ones1f", "ident"):
            P_.emit("sync", lambda e, nm=nm: e.dma_start(out=T[nm][:], in_=DR[nm]),
                    inc="dS", tag=f"c_{nm}")
        P_.consts_ok = P_.snap("dS")
        qg = iter(())
        if "quant" in phases:
            qg = _quant_gen(P_, T, DR, P_.consts_ok)

        def qstep():
            next(qg, None)

        if "attn1" in phases:
            _attn1(P_, T, DR, qstep)
        for _ in qg:
            pass
        if "attn2" in phases:
            _attn2(P_, T, DR)
        if "attn3" in phases:
            _attn3(P_, T, DR)
        if "ffn" in phases:
            _ffn(P_, T, DR)
        fin = []
        for cls in Prog.DCLS:
            fin.extend(P_.snap(cls))
        P_.emit("sync", None, waits=fin, tag="fin")
        P_.render(block, sems)
    return nc


def _host_inputs(inputs):
    x = np.asarray(inputs["L0_attn_x"], dtype=np.float32)
    wq = np.asarray(inputs["L0_attn_wq"], dtype=np.float32)
    wk = np.asarray(inputs["L0_attn_wk"], dtype=np.float32)
    wv = np.asarray(inputs["L0_attn_wv"], dtype=np.float32)
    wo = np.asarray(inputs["L0_attn_wo"], dtype=np.float32)
    wgu = np.asarray(inputs["L0_ffn_w_gate_up"], dtype=np.float32)
    wp = np.asarray(inputs["L0_ffn_w_proj"], dtype=np.float32)

    wqT = np.ascontiguousarray(wq.T)
    wkT = np.ascontiguousarray(wk.T)
    wvT = np.ascontiguousarray(wv.T)
    woT = np.ascontiguousarray(wo.T)
    wguT = np.ascontiguousarray(wgu.T)
    wpT = np.ascontiguousarray(wp.T)

    b2 = np.zeros((128, 2), dtype=bfloat16)
    b2[:64, 0] = 1
    b2[64:, 1] = 1
    c2 = np.ascontiguousarray(b2.T)
    ones1 = np.ones((128, 1), dtype=bfloat16)
    ones1f = np.ones((1, 128), dtype=np.float32)
    ident = np.eye(128, dtype=np.float32)

    in_maps = []
    for c in range(8):
        b, hf = c // 2, c % 2
        xt = np.ascontiguousarray(x[b].T)
        if hf == 1:
            xt = np.ascontiguousarray(
                xt.reshape(D, S // 256, 2, 128)[:, :, ::-1, :].reshape(D, S))
        qmask = np.zeros((128, MSK_TOT), dtype=np.float32)
        for i in range(8):
            grow = 256 * i + 128 * hf + np.arange(128)[:, None]
            lk = 256 * i + np.arange(MSK_W[i])[None, :]
            gk = ((lk // 128) ^ hf) * 128 + lk % 128
            qmask[:, MSK_OFF[i]:MSK_OFF[i + 1]] = np.where(gk <= grow, 0.0, NEG_BIG)
        in_maps.append(dict(
            xT=xt, wqT=wqT, wkT=wkT, wvT=wvT, woT=woT, wguT=wguT, wpT=wpT,
            qmask=qmask, b2=b2, c2=c2, ones1=ones1, ones1f=ones1f, ident=ident))
    return in_maps


def _gather(results):
    out = np.empty((B, S, D), dtype=np.float32)
    for c in range(8):
        b, hf = c // 2, c % 2
        o = results[c]["out"]
        ot = o.T
        for i in range(8):
            out[b, 256 * i + 128 * hf:256 * i + 128 * hf + 128, :] = \
                ot[i * 128:(i + 1) * 128, :]
    return out


def kernel(**inputs):
    in_maps = _host_inputs(inputs)
    nc = build_program()
    res = run_bass_kernel_spmd(nc, in_maps, core_ids=list(range(8)))
    return _gather(res.results)


# version 13
# speedup vs baseline: 1.1513x; 1.1513x over previous
import numpy as np
import ml_dtypes
import concourse.bass as bass
import concourse.mybir as mybir
from concourse.bass_utils import run_bass_kernel_spmd

F32 = mybir.dt.float32
BF16 = mybir.dt.bfloat16
I16 = mybir.dt.int16
ALU = mybir.AluOpType
AF = mybir.ActivationFunctionType
AX = mybir.AxisListType
bfloat16 = ml_dtypes.bfloat16

B, S, D, H, HD = 4, 2048, 2048, 16, 128
TOK = 1024
KC = D // 128
GU, HF = 16384, 8192
M_THR = 0.501953125
EPS8 = float(np.float32(bfloat16(1e-8)))
LN_EPS = 1e-5
RMS_EPS = float(np.float32(np.finfo(np.float32).eps))
NEG_BIG = -1.0e30
SCL_ATT = 1.0 / float(np.sqrt(HD))
W_QT = [512, 1024, 1024, 1536, 1536, 2048, 2048, 2048]
MSK_W = [W_QT[i] - 256 * i for i in range(8)]
MSK_OFF = np.cumsum([0] + MSK_W).tolist()
MSK_TOT = MSK_OFF[-1]

_DBG_OUTS = ()
_DBG_PHASES = None
_DBG_QSTORE = None
_DBG_QLIMIT = None


class Prog:
    ENGS = ("pe", "dve", "act", "gp", "sync")
    NROT = 4
    DCLS = ("dS", "dG", "dC", "dO", "dQ")
    SEMS = ("sP", "sV", "sA", "sG") + tuple(
        f"{c}{r}" for c in ("dS", "dG", "dC", "dO", "dQ") for r in range(4))

    def __init__(self):
        self.recs = {e: [] for e in self.ENGS}
        self.cnt = {s: 0 for s in self.SEMS}
        self.rr = {c: 0 for c in self.DCLS}
        self.consts_ok = []

    def snap(self, cls):
        return [(f"{cls}{r}", self.cnt[f"{cls}{r}"]) for r in range(self.NROT)
                if self.cnt[f"{cls}{r}"] > 0]

    def emit(self, eng, fn, waits=(), inc=None, tag=""):
        waits = list(waits)
        if inc is not None and inc in self.DCLS:
            phys = f"{inc}{self.rr[inc] % self.NROT}"
            self.rr[inc] += 1
            if self.cnt[phys] > 0:
                waits.append((phys, self.cnt[phys]))
            inc = phys
        for (sem, v) in waits:
            assert v <= self.cnt[sem], f"wait on future {sem}={v}>{self.cnt[sem]} at {tag}"
        post = None
        if inc is not None:
            self.cnt[inc] += 16 if inc.startswith("d") else 1
            post = self.cnt[inc]
        self.recs[eng].append((tuple(waits), fn, inc, post, tag))
        return (inc, post) if inc is not None and inc.startswith("d") else post

    def render(self, block, sems):
        def rnd(eng_obj, recs, own_sem=None):
            last_own = 0
            for waits, fn, inc, post, tag in recs:
                if own_sem is not None and last_own > 0 and fn is not None:
                    eng_obj.wait_ge(sems[own_sem], last_own)
                for (sem, v) in waits:
                    if v > 0:
                        eng_obj.wait_ge(sems[sem], v)
                if fn is not None:
                    ins = fn(eng_obj)
                    if inc is not None:
                        ins.then_inc(sems[inc], 16 if inc.startswith("d") else 1)
                if inc == own_sem and post is not None:
                    last_own = post

        @block.tensor
        def _(t):
            rnd(t, self.recs["pe"])

        @block.vector
        def _(v):
            rnd(v, self.recs["dve"], own_sem="sV")

        @block.scalar
        def _(s):
            rnd(s, self.recs["act"], own_sem="sA")

        @block.gpsimd
        def _(g):
            rnd(g, self.recs["gp"], own_sem="sG")

        @block.sync
        def _(s):
            rnd(s, self.recs["sync"])


def _alloc(nc):
    T, DR = {}, {}
    DR["xT"] = nc.dram_tensor("xT", [D, S], F32, kind="ExternalInput").ap()
    for w in ("wqT", "wkT", "wvT", "woT"):
        DR[w] = nc.dram_tensor(w, [D, D], F32, kind="ExternalInput").ap()
    DR["wguT"] = nc.dram_tensor("wguT", [D, GU], F32, kind="ExternalInput").ap()
    DR["wpT"] = nc.dram_tensor("wpT", [HF, D], F32, kind="ExternalInput").ap()
    DR["qmask"] = nc.dram_tensor("qmask", [128, MSK_TOT], F32, kind="ExternalInput").ap()
    DR["b2"] = nc.dram_tensor("b2", [128, 2], BF16, kind="ExternalInput").ap()
    DR["c2"] = nc.dram_tensor("c2", [2, 128], BF16, kind="ExternalInput").ap()
    DR["ones1"] = nc.dram_tensor("ones1", [128, 1], BF16, kind="ExternalInput").ap()
    DR["ones1f"] = nc.dram_tensor("ones1f", [1, 128], F32, kind="ExternalInput").ap()
    DR["ident"] = nc.dram_tensor("ident", [128, 128], F32, kind="ExternalInput").ap()
    DR["out"] = nc.dram_tensor("out", [D, TOK], F32, kind="ExternalOutput").ap()
    for nm, shape, dt_ in (("wgu_hat", [D, GU], BF16), ("wp_hat", [HF, D], BF16),
                           ("k_scr", [D, S], F32), ("v_scr", [S, D], F32),
                           ("q_scr", [D, TOK], F32), ("x1_scr", [D, TOK], F32)):
        if nm in _DBG_OUTS:
            DR[nm] = nc.dram_tensor(nm, shape, dt_, kind="ExternalOutput").ap()
        else:
            DR[nm] = nc.dram_tensor(nm, shape, dt_).ap()

    al = nc.alloc_sbuf_tensor
    T["b2"] = al("b2_t", [128, 2], BF16)
    T["c2"] = al("c2_t", [2, 128], BF16)
    T["ones1"] = al("ones1_t", [128, 1], BF16)
    T["ones1f"] = al("ones1f_t", [1, 128], F32)
    T["negmax"] = al("negmax", [128, 1], F32)
    T["bias1"] = al("bias1", [128, 1], F32)
    T["rowsum"] = al("rowsum", [128, 1], F32)
    T["recip"] = al("recip", [128, 1], F32)

    BASE = 18432
    at = lambda nm, shape, dt_, offset: nc.alloc_sbuf_tensor_at(
        nm, shape, dt_, offset=BASE + offset)
    T["QW"] = at("QW", [128, 8192], BF16, 0)
    T["QABS"] = at("QABS", [128, 4096], BF16, 16384)
    T["QSG"] = at("QSG", [128, 4096], BF16, 24576)
    T["QT"] = at("QT", [128, 2048], BF16, 32768)
    T["QBB"] = at("QBB", [128, 2560], F32, 36864)
    T["QSCL"] = at("QSCL", [2, 4096], BF16, 47104)
    T["QLND"] = at("QLND", [128, 4096], I16, 55296)
    T["AR_A"] = at("AR_A", [128, 16384], F32, 0)
    T["AR_B"] = at("AR_B", [128, 12288], F32, 65536)
    T["AR_C"] = at("AR_C", [128, 8192], F32, 114688)
    T["AR_D"] = at("AR_D", [128, 8192], F32, 147456)
    T["AR_W"] = at("AR_W", [128, 4096], F32, 180224)
    T["LNXB"] = at("LNXB", [128, 1024], BF16, 196608)
    T["stg"] = at("stg", [128, 1024], F32, 198656)
    T["tmp"] = at("tmp", [128, 512], F32, 202752)
    T["ptb"] = at("ptb", [128, 256], F32, 204800)
    T["ident"] = at("ident_t", [128, 128], F32, 205824)
    T["stat1"] = at("stat1", [1, 512], F32, 206336)
    T["stat2"] = at("stat2", [1, 512], F32, 208384)
    T["HH"] = at("HH", [128, 16384], BF16, 0)
    T["HL"] = at("HL", [128, 16384], BF16, 32768)
    T["LN2H"] = at("LN2H", [128, 4096], BF16, 65536)
    T["LN2L"] = at("LN2L", [128, 4096], BF16, 65536 + 8192)
    T["AR_WB"] = at("AR_WB", [128, 8192], BF16, 180224)

    T["pbE"] = nc.alloc_psum_tensor("pbE", [128, 1024], F32)
    T["pbF"] = nc.alloc_psum_tensor("pbF", [128, 1024], F32)
    T["pbG"] = nc.alloc_psum_tensor("pbG", [128, 1024], F32)
    T["pbH"] = nc.alloc_psum_tensor("pbH", [128, 1024], F32)
    return T, DR


def _qtiles(T, i):
    s4 = slice((i % 4) * 2048, (i % 4) * 2048 + 2048)
    s2 = slice((i % 2) * 2048, (i % 2) * 2048 + 2048)
    return (T["QW"].ap()[:, s4], T["QABS"].ap()[:, s2], T["QSG"].ap()[:, s2],
            T["QT"].ap()[:, 0:2048], T["QSCL"].ap()[:, s2],
            T["QLND"].ap().bitcast(BF16)[:, s2])


def _quant_gen(P_, T, DR, consts_ok):
    FQ = 2048
    jobs = []
    for d0 in range(0, D, 128):
        for h0 in range(0, GU, FQ):
            jobs.append((DR["wguT"][d0:d0 + 128, h0:h0 + FQ],
                         DR["wgu_hat"][d0:d0 + 128, h0:h0 + FQ]))
    for h0 in range(0, HF, 128):
        for d0 in range(0, D, FQ):
            jobs.append((DR["wpT"][h0:h0 + 128, d0:d0 + FQ],
                         DR["wp_hat"][h0:h0 + 128, d0:d0 + FQ]))
    if _DBG_QLIMIT:
        jobs = jobs[:_DBG_QLIMIT]
    N = len(jobs)
    pbE, pbF = T["pbE"], T["pbF"]
    bcreg = [pbF[:, 0:512], pbF[:, 512:1024], pbE[:, 512:1024]]
    qbb = T["QBB"].ap()
    thrb = qbb[:, 0:512]
    qbct = [qbb[:, 512 + q * 512:1024 + q * 512] for q in range(4)]
    ld_tok, ab_tok, sg_tok = {}, {}, {}
    sc_tok, ev_tok, t_tok, tsg_tok = {}, {}, {}, {}
    d_tok, wf_tok, st_tok, bc3_tok = {}, {}, {}, {}

    def stage_act(j):
        if j >= N:
            return
        w, absw, sg = _qtiles(T, j)[0:3]
        ws = [("sV", t_tok[(j - 2, 3)])] if (j - 2, 3) in t_tok else []
        ab_tok[j] = P_.emit("act", lambda e: e.activation(
            out=absw, in_=w, func=AF.Abs), waits=[ld_tok[j]] + ws,
            inc="sA", tag=f"q_ab{j}")
        ws = [("sG", wf_tok[j - 2])] if (j - 2) in wf_tok else []
        sg_tok[j] = P_.emit("act", lambda e: e.activation(
            out=sg, in_=w, func=AF.Sign), waits=ws, inc="sA", tag=f"q_sg{j}")

    def stage_a(i):
        w, absw, sg, tt, scl, laund = _qtiles(T, i)
        for q in range(4):
            base = (0, 32, 64, 0)[q]
            ws = [("sA", ab_tok[i])] if q == 0 else []
            if i == 0 and q == 0:
                ws += list(consts_ok)
            prevq = (i, 0) if q == 3 else (i - 1, (3, 1, 2)[q])
            if prevq in sc_tok:
                ws.append(("sV", sc_tok[prevq]))
            mm = P_.emit("pe", lambda e, q=q, base=base: e.matmul(
                pbE[base:base + 2, 0:512], T["b2"][:],
                absw[:, q * 512:(q + 1) * 512], start=True, stop=True),
                waits=ws, inc="sP", tag=f"q_sum{i}.{q}")
            ws2 = [("sP", mm)]
            if (i - 2) in bc3_tok:
                ws2.append(("sP", bc3_tok[i - 2]))
            sc_tok[(i, q)] = P_.emit("dve", lambda e, q=q, base=base:
                e.tensor_scalar(
                    out=scl[:, q * 512:(q + 1) * 512],
                    in0=pbE[base:base + 2, 0:512],
                    scalar1=1.0 / 64, scalar2=EPS8, op0=ALU.mult, op1=ALU.max),
                waits=ws2, inc="sV", tag=f"q_sc{i}.{q}")

    def stage_b(i, src, dst):
        w, absw, sg, tt, scl, laund = _qtiles(T, i)
        for q in range(4):
            ws = [("sV", sc_tok[(i, q)])]
            if i == 0 and q == 0:
                ws += list(consts_ok)
            prev = (i, 0) if q == 3 else (i - 1, [3, 1, 2][q])
            if prev in ev_tok:
                ws.append(("sA", ev_tok[prev]))
            bc = P_.emit("pe", lambda e, q=q: e.matmul(
                bcreg[q % 3][:], T["c2"][:], scl[:, q * 512:(q + 1) * 512],
                start=True, stop=True), waits=ws, inc="sP", tag=f"q_bc{i}.{q}")
            if q == 3:
                bc3_tok[i] = bc
            ws = [("sP", bc)]
            if (i - 1, q) in tsg_tok:
                ws.append(("sV", tsg_tok[(i - 1, q)]))
            ev_tok[(i, q)] = P_.emit("act", lambda e, q=q: e.activation(
                out=qbct[q], in_=bcreg[q % 3][:], func=AF.Copy),
                waits=ws, inc="sA", tag=f"q_ev{i}.{q}")
        for q in range(4):
            th = P_.emit("dve", lambda e, q=q: e.tensor_scalar(
                out=thrb, in0=qbct[q], scalar1=M_THR, scalar2=None,
                op0=ALU.mult), waits=[("sA", ev_tok[(i, q)])],
                inc="sV", tag=f"q_th{i}.{q}")
            tw = [("sG", d_tok[i - 1])] if (q == 0 and (i - 1) in d_tok) \
                else []
            t_tok[(i, q)] = P_.emit("dve", lambda e, q=q: e.tensor_tensor(
                out=tt[:, q * 512:(q + 1) * 512],
                in0=absw[:, q * 512:(q + 1) * 512], in1=thrb, op=ALU.is_gt),
                waits=tw, inc="sV", tag=f"q_t{i}.{q}")
            g1w = [("sA", sg_tok[i])] if q == 0 else []
            P_.emit("dve", lambda e, q=q: e.tensor_tensor(
                out=tt[:, q * 512:(q + 1) * 512],
                in0=tt[:, q * 512:(q + 1) * 512],
                in1=sg[:, q * 512:(q + 1) * 512], op=ALU.mult),
                waits=g1w, inc="sV", tag=f"q_tsg{i}.{q}")
            tsg_tok[(i, q)] = P_.emit("dve", lambda e, q=q: e.tensor_tensor(
                out=tt[:, q * 512:(q + 1) * 512],
                in0=tt[:, q * 512:(q + 1) * 512],
                in1=qbct[q], op=ALU.mult),
                inc="sV", tag=f"q_wh{i}.{q}")
        d_tok[i] = P_.emit("gp", lambda e: e.tensor_tensor(
            out=sg, in0=tt, in1=w, op=ALU.subtract),
            waits=[("sV", tsg_tok[(i, 3)])], inc="sG", tag=f"q_d{i}")
        ws = [st_tok[i - 2]] if (i - 2) in st_tok else []
        wf = P_.emit("gp", lambda e: e.tensor_tensor(
            out=laund, in0=w, in1=sg, op=ALU.add),
            waits=ws, inc="sG", tag=f"q_wf{i}")
        wf_tok[i] = wf
        st_tok[i] = P_.emit("gp", lambda e: e.dma_start(
            out=dst, in_=laund), waits=[("sG", wf)], inc="dQ", tag=f"q_st{i}")

    def do_ld(j):
        if j >= N:
            return
        w = _qtiles(T, j)[0]
        ld_tok[j] = P_.emit("gp", lambda e: e.dma_start(
            out=w, in_=jobs[j][0]), inc="dG", tag=f"q_ld{j}")

    for i in range(N + 1):
        if i == 0:
            do_ld(0)
            do_ld(1)
            stage_act(0)
        do_ld(i + 2)
        if i >= 1:
            stage_b(i - 1, *jobs[i - 1])
        if i < N:
            stage_a(i)
        stage_act(i + 1)
        if i < N:
            yield


def _ln_block(P_, T, src, ntok, eps, tagp, first_waits=()):
    assert ntok <= 512
    pbE = T["pbE"]
    mm1 = mm2 = None
    for k in range(KC):
        ws = list(first_waits) if k == 0 else [("sP", mm2)]
        if k == 0:
            ws.append(("sP", P_.cnt["sP"]))
            ws.extend(P_.consts_ok)
        cb = P_.emit("dve", lambda e, k=k: e.tensor_copy(
            out=T["LNXB"].ap()[:, 0:ntok], in_=src(k)), waits=ws,
            inc="sV", tag=f"{tagp}cb{k}")
        sq = P_.emit("act", lambda e, k=k: e.activation(
            out=T["LNXB"].ap()[:, 512:512 + ntok], in_=src(k), func=AF.Square),
            waits=(list(first_waits) if k == 0 else [("sP", mm2)]),
            inc="sA", tag=f"{tagp}sq{k}")
        mm1 = P_.emit("pe", lambda e, k=k: e.matmul(
            pbE[0:1, 0:ntok], T["ones1"][:], T["LNXB"].ap()[:, 0:ntok],
            start=(k == 0), stop=(k == KC - 1)),
            waits=[("sV", cb)], inc="sP", tag=f"{tagp}m1{k}")
        mm2 = P_.emit("pe", lambda e, k=k: e.matmul(
            pbE[0:1, 512:512 + ntok], T["ones1"][:],
            T["LNXB"].ap()[:, 512:512 + ntok],
            start=(k == 0), stop=(k == KC - 1)),
            waits=[("sA", sq)], inc="sP", tag=f"{tagp}m2{k}")
    a1 = P_.emit("dve", lambda e: e.tensor_scalar(
        out=T["stat1"].ap()[:, 0:ntok], in0=pbE[0:1, 0:ntok],
        scalar1=1.0 / D, scalar2=None, op0=ALU.mult),
        waits=[("sP", mm2)], inc="sV", tag=f"{tagp}mean")
    a2 = P_.emit("dve", lambda e: e.tensor_scalar(
        out=T["stat2"].ap()[:, 0:ntok], in0=pbE[0:1, 512:512 + ntok],
        scalar1=1.0 / D, scalar2=None, op0=ALU.mult),
        inc="sV", tag=f"{tagp}msq")
    m2 = P_.emit("dve", lambda e: e.tensor_tensor(
        out=pbE[0:1, 0:ntok], in0=T["stat1"].ap()[:, 0:ntok],
        in1=T["stat1"].ap()[:, 0:ntok], op=ALU.mult),
        inc="sV", tag=f"{tagp}m2")
    vr = P_.emit("dve", lambda e, eps=eps: e.scalar_tensor_tensor(
        out=T["stat2"].ap()[:, 0:ntok], in0=T["stat2"].ap()[:, 0:ntok],
        scalar=eps, in1=pbE[0:1, 0:ntok], op0=ALU.add, op1=ALU.subtract),
        inc="sV", tag=f"{tagp}var")
    sd = P_.emit("act", lambda e: e.activation(
        out=T["stat2"].ap()[:, 0:ntok], in_=T["stat2"].ap()[:, 0:ntok],
        func=AF.Sqrt), waits=[("sV", vr)], inc="sA", tag=f"{tagp}sd")
    rs = P_.emit("dve", lambda e: e.reciprocal(
        out=T["stat2"].ap()[:, 0:ntok], in_=T["stat2"].ap()[:, 0:ntok]),
        waits=[("sA", sd)], inc="sV", tag=f"{tagp}rstd")
    mr = P_.emit("dve", lambda e: e.tensor_tensor(
        out=T["stat1"].ap()[:, 0:ntok], in0=T["stat1"].ap()[:, 0:ntok],
        in1=T["stat2"].ap()[:, 0:ntok], op=ALU.mult),
        inc="sV", tag=f"{tagp}mr")
    b1 = P_.emit("pe", lambda e: e.matmul(
        pbE[:, 0:ntok], T["ones1f"][:], T["stat2"].ap()[:, 0:ntok],
        start=True, stop=True), waits=[("sV", rs), ("sV", vr)], inc="sP",
        tag=f"{tagp}bc1")
    b2v = P_.emit("pe", lambda e: e.matmul(
        pbE[:, 512:512 + ntok], T["ones1f"][:], T["stat1"].ap()[:, 0:ntok],
        start=True, stop=True), waits=[("sV", mr)], inc="sP", tag=f"{tagp}bc2")
    last = None
    for k in range(KC):
        P_.emit("dve", lambda e, k=k: e.tensor_tensor(
            out=src(k), in0=src(k), in1=pbE[:, 0:ntok], op=ALU.mult),
            waits=[("sP", b2v)] if k == 0 else [], inc="sV", tag=f"{tagp}ap1.{k}")
        last = P_.emit("dve", lambda e, k=k: e.tensor_tensor(
            out=src(k), in0=src(k), in1=pbE[:, 512:512 + ntok], op=ALU.subtract),
            inc="sV", tag=f"{tagp}ap2.{k}")
    return last


def _slab(P_, T, src_cols, half, waits, tag):
    return P_.emit("sync", lambda e: e.dma_start(
        out=T["AR_W"].ap()[:, half * 2048:(half + 1) * 2048].rearrange(
            "p (c t) -> p c t", c=16),
        in_=src_cols.rearrange("(c p) m -> p c m", p=128)),
        waits=waits, inc="dS", tag=tag)


def _attn1(P_, T, DR, qstep):
    pbG, pbH = T["pbG"], T["pbH"]
    lnC = lambda k: T["AR_C"].ap()[:, k * 512:(k + 1) * 512]
    slab_free = {0: 0, 1: 0}
    for tb in range(4):
        xl = P_.emit("sync", lambda e, tb=tb: e.dma_start(
            out=T["AR_C"].ap().rearrange("p (c t) -> p c t", c=16),
            in_=DR["xT"].rearrange("(c p) t -> p c t", p=128)[
                :, :, tb * 512:(tb + 1) * 512]),
            waits=[("sP", P_.cnt["sP"]), ("sV", P_.cnt["sV"])],
            inc="dS", tag=f"a1_x{tb}")
        ln_done = _ln_block(P_, T, lnC, 512, LN_EPS, f"a1ln{tb}_",
                            first_waits=[xl])
        ev_prev = {}
        st_prev = {}
        vst_prev = {}
        for kind in ("k", "v", "q"):
            wname = {"k": "wkT", "v": "wvT", "q": "wqT"}[kind]
            for m in range(16):
                half = m % 2
                sl = _slab(P_, T, DR[wname][:, m * 128:(m + 1) * 128], half,
                           waits=[("sP", slab_free[half])], tag=f"a1w{kind}{tb}.{m}")
                bank = {"k": pbG[:, 0:512], "v": pbG[:, 512:1024],
                        "q": pbH[:, 0:256]}[kind]
                mm = None
                for k in range(KC):
                    ws = [sl, ("sV", ln_done)] if k == 0 else []
                    if k == 0 and kind in ev_prev:
                        ws.append(("sA", ev_prev[kind]))
                    if kind == "q":
                        rhs = (T["AR_C"].ap()[:, k * 512:(k + 1) * 512]
                               .rearrange("p (a b t) -> p a b t", a=2, b=2)
                               [:, :, 0:1, :])
                    else:
                        rhs = T["AR_C"].ap()[:, k * 512:(k + 1) * 512]
                    mm = P_.emit("pe", lambda e, bank=bank, half=half, k=k, rhs=rhs:
                                 e.matmul(bank,
                                          T["AR_W"].ap()[:, half * 2048 + k * 128:
                                                         half * 2048 + (k + 1) * 128],
                                          rhs, start=(k == 0), stop=(k == KC - 1)),
                                 waits=ws, inc="sP", tag=f"a1{kind}{tb}.{m}.{k}")
                slab_free[half] = mm
                if kind in ("k", "q"):
                    sh, n = (0, 512) if kind == "k" else (1, 256)
                    evw = [("sP", mm)]
                    if kind in st_prev:
                        evw.append(st_prev[kind])
                    ev = P_.emit("act", lambda e, bank=bank, sh=sh, n=n: e.activation(
                        out=T["stg"].ap()[:, sh * 512:sh * 512 + n], in_=bank,
                        func=AF.Copy), waits=evw, inc="sA",
                        tag=f"a1ev{kind}{tb}.{m}")
                    dst = (DR["k_scr"][m * 128:(m + 1) * 128, tb * 512:(tb + 1) * 512]
                           if kind == "k" else
                           DR["q_scr"][m * 128:(m + 1) * 128, tb * 256:(tb + 1) * 256])
                    stv = P_.emit("act", lambda e, dst=dst, sh=sh, n=n: e.dma_start(
                        out=dst, in_=T["stg"].ap()[:, sh * 512:sh * 512 + n]),
                        waits=[], inc="dC", tag=f"a1st{kind}{tb}.{m}")
                    st_prev[kind] = stv
                    ev_prev[kind] = ev
                else:
                    evw = [("sP", mm)]
                    if "v_tr" in st_prev:
                        evw.append(("sP", st_prev["v_tr"]))
                    ev = P_.emit("act", lambda e, bank=bank: e.activation(
                        out=T["stg"].ap()[:, 512:1024], in_=bank, func=AF.Copy),
                        waits=evw, inc="sA", tag=f"a1evv{tb}.{m}")
                    ev_prev[kind] = ev
                    tr_prev = None
                    for kt in range(4):
                        trb = pbH[:, 512:640]
                        ws = [("sA", ev)] + list(P_.consts_ok)
                        if tr_prev is not None:
                            ws.append(("sV", tr_prev))
                        elif vst_prev:
                            ws.append(("sV", vst_prev["last_te"]))
                        tr = P_.emit("pe", lambda e, trb=trb, kt=kt: e.transpose(
                            trb, T["stg"].ap()[:, 512 + kt * 128:512 + (kt + 1) * 128],
                            T["ident"][:]), waits=ws, inc="sP",
                            tag=f"a1tr{tb}.{m}.{kt}")
                        pcol = 128 * (kt % 2)
                        tvw = [("sP", tr)]
                        if (m, kt - 2) in vst_prev:
                            tvw.append(vst_prev[(m, kt - 2)])
                        elif (m - 1, kt + 2) in vst_prev:
                            tvw.append(vst_prev[(m - 1, kt + 2)])
                        tvv = P_.emit("dve", lambda e, trb=trb, pcol=pcol:
                                      e.tensor_copy(
                                          out=T["ptb"].ap()[:, pcol:pcol + 128],
                                          in_=trb),
                                      waits=tvw, inc="sV", tag=f"a1te{tb}.{m}.{kt}")
                        tr_prev = tvv
                        stv = P_.emit(
                            "act", lambda e, tb=tb, m=m, kt=kt, pcol=pcol:
                            e.dma_start(
                                out=DR["v_scr"][tb * 512 + kt * 128:
                                                tb * 512 + (kt + 1) * 128,
                                                m * 128:(m + 1) * 128],
                                in_=T["ptb"].ap()[:, pcol:pcol + 128]),
                            waits=[("sV", tvv)], inc="dC", tag=f"a1sv{tb}.{m}.{kt}")
                        vst_prev[(m, kt)] = stv
                        vst_prev["last_st"] = stv
                        vst_prev["last_te"] = tvv
                    st_prev["v_tr"] = tr
                qstep()


def _attn2(P_, T, DR):
    BAP = T["AR_B"].ap()
    kh = BAP[:, 0:2048]
    vT = BAP[:, 2048:4096]
    qh = BAP[:, 4096:5120]
    prob = BAP[:, 5120:7168]
    msk = BAP[:, 7168:7168 + MSK_TOT]
    pbE, pbF, pbG, pbH = T["pbE"], T["pbF"], T["pbG"], T["pbH"]

    a1_done = P_.snap("dC")
    quant_bar = P_.snap("dQ") + P_.snap("dG") + [("sG", P_.cnt["sG"]),
                                                 ("sV", P_.cnt["sV"]),
                                                 ("sA", P_.cnt["sA"])]
    ml = P_.emit("sync", lambda e: e.dma_start(out=msk, in_=DR["qmask"]),
                 waits=list(quant_bar), inc="dS", tag="a2_msk")
    last_mm_on = {"kh": 0, "vT": 0, "qh": 0}
    for h in range(H):
        ws = (list(a1_done) + [ml] + list(quant_bar)) if h == 0 else []
        lk = P_.emit("sync", lambda e, h=h: e.dma_start(
            out=kh, in_=DR["k_scr"][h * 128:(h + 1) * 128, :]),
            waits=ws + [("sP", last_mm_on["kh"])], inc="dS", tag=f"a2lk{h}")
        lv = P_.emit("sync", lambda e, h=h: e.dma_start(
            out=vT.rearrange("p (c t) -> p c t", c=16),
            in_=DR["v_scr"].rearrange("(c p) d -> p c d", p=128)[
                :, :, h * 128:(h + 1) * 128]),
            waits=[("sP", last_mm_on["vT"])] if h else list(a1_done),
            inc="dS", tag=f"a2lv{h}")
        lq = P_.emit("sync", lambda e, h=h: e.dma_start(
            out=qh, in_=DR["q_scr"][h * 128:(h + 1) * 128, :]),
            waits=[("sP", last_mm_on["qh"])] if h else list(a1_done),
            inc="dS", tag=f"a2lq{h}")
        for qt in range(8):
            W = W_QT[qt]
            scb = [(pbE, 0), (pbE, 512), (pbF, 0), (pbF, 512)]
            mm = None
            for kb in range(W // 512):
                t_, c_ = scb[kb]
                ws = []
                if kb == 0:
                    ws = [lk, lq, ("sA", P_.cnt["sA"])]
                mm = P_.emit("pe", lambda e, t_=t_, c_=c_, kb=kb, qt=qt: e.matmul(
                    t_[:, c_:c_ + 512],
                    qh[:, qt * 128:(qt + 1) * 128],
                    kh[:, kb * 512:(kb + 1) * 512],
                    start=True, stop=True), waits=ws, inc="sP",
                    tag=f"a2sc{h}.{qt}.{kb}")
            lo, hi = 256 * qt, W
            mvs = []
            off = MSK_OFF[qt]
            seg = lo
            while seg < hi:
                segend = min(hi, (seg // 1024 + 1) * 1024)
                t_ = pbE if seg < 1024 else pbF
                c_ = seg % 1024
                n = segend - seg
                moff = off + (seg - lo)
                mvs.append(P_.emit("dve", lambda e, t_=t_, c_=c_, n=n, moff=moff:
                                   e.tensor_tensor(
                                       out=t_[:, c_:c_ + n], in0=t_[:, c_:c_ + n],
                                       in1=msk[:, moff:moff + n], op=ALU.add),
                                   waits=[("sP", mm)], inc="sV",
                                   tag=f"a2mk{h}.{qt}.{seg}"))
                seg = segend
            if W <= 1024:
                rmx = P_.emit("dve", lambda e, W=W: e.tensor_reduce(
                    out=T["negmax"][:], in_=pbE[:, 0:W], axis=AX.X, op=ALU.max,
                    negate=True), waits=[("sV", mvs[-1])], inc="sV",
                    tag=f"a2mx{h}.{qt}")
            else:
                P_.emit("dve", lambda e: e.tensor_reduce(
                    out=T["negmax"][:], in_=pbE[:, 0:1024], axis=AX.X, op=ALU.max,
                    negate=True), waits=[("sV", mvs[-1])], inc="sV",
                    tag=f"a2mxa{h}.{qt}")
                P_.emit("dve", lambda e, W=W: e.tensor_reduce(
                    out=T["bias1"][:], in_=pbF[:, 0:W - 1024], axis=AX.X, op=ALU.max,
                    negate=True), inc="sV", tag=f"a2mxb{h}.{qt}")
                rmx = P_.emit("dve", lambda e: e.tensor_tensor(
                    out=T["negmax"][:], in0=T["negmax"][:], in1=T["bias1"][:],
                    op=ALU.min), inc="sV", tag=f"a2mxc{h}.{qt}")
            bs = P_.emit("dve", lambda e: e.tensor_scalar(
                out=T["bias1"][:], in0=T["negmax"][:], scalar1=SCL_ATT,
                scalar2=None, op0=ALU.mult), waits=[("sV", rmx)], inc="sV",
                tag=f"a2bs{h}.{qt}")
            if W <= 1024:
                ex = P_.emit("act", lambda e, W=W: e.activation(
                    out=prob[:, 0:W], in_=pbE[:, 0:W], func=AF.Exp,
                    bias=T["bias1"][:], scale=SCL_ATT, accum_out=T["rowsum"][:]),
                    waits=[("sV", bs)], inc="sA", tag=f"a2ex{h}.{qt}")
                rsv = P_.emit("dve", lambda e: e.reciprocal(
                    out=T["recip"][:], in_=T["rowsum"][:]),
                    waits=[("sA", ex)], inc="sV", tag=f"a2rc{h}.{qt}")
            else:
                P_.emit("act", lambda e: e.activation(
                    out=prob[:, 0:1024], in_=pbE[:, 0:1024], func=AF.Exp,
                    bias=T["bias1"][:], scale=SCL_ATT, accum_out=T["rowsum"][:]),
                    waits=[("sV", bs)], inc="sA", tag=f"a2exa{h}.{qt}")
                ex = P_.emit("act", lambda e, W=W: e.activation(
                    out=prob[:, 1024:W], in_=pbF[:, 0:W - 1024], func=AF.Exp,
                    bias=T["bias1"][:], scale=SCL_ATT, accum_out=T["negmax"][:]),
                    inc="sA", tag=f"a2exb{h}.{qt}")
                sm = P_.emit("dve", lambda e: e.tensor_tensor(
                    out=T["rowsum"][:], in0=T["rowsum"][:], in1=T["negmax"][:],
                    op=ALU.add), waits=[("sA", ex)], inc="sV", tag=f"a2sm{h}.{qt}")
                rsv = P_.emit("dve", lambda e: e.reciprocal(
                    out=T["recip"][:], in_=T["rowsum"][:]), inc="sV",
                    tag=f"a2rc{h}.{qt}")
            pn = P_.emit("dve", lambda e, W=W: e.tensor_scalar(
                out=prob[:, 0:W], in0=prob[:, 0:W], scalar1=T["recip"][:],
                scalar2=None, op0=ALU.mult), waits=[("sV", rsv)], inc="sV",
                tag=f"a2pn{h}.{qt}")
            abank = pbH[:, 0:128] if qt % 2 == 0 else pbH[:, 512:640]
            te_prev = {}
            av_prev = {}
            mm_av = None
            for kc in range(W // 128):
                trb = pbG[:, 0:128] if kc % 2 == 0 else pbG[:, 512:640]
                ws = [("sV", pn)] + list(P_.consts_ok)
                if kc >= 2:
                    ws.append(("sV", te_prev[kc % 2]))
                tr = P_.emit("pe", lambda e, trb=trb, kc=kc: e.transpose(
                    trb, prob[:, kc * 128:(kc + 1) * 128], T["ident"][:]),
                    waits=ws, inc="sP", tag=f"a2tr{h}.{qt}.{kc}")
                pcol = 128 * (kc % 2)
                tew = [("sP", tr)]
                if kc - 2 in av_prev:
                    tew.append(("sP", av_prev[kc - 2]))
                te = P_.emit("dve", lambda e, trb=trb, pcol=pcol: e.tensor_copy(
                    out=T["ptb"].ap()[:, pcol:pcol + 128], in_=trb),
                    waits=tew, inc="sV", tag=f"a2te{h}.{qt}.{kc}")
                te_prev[kc % 2] = te
                ws2 = [("sV", te), lv]
                mm_av = P_.emit("pe", lambda e, kc=kc, pcol=pcol, abank=abank, W=W:
                                e.matmul(abank, vT[:, kc * 128:(kc + 1) * 128],
                                         T["ptb"].ap()[:, pcol:pcol + 128],
                                         start=(kc == 0), stop=(kc == W // 128 - 1)),
                                waits=ws2, inc="sP", tag=f"a2av{h}.{qt}.{kc}")
                av_prev[kc] = mm_av
            at_ = T["AR_C"] if h < 8 else T["AR_D"]
            hc = h % 8
            P_.emit("act", lambda e, hc=hc, qt=qt, abank=abank, at_=at_:
                    e.activation(
                        out=at_.ap()[:, hc * 1024 + qt * 128:
                                     hc * 1024 + (qt + 1) * 128],
                        in_=abank, func=AF.Copy), waits=[("sP", mm_av)], inc="sA",
                    tag=f"a2ae{h}.{qt}")
        last_mm_on = {"kh": P_.cnt["sP"], "vT": P_.cnt["sP"], "qh": P_.cnt["sP"]}


def _attn3(P_, T, DR):
    pbG = T["pbG"]
    oB = lambda m: T["AR_B"].ap()[:, m * 512:(m + 1) * 512]

    def a_ap(k, tb):
        at_ = T["AR_C"] if k < 8 else T["AR_D"]
        return at_.ap()[:, (k % 8) * 1024 + tb * 512:(k % 8) * 1024 + (tb + 1) * 512]

    slab_free = {0: P_.cnt["sP"], 1: P_.cnt["sP"]}
    for tb in range(2):
        a2_done = [("sA", P_.cnt["sA"])] if tb == 0 else P_.snap("dC")
        ev_prev = {}
        for m in range(16):
            half = m % 2
            sl = _slab(P_, T, DR["woT"][:, m * 128:(m + 1) * 128], half,
                       waits=[("sP", slab_free[half])], tag=f"a3w{tb}.{m}")
            bank = (pbG[:, 0:512], pbG[:, 512:1024])[m % 2]
            mm = None
            for k in range(KC):
                ws = ([sl] + (list(a2_done) if m == 0 else [])) if k == 0 else []
                if k == 0 and (m % 2) in ev_prev:
                    ws.append(("sA", ev_prev[m % 2]))
                if k == 0 and m == 0 and tb == 1:
                    ws.extend(P_.snap("dC"))
                mm = P_.emit("pe", lambda e, bank=bank, half=half, k=k, tb=tb:
                             e.matmul(bank,
                                      T["AR_W"].ap()[:, half * 2048 + k * 128:
                                                     half * 2048 + (k + 1) * 128],
                                      a_ap(k, tb),
                                      start=(k == 0), stop=(k == KC - 1)),
                             waits=ws, inc="sP", tag=f"a3o{tb}.{m}.{k}")
            slab_free[half] = mm
            ev_prev[m % 2] = P_.emit("act", lambda e, bank=bank, m=m: e.activation(
                out=oB(m), in_=bank, func=AF.Copy), waits=[("sP", mm)], inc="sA",
                tag=f"a3ev{tb}.{m}")
        ln_done = _ln_block(P_, T, oB, 512, LN_EPS, f"a3ln{tb}_",
                            first_waits=[("sA", P_.cnt["sA"])])
        ad_prev = {}
        for k in range(KC):
            sh = k % 2
            xlw = ([("sV", P_.cnt["sV"])] + P_.snap("dC")) if k < 2 else \
                  [("sV", ad_prev[k - 2])]
            xl = P_.emit("sync", lambda e, k=k, tb=tb, sh=sh: e.dma_start(
                out=T["stg"].ap()[:, sh * 512:(sh + 1) * 512].rearrange(
                    "p (a b t) -> p a b t", a=4, b=1),
                in_=DR["xT"][k * 128:(k + 1) * 128, :].rearrange(
                    "p (a b t) -> p a b t", a=8, b=2)[:, 4 * tb:4 * tb + 4, 0:1, :]),
                waits=xlw, inc="dS", tag=f"a3x{tb}.{k}")
            ad = P_.emit("dve", lambda e, k=k, sh=sh: e.tensor_tensor(
                out=oB(k), in0=oB(k), in1=T["stg"].ap()[:, sh * 512:(sh + 1) * 512],
                op=ALU.add),
                waits=[xl, ("sV", ln_done)], inc="sV", tag=f"a3x1{tb}.{k}")
            ad_prev[k] = ad
            P_.emit("act", lambda e, k=k, tb=tb: e.dma_start(
                out=DR["x1_scr"][k * 128:(k + 1) * 128, tb * 512:(tb + 1) * 512],
                in_=oB(k)), waits=[("sV", ad)], inc="dC", tag=f"a3s1{tb}.{k}")


def _ffn(P_, T, DR):
    pbE, pbF, pbG, pbH = T["pbE"], T["pbF"], T["pbG"], T["pbH"]
    lnD = lambda k: T["AR_D"].ap()[:, k * 256:(k + 1) * 256]
    lnY = lambda k: T["AR_D"].ap()[:, 4096 + k * 256: 4096 + (k + 1) * 256]
    l2h = lambda k: T["LN2H"].ap()[:, k * 256:(k + 1) * 256]
    l2l = lambda k: T["LN2L"].ap()[:, k * 256:(k + 1) * 256]
    hh = lambda c: T["HH"].ap()[:, c * 256:(c + 1) * 256]
    hl = lambda c: T["HL"].ap()[:, c * 256:(c + 1) * 256]
    wb = lambda q: T["AR_WB"].ap()[:, q * 2048:(q + 1) * 2048]

    x1_ok = P_.snap("dC") + P_.snap("dQ")
    for fb in range(4):
        xl = P_.emit("sync", lambda e, fb=fb: e.dma_start(
            out=T["AR_D"].ap()[:, 0:4096].rearrange("p (c t) -> p c t", c=16),
            in_=DR["x1_scr"].rearrange("(c p) t -> p c t", p=128)[
                :, :, fb * 256:(fb + 1) * 256]),
            waits=list(x1_ok) + [("sV", P_.cnt["sV"]), ("sP", P_.cnt["sP"])]
                  + P_.snap("dO"),
            inc="dS", tag=f"f_x1{fb}")
        ln2_done = _ln_block(P_, T, lnD, 256, LN_EPS, f"fln{fb}_",
                             first_waits=[xl])
        sp = None
        for k in range(KC):
            ws = [("sV", ln2_done), ("sP", P_.cnt["sP"])] if k == 0 else []
            P_.emit("dve", lambda e, k=k: e.tensor_copy(out=l2h(k), in_=lnD(k)),
                    waits=ws, inc="sV", tag=f"fsh{fb}.{k}")
            sp = P_.emit("dve", lambda e, k=k: e.tensor_tensor(
                out=l2l(k), in0=lnD(k), in1=l2h(k), op=ALU.subtract),
                inc="sV", tag=f"fsl{fb}.{k}")
        slabs = {}
        hsq_mm = None
        for c in range(64):
            r = c % 2
            gbank = (pbG[:, 0:256], pbG[:, 512:768])[r]
            ubank = (pbH[:, 0:256], pbH[:, 512:768])[r]
            wsl = [("sP", slabs[r])] if r in slabs else                 [("sP", P_.cnt["sP"])] + list(x1_ok)
            sl_g = P_.emit("sync", lambda e, c=c, r=r: e.dma_start(
                out=wb(2 * r).rearrange("p (kk t) -> p kk t", kk=16),
                in_=DR["wgu_hat"].rearrange("(kk p) m -> p kk m", p=128)[
                    :, :, c * 128:(c + 1) * 128]),
                waits=wsl, inc="dS", tag=f"fwg{fb}.{c}")
            sl_u = P_.emit("sync", lambda e, c=c, r=r: e.dma_start(
                out=wb(2 * r + 1).rearrange("p (kk t) -> p kk t", kk=16),
                in_=DR["wgu_hat"].rearrange("(kk p) m -> p kk m", p=128)[
                    :, :, 8192 + c * 128:8192 + (c + 1) * 128]),
                inc="dS", tag=f"fwu{fb}.{c}")
            mmu = None
            for k in range(KC):
                ws = [sl_g, sl_u, ("sV", sp)] if k == 0 else []
                for p, rhs_f in ((0, l2h), (1, l2l)):
                    P_.emit("pe", lambda e, gbank=gbank, r=r, k=k, rhs_f=rhs_f, p=p:
                            e.matmul(gbank, wb(2 * r)[:, k * 128:(k + 1) * 128],
                                     rhs_f(k), start=(k == 0 and p == 0),
                                     stop=(k == KC - 1 and p == 1)),
                            waits=ws if p == 0 else [], inc="sP",
                            tag=f"fg{fb}.{c}.{k}.{p}")
                    ws = []
                    mmu = P_.emit("pe", lambda e, ubank=ubank, r=r, k=k, rhs_f=rhs_f,
                                  p=p: e.matmul(
                                      ubank, wb(2 * r + 1)[:, k * 128:(k + 1) * 128],
                                      rhs_f(k), start=(k == 0 and p == 0),
                                      stop=(k == KC - 1 and p == 1)),
                                  inc="sP", tag=f"fu{fb}.{c}.{k}.{p}")
            slabs[r] = mmu
            sv = P_.emit("act", lambda e, gbank=gbank: e.activation(
                out=T["tmp"].ap()[:, 0:256], in_=gbank, func=AF.Silu),
                waits=[("sP", mmu)], inc="sA", tag=f"fsilu{fb}.{c}")
            ht = P_.emit("dve", lambda e, ubank=ubank: e.tensor_tensor(
                out=T["tmp"].ap()[:, 256:512], in0=T["tmp"].ap()[:, 0:256],
                in1=ubank, op=ALU.mult),
                waits=[("sA", sv)], inc="sV", tag=f"fht{fb}.{c}")
            P_.emit("dve", lambda e, c=c: e.tensor_copy(
                out=hh(c), in_=T["tmp"].ap()[:, 256:512]), inc="sV",
                tag=f"fhh{fb}.{c}")
            hv = P_.emit("dve", lambda e, c=c: e.tensor_tensor(
                out=hl(c), in0=T["tmp"].ap()[:, 256:512], in1=hh(c),
                op=ALU.subtract), inc="sV", tag=f"fhl{fb}.{c}")
            hq = P_.emit("act", lambda e: e.activation(
                out=T["LNXB"].ap()[:, 0:256], in_=T["tmp"].ap()[:, 256:512],
                func=AF.Square),
                waits=[("sV", ht)], inc="sA", tag=f"fhsq{fb}.{c}")
            hsq_mm = P_.emit("pe", lambda e, c=c: e.matmul(
                pbE[0:1, 0:256], T["ones1"][:], T["LNXB"].ap()[:, 0:256],
                start=(c == 0), stop=(c == 63)),
                waits=[("sA", hq)], inc="sP", tag=f"fhs{fb}.{c}")
        r1 = P_.emit("dve", lambda e: e.tensor_scalar(
            out=T["stat2"].ap()[:, 0:256], in0=pbE[0:1, 0:256], scalar1=1.0 / HF,
            scalar2=RMS_EPS, op0=ALU.mult, op1=ALU.add),
            waits=[("sP", hsq_mm)], inc="sV", tag=f"fr1{fb}")
        r2 = P_.emit("act", lambda e: e.activation(
            out=T["stat2"].ap()[:, 0:256], in_=T["stat2"].ap()[:, 0:256],
            func=AF.Sqrt), waits=[("sV", r1)], inc="sA", tag=f"fr2{fb}")
        r3 = P_.emit("dve", lambda e: e.reciprocal(
            out=T["stat2"].ap()[:, 0:256], in_=T["stat2"].ap()[:, 0:256]),
            waits=[("sA", r2)], inc="sV", tag=f"fr3{fb}")
        r4 = P_.emit("pe", lambda e: e.matmul(
            pbE[:, 512:768], T["ones1f"][:], T["stat2"].ap()[:, 0:256],
            start=True, stop=True), waits=[("sV", r3)], inc="sP", tag=f"fr4{fb}")
        r5 = P_.emit("dve", lambda e: e.tensor_copy(
            out=T["tmp"].ap()[:, 0:256], in_=pbE[:, 512:768]),
            waits=[("sP", r4)], inc="sV", tag=f"fr5{fb}")
        banks = [pbG[:, 0:256], pbG[:, 512:768], pbH[:, 0:256], pbH[:, 512:768],
                 pbE[:, 0:256], pbE[:, 512:768], pbF[:, 0:256], pbF[:, 512:768]]
        yv = {}
        for sw in range(2):
            for dm in range(8):
                dc = sw * 8 + dm
                bank = banks[dm]
                mm = None
                for ss in range(4):
                    half = ss % 2
                    wsl = [("sP", P_.cnt["sP"] if (sw or dm or ss) else 0)]
                    if dc == 0 and ss == 0:
                        wsl = [("sV", r5), ("sP", P_.cnt["sP"])]
                    sl = P_.emit("sync", lambda e, dc=dc, ss=ss, half=half:
                                 e.dma_start(
                                     out=wb(half).rearrange(
                                         "p (kk t) -> p kk t", kk=16),
                                     in_=DR["wp_hat"].rearrange(
                                         "(kk p) m -> p kk m", p=128)[
                                         :, ss * 16:(ss + 1) * 16,
                                         dc * 128:(dc + 1) * 128]),
                                 waits=wsl, inc="dS", tag=f"fpw{fb}.{dc}.{ss}")
                    for kk in range(16):
                        hk = ss * 16 + kk
                        ws = [sl] if kk == 0 else []
                        if dm == 0 and ss == 0 and kk == 0 and sw == 1:
                            ws.append(("sV", yv[7]))
                        for p, h_f in ((0, hh), (1, hl)):
                            mm = P_.emit("pe", lambda e, bank=bank, half=half,
                                         kk=kk, hk=hk, h_f=h_f, p=p: e.matmul(
                                             bank, wb(half)[:, kk * 128:
                                                            (kk + 1) * 128],
                                             h_f(hk), start=(hk == 0 and p == 0),
                                             stop=(hk == 63 and p == 1)),
                                         waits=ws if p == 0 else [], inc="sP",
                                         tag=f"fp{fb}.{dc}.{hk}.{p}")
                            ws = []
                yv[dm] = P_.emit("dve", lambda e, bank=bank, dc=dc: e.tensor_tensor(
                    out=lnY(dc), in0=bank, in1=T["tmp"].ap()[:, 0:256],
                    op=ALU.mult), waits=[("sP", mm)], inc="sV", tag=f"fy{fb}.{dc}")
        ln_done = _ln_block(P_, T, lnY, 256, LN_EPS, f"fyln{fb}_",
                            first_waits=[("sV", yv[7])])
        ad_prev = {}
        for k in range(KC):
            sh = k % 2
            xlw = ([("sV", P_.cnt["sV"])] + P_.snap("dC")) if k < 2 else \
                  [("sV", ad_prev[k - 2])]
            xl = P_.emit("sync", lambda e, k=k, fb=fb, sh=sh: e.dma_start(
                out=T["stg"].ap()[:, sh * 512:sh * 512 + 256],
                in_=DR["x1_scr"][k * 128:(k + 1) * 128, fb * 256:(fb + 1) * 256]),
                waits=xlw, inc="dS", tag=f"fx1{fb}.{k}")
            ad = P_.emit("dve", lambda e, k=k, sh=sh: e.tensor_tensor(
                out=lnY(k), in0=lnY(k), in1=T["stg"].ap()[:, sh * 512:sh * 512 + 256],
                op=ALU.add), waits=[xl, ("sV", ln_done)], inc="sV",
                tag=f"fx2{fb}.{k}")
            ad_prev[k] = ad
            P_.emit("sync", lambda e, k=k, fb=fb: e.dma_start(
                out=DR["out"][k * 128:(k + 1) * 128, fb * 256:(fb + 1) * 256],
                in_=lnY(k)), waits=[("sV", ad)], inc="dO", tag=f"fo{fb}.{k}")


def build_program():
    nc = bass.Bass("TRN2", target_bir_lowering=False)
    T, DR = _alloc(nc)
    P_ = Prog()
    phases = _DBG_PHASES or ("quant", "attn1", "attn2", "attn3", "ffn")

    import contextlib
    with contextlib.ExitStack() as stack:
        sems = {nm: stack.enter_context(nc.semaphore(nm)) for nm in Prog.SEMS}
        block = stack.enter_context(nc.Block())
        for nm in ("b2", "c2", "ones1", "ones1f", "ident"):
            P_.emit("sync", lambda e, nm=nm: e.dma_start(out=T[nm][:], in_=DR[nm]),
                    inc="dS", tag=f"c_{nm}")
        P_.consts_ok = P_.snap("dS")
        qg = iter(())
        if "quant" in phases:
            qg = _quant_gen(P_, T, DR, P_.consts_ok)

        def qstep():
            next(qg, None)

        if "attn1" in phases:
            _attn1(P_, T, DR, qstep)
        for _ in qg:
            pass
        if "attn2" in phases:
            _attn2(P_, T, DR)
        if "attn3" in phases:
            _attn3(P_, T, DR)
        if "ffn" in phases:
            _ffn(P_, T, DR)
        fin = []
        for cls in Prog.DCLS:
            fin.extend(P_.snap(cls))
        P_.emit("sync", None, waits=fin, tag="fin")
        P_.render(block, sems)
    return nc


def _host_inputs(inputs):
    x = np.asarray(inputs["L0_attn_x"], dtype=np.float32)
    wq = np.asarray(inputs["L0_attn_wq"], dtype=np.float32)
    wk = np.asarray(inputs["L0_attn_wk"], dtype=np.float32)
    wv = np.asarray(inputs["L0_attn_wv"], dtype=np.float32)
    wo = np.asarray(inputs["L0_attn_wo"], dtype=np.float32)
    wgu = np.asarray(inputs["L0_ffn_w_gate_up"], dtype=np.float32)
    wp = np.asarray(inputs["L0_ffn_w_proj"], dtype=np.float32)

    wqT = np.ascontiguousarray(wq.T)
    wkT = np.ascontiguousarray(wk.T)
    wvT = np.ascontiguousarray(wv.T)
    woT = np.ascontiguousarray(wo.T)
    wguT = np.ascontiguousarray(wgu.T)
    wpT = np.ascontiguousarray(wp.T)

    b2 = np.zeros((128, 2), dtype=bfloat16)
    b2[:64, 0] = 1
    b2[64:, 1] = 1
    c2 = np.ascontiguousarray(b2.T)
    ones1 = np.ones((128, 1), dtype=bfloat16)
    ones1f = np.ones((1, 128), dtype=np.float32)
    ident = np.eye(128, dtype=np.float32)

    in_maps = []
    for c in range(8):
        b, hf = c // 2, c % 2
        xt = np.ascontiguousarray(x[b].T)
        if hf == 1:
            xt = np.ascontiguousarray(
                xt.reshape(D, S // 256, 2, 128)[:, :, ::-1, :].reshape(D, S))
        qmask = np.zeros((128, MSK_TOT), dtype=np.float32)
        for i in range(8):
            grow = 256 * i + 128 * hf + np.arange(128)[:, None]
            lk = 256 * i + np.arange(MSK_W[i])[None, :]
            gk = ((lk // 128) ^ hf) * 128 + lk % 128
            qmask[:, MSK_OFF[i]:MSK_OFF[i + 1]] = np.where(gk <= grow, 0.0, NEG_BIG)
        in_maps.append(dict(
            xT=xt, wqT=wqT, wkT=wkT, wvT=wvT, woT=woT, wguT=wguT, wpT=wpT,
            qmask=qmask, b2=b2, c2=c2, ones1=ones1, ones1f=ones1f, ident=ident))
    return in_maps


def _gather(results):
    out = np.empty((B, S, D), dtype=np.float32)
    for c in range(8):
        b, hf = c // 2, c % 2
        o = results[c]["out"]
        ot = o.T
        for i in range(8):
            out[b, 256 * i + 128 * hf:256 * i + 128 * hf + 128, :] = \
                ot[i * 128:(i + 1) * 128, :]
    return out


def kernel(**inputs):
    in_maps = _host_inputs(inputs)
    nc = build_program()
    res = run_bass_kernel_spmd(nc, in_maps, core_ids=list(range(8)))
    return _gather(res.results)


# version 17
# speedup vs baseline: 1.2291x; 1.0676x over previous
import numpy as np
import ml_dtypes
import concourse.bass as bass
import concourse.mybir as mybir
from concourse.bass_utils import run_bass_kernel_spmd

F32 = mybir.dt.float32
BF16 = mybir.dt.bfloat16
I16 = mybir.dt.int16
ALU = mybir.AluOpType
AF = mybir.ActivationFunctionType
AX = mybir.AxisListType
bfloat16 = ml_dtypes.bfloat16

B, S, D, H, HD = 4, 2048, 2048, 16, 128
TOK = 1024
KC = D // 128
GU, HF = 16384, 8192
M_THR = 0.501953125
EPS8 = float(np.float32(bfloat16(1e-8)))
LN_EPS = 1e-5
RMS_EPS = float(np.float32(np.finfo(np.float32).eps))
NEG_BIG = -1.0e30
SCL_ATT = 1.0 / float(np.sqrt(HD))
W_QT = [512, 1024, 1024, 1536, 1536, 2048, 2048, 2048]
MSK_W = [W_QT[i] - 256 * i for i in range(8)]
MSK_OFF = np.cumsum([0] + MSK_W).tolist()
MSK_TOT = MSK_OFF[-1]

_DBG_OUTS = ()
_DBG_PHASES = None
_DBG_QSTORE = None
_DBG_QLIMIT = None


class Prog:
    ENGS = ("pe", "dve", "act", "gp", "sync")
    NROT = 4
    DCLS = ("dS", "dG", "dC", "dO", "dQ")
    SEMS = ("sP", "sV", "sA", "sG") + tuple(
        f"{c}{r}" for c in ("dS", "dG", "dC", "dO", "dQ") for r in range(4))

    def __init__(self):
        self.recs = {e: [] for e in self.ENGS}
        self.cnt = {s: 0 for s in self.SEMS}
        self.rr = {c: 0 for c in self.DCLS}
        self.consts_ok = []

    def snap(self, cls):
        return [(f"{cls}{r}", self.cnt[f"{cls}{r}"]) for r in range(self.NROT)
                if self.cnt[f"{cls}{r}"] > 0]

    def emit(self, eng, fn, waits=(), inc=None, tag=""):
        waits = list(waits)
        if inc is not None and inc in self.DCLS:
            phys = f"{inc}{self.rr[inc] % self.NROT}"
            self.rr[inc] += 1
            if self.cnt[phys] > 0:
                waits.append((phys, self.cnt[phys]))
            inc = phys
        for (sem, v) in waits:
            assert v <= self.cnt[sem], f"wait on future {sem}={v}>{self.cnt[sem]} at {tag}"
        post = None
        if inc is not None:
            self.cnt[inc] += 16 if inc.startswith("d") else 1
            post = self.cnt[inc]
        self.recs[eng].append((tuple(waits), fn, inc, post, tag))
        return (inc, post) if inc is not None and inc.startswith("d") else post

    def render(self, block, sems):
        def rnd(eng_obj, recs, own_sem=None):
            last_own = 0
            for waits, fn, inc, post, tag in recs:
                if own_sem is not None and last_own > 0 and fn is not None:
                    eng_obj.wait_ge(sems[own_sem], last_own)
                for (sem, v) in waits:
                    if v > 0:
                        eng_obj.wait_ge(sems[sem], v)
                if fn is not None:
                    ins = fn(eng_obj)
                    if inc is not None:
                        ins.then_inc(sems[inc], 16 if inc.startswith("d") else 1)
                if inc == own_sem and post is not None:
                    last_own = post

        @block.tensor
        def _(t):
            rnd(t, self.recs["pe"])

        @block.vector
        def _(v):
            rnd(v, self.recs["dve"], own_sem="sV")

        @block.scalar
        def _(s):
            rnd(s, self.recs["act"], own_sem="sA")

        @block.gpsimd
        def _(g):
            rnd(g, self.recs["gp"], own_sem="sG")

        @block.sync
        def _(s):
            rnd(s, self.recs["sync"])


def _alloc(nc):
    T, DR = {}, {}
    DR["xT"] = nc.dram_tensor("xT", [D, S], F32, kind="ExternalInput").ap()
    for w in ("wqT", "wkT", "wvT", "woT"):
        DR[w] = nc.dram_tensor(w, [D, D], F32, kind="ExternalInput").ap()
    DR["wguT"] = nc.dram_tensor("wguT", [D, GU], F32, kind="ExternalInput").ap()
    DR["wpT"] = nc.dram_tensor("wpT", [HF, D], F32, kind="ExternalInput").ap()
    DR["qmask"] = nc.dram_tensor("qmask", [128, MSK_TOT], F32, kind="ExternalInput").ap()
    DR["b2"] = nc.dram_tensor("b2", [128, 2], BF16, kind="ExternalInput").ap()
    DR["c2"] = nc.dram_tensor("c2", [2, 128], BF16, kind="ExternalInput").ap()
    DR["ones1"] = nc.dram_tensor("ones1", [128, 1], BF16, kind="ExternalInput").ap()
    DR["ones1f"] = nc.dram_tensor("ones1f", [1, 128], F32, kind="ExternalInput").ap()
    DR["ident"] = nc.dram_tensor("ident", [128, 128], F32, kind="ExternalInput").ap()
    DR["out"] = nc.dram_tensor("out", [D, TOK], F32, kind="ExternalOutput").ap()
    for nm, shape, dt_ in (("wgu_hat", [D, GU], BF16), ("wp_hat", [HF, D], BF16),
                           ("k_scr", [D, S], F32), ("v_scr", [S, D], F32),
                           ("q_scr", [D, TOK], F32), ("x1_scr", [D, TOK], F32)):
        if nm in _DBG_OUTS:
            DR[nm] = nc.dram_tensor(nm, shape, dt_, kind="ExternalOutput").ap()
        else:
            DR[nm] = nc.dram_tensor(nm, shape, dt_).ap()

    al = nc.alloc_sbuf_tensor
    T["b2"] = al("b2_t", [128, 2], BF16)
    T["c2"] = al("c2_t", [2, 128], BF16)
    T["ones1"] = al("ones1_t", [128, 1], BF16)
    T["ones1f"] = al("ones1f_t", [1, 128], F32)
    T["negmax"] = al("negmax", [128, 1], F32)
    T["bias1"] = al("bias1", [128, 1], F32)
    T["rowsum"] = al("rowsum", [128, 1], F32)
    T["recip"] = al("recip", [128, 1], F32)

    BASE = 18432
    at = lambda nm, shape, dt_, offset: nc.alloc_sbuf_tensor_at(
        nm, shape, dt_, offset=BASE + offset)
    T["QW"] = at("QW", [128, 8192], BF16, 0)
    T["QABS"] = at("QABS", [128, 4096], BF16, 16384)
    T["QSG"] = at("QSG", [128, 4096], BF16, 24576)
    T["QT"] = at("QT", [128, 2048], BF16, 32768)
    T["QBB"] = at("QBB", [128, 2560], F32, 36864)
    T["QSCL"] = at("QSCL", [2, 4096], BF16, 47104)
    T["QLND"] = at("QLND", [128, 4096], I16, 55296)
    T["AR_A"] = at("AR_A", [128, 16384], F32, 0)
    T["AR_B"] = at("AR_B", [128, 12288], F32, 65536)
    T["AR_C"] = at("AR_C", [128, 8192], F32, 114688)
    T["AR_D"] = at("AR_D", [128, 8192], F32, 147456)
    T["AR_W"] = at("AR_W", [128, 4096], F32, 180224)
    T["LNXB"] = at("LNXB", [128, 1024], BF16, 196608)
    T["stg"] = at("stg", [128, 1024], F32, 198656)
    T["tmp"] = at("tmp", [128, 512], F32, 202752)
    T["ptb"] = at("ptb", [128, 256], F32, 204800)
    T["ident"] = at("ident_t", [128, 128], F32, 205824)
    T["stat1"] = at("stat1", [1, 512], F32, 206336)
    T["stat2"] = at("stat2", [1, 512], F32, 208384)
    T["HH"] = at("HH", [128, 16384], BF16, 0)
    T["HL"] = at("HL", [128, 16384], BF16, 32768)
    T["LN2H"] = at("LN2H", [128, 4096], BF16, 65536)
    T["LN2L"] = at("LN2L", [128, 4096], BF16, 65536 + 8192)
    T["AR_WB"] = at("AR_WB", [128, 8192], BF16, 180224)

    T["pbE"] = nc.alloc_psum_tensor("pbE", [128, 1024], F32)
    T["pbF"] = nc.alloc_psum_tensor("pbF", [128, 1024], F32)
    T["pbG"] = nc.alloc_psum_tensor("pbG", [128, 1024], F32)
    T["pbH"] = nc.alloc_psum_tensor("pbH", [128, 1024], F32)
    return T, DR


def _qtiles(T, i):
    s4 = slice((i % 4) * 2048, (i % 4) * 2048 + 2048)
    s2 = slice((i % 2) * 2048, (i % 2) * 2048 + 2048)
    return (T["QW"].ap()[:, s4], T["QABS"].ap()[:, s2], T["QSG"].ap()[:, s2],
            T["QT"].ap()[:, 0:2048], T["QSCL"].ap()[:, s2],
            T["QLND"].ap().bitcast(BF16)[:, s2])


def _quant_gen(P_, T, DR, consts_ok):
    FQ = 2048
    jobs = []
    for d0 in range(0, D, 128):
        for h0 in range(0, GU, FQ):
            jobs.append((DR["wguT"][d0:d0 + 128, h0:h0 + FQ],
                         DR["wgu_hat"][d0:d0 + 128, h0:h0 + FQ]))
    for h0 in range(0, HF, 128):
        for d0 in range(0, D, FQ):
            jobs.append((DR["wpT"][h0:h0 + 128, d0:d0 + FQ],
                         DR["wp_hat"][h0:h0 + 128, d0:d0 + FQ]))
    if _DBG_QLIMIT:
        jobs = jobs[:_DBG_QLIMIT]
    N = len(jobs)
    pbE, pbF = T["pbE"], T["pbF"]
    bcreg = [pbF[:, 0:512], pbF[:, 512:1024], pbE[:, 512:1024]]
    ld_tok, ab_tok, sg_tok = {}, {}, {}
    sc_tok, t_tok, wh_tok, sum_tok = {}, {}, {}, {}
    d_tok, wf_tok, st_tok, bc3_tok = {}, {}, {}, {}

    def do_ld(j):
        if j >= N:
            return
        w = _qtiles(T, j)[0]
        ld_tok[j] = P_.emit("gp", lambda e: e.dma_start(
            out=w, in_=jobs[j][0]), inc="dG", tag=f"q_ld{j}")

    def do_ab(j):
        if j >= N:
            return
        w, absw = _qtiles(T, j)[0:2]
        ws = [ld_tok[j]]
        if (j - 2, 3) in t_tok:
            ws += [("sV", t_tok[(j - 2, 3)]), ("sP", sum_tok[j - 2])]
        ab_tok[j] = P_.emit("act", lambda e: e.activation(
            out=absw, in_=w, func=AF.Abs),
            waits=ws, inc="sA", tag=f"q_ab{j}")

    def do_sg(j):
        if j >= N:
            return
        w, _, sg = _qtiles(T, j)[0:3]
        ws = [ld_tok[j]]
        if (j - 2) in wf_tok:
            ws.append(("sG", wf_tok[j - 2]))
        sg_tok[j] = P_.emit("act", lambda e: e.activation(
            out=sg, in_=w, func=AF.Sign), waits=ws, inc="sA", tag=f"q_sg{j}")

    def chain(i, q, bc):
        w, absw, sg, tt, scl, laund = _qtiles(T, i)
        tw = [("sP", bc)]
        if q == 0 and (i - 1) in d_tok:
            tw.append(("sG", d_tok[i - 1]))
        t_tok[(i, q)] = P_.emit("dve", lambda e, q=q: e.scalar_tensor_tensor(
            out=tt[:, q * 512:(q + 1) * 512], in0=bcreg[q % 3][:],
            scalar=M_THR, in1=absw[:, q * 512:(q + 1) * 512],
            op0=ALU.mult, op1=ALU.is_lt),
            waits=tw, inc="sV", tag=f"q_t{i}.{q}")
        g1w = [("sA", sg_tok[i])] if q == 0 else []
        P_.emit("dve", lambda e, q=q: e.tensor_tensor(
            out=tt[:, q * 512:(q + 1) * 512],
            in0=tt[:, q * 512:(q + 1) * 512],
            in1=sg[:, q * 512:(q + 1) * 512], op=ALU.mult),
            waits=g1w, inc="sV", tag=f"q_tsg{i}.{q}")
        wh_tok[(i, q)] = P_.emit("dve", lambda e, q=q: e.tensor_tensor(
            out=tt[:, q * 512:(q + 1) * 512],
            in0=tt[:, q * 512:(q + 1) * 512],
            in1=bcreg[q % 3][:], op=ALU.mult),
            inc="sV", tag=f"q_wh{i}.{q}")

    def do_bc(i, q, bar):
        w, absw, sg, tt, scl, laund = _qtiles(T, i)
        ws = [("sV", sc_tok[(i, q)])]
        if i == 0 and q == 0:
            ws += list(consts_ok)
        if bar:
            ws.append(("sV", P_.cnt["sV"]))
        prev = (i, 0) if q == 3 else (i - 1, (3, 1, 2)[q])
        if prev in wh_tok:
            ws.append(("sV", wh_tok[prev]))
        bc = P_.emit("pe", lambda e, q=q: e.matmul(
            bcreg[q % 3][:], T["c2"][:], scl[:, q * 512:(q + 1) * 512],
            start=True, stop=True), waits=ws, inc="sP", tag=f"q_bc{i}.{q}")
        if q == 3:
            bc3_tok[i] = bc
        chain(i, q, bc)

    def stage_a(i, bar=False):
        w, absw, sg, tt, scl, laund = _qtiles(T, i)
        for q in range(4):
            base = (0, 32, 64, 0)[q]
            ws = [("sA", ab_tok[i])] if q == 0 else []
            if i == 0 and q == 0:
                ws += list(consts_ok)
            if bar and q == 0:
                ws.append(("sV", P_.cnt["sV"]))
            prevq = (i, 0) if q == 3 else (i - 1, (3, 1, 2)[q])
            if prevq in sc_tok:
                ws.append(("sV", sc_tok[prevq]))
            mm = P_.emit("pe", lambda e, q=q, base=base: e.matmul(
                pbE[base:base + 2, 0:512], T["b2"][:],
                absw[:, q * 512:(q + 1) * 512], start=True, stop=True),
                waits=ws, inc="sP", tag=f"q_sum{i}.{q}")
            sum_tok[i] = mm
            ws2 = [("sP", mm)]
            if (i - 2) in bc3_tok:
                ws2.append(("sP", bc3_tok[i - 2]))
            sc_tok[(i, q)] = P_.emit("dve", lambda e, q=q, base=base:
                e.tensor_scalar(
                    out=scl[:, q * 512:(q + 1) * 512],
                    in0=pbE[base:base + 2, 0:512],
                    scalar1=1.0 / 64, scalar2=EPS8, op0=ALU.mult, op1=ALU.max),
                waits=ws2, inc="sV", tag=f"q_sc{i}.{q}")

    def stage_c(i, dst):
        w, absw, sg, tt, scl, laund = _qtiles(T, i)
        d_tok[i] = P_.emit("gp", lambda e: e.tensor_tensor(
            out=sg, in0=tt, in1=w, op=ALU.subtract),
            waits=[("sV", wh_tok[(i, 3)])], inc="sG", tag=f"q_d{i}")
        ws = [st_tok[i - 2]] if (i - 2) in st_tok else []
        wf_tok[i] = P_.emit("gp", lambda e: e.tensor_tensor(
            out=laund, in0=w, in1=sg, op=ALU.add),
            waits=ws, inc="sG", tag=f"q_wf{i}")
        st_tok[i] = P_.emit("gp", lambda e: e.dma_start(
            out=dst, in_=laund), waits=[("sG", wf_tok[i])], inc="dQ",
            tag=f"q_st{i}")

    bar = False
    for i in range(N + 1):
        if i == 0:
            do_ld(0)
            do_ld(1)
            do_ab(0)
        do_ld(i + 2)
        if i >= 1:
            for q in range(3):
                do_bc(i - 1, q, bar and q == 2)
        if i < N:
            stage_a(i, bar)
        if i >= 1:
            do_bc(i - 1, 3, False)
            stage_c(i - 1, jobs[i - 1][1])
        do_ab(i + 1)
        do_sg(i + 1 if i else 0)
        if i == 0:
            do_sg(1)
        if i < N:
            bar = yield


def _ln_block(P_, T, src, ntok, eps, tagp, first_waits=()):
    assert ntok <= 512
    pbE = T["pbE"]
    mm1 = mm2 = None
    for k in range(KC):
        ws = list(first_waits) if k == 0 else [("sP", mm2)]
        if k == 0:
            ws.append(("sP", P_.cnt["sP"]))
            ws.append(("sV", P_.cnt["sV"]))
            ws.extend(P_.consts_ok)
        cb = P_.emit("dve", lambda e, k=k: e.tensor_copy(
            out=T["LNXB"].ap()[:, 0:ntok], in_=src(k)), waits=ws,
            inc="sV", tag=f"{tagp}cb{k}")
        sq = P_.emit("act", lambda e, k=k: e.activation(
            out=T["LNXB"].ap()[:, 512:512 + ntok], in_=src(k), func=AF.Square),
            waits=(list(first_waits) if k == 0 else [("sP", mm2)]),
            inc="sA", tag=f"{tagp}sq{k}")
        mm1 = P_.emit("pe", lambda e, k=k: e.matmul(
            pbE[0:1, 0:ntok], T["ones1"][:], T["LNXB"].ap()[:, 0:ntok],
            start=(k == 0), stop=(k == KC - 1)),
            waits=[("sV", cb)], inc="sP", tag=f"{tagp}m1{k}")
        mm2 = P_.emit("pe", lambda e, k=k: e.matmul(
            pbE[0:1, 512:512 + ntok], T["ones1"][:],
            T["LNXB"].ap()[:, 512:512 + ntok],
            start=(k == 0), stop=(k == KC - 1)),
            waits=[("sA", sq)] + ([("sV", cb)] if k == 0 else []),
            inc="sP", tag=f"{tagp}m2{k}")
    a1 = P_.emit("dve", lambda e: e.tensor_scalar(
        out=T["stat1"].ap()[:, 0:ntok], in0=pbE[0:1, 0:ntok],
        scalar1=1.0 / D, scalar2=None, op0=ALU.mult),
        waits=[("sP", mm2)], inc="sV", tag=f"{tagp}mean")
    a2 = P_.emit("dve", lambda e: e.tensor_scalar(
        out=T["stat2"].ap()[:, 0:ntok], in0=pbE[0:1, 512:512 + ntok],
        scalar1=1.0 / D, scalar2=None, op0=ALU.mult),
        inc="sV", tag=f"{tagp}msq")
    m2 = P_.emit("dve", lambda e: e.tensor_tensor(
        out=pbE[0:1, 0:ntok], in0=T["stat1"].ap()[:, 0:ntok],
        in1=T["stat1"].ap()[:, 0:ntok], op=ALU.mult),
        inc="sV", tag=f"{tagp}m2")
    vr = P_.emit("dve", lambda e, eps=eps: e.scalar_tensor_tensor(
        out=T["stat2"].ap()[:, 0:ntok], in0=T["stat2"].ap()[:, 0:ntok],
        scalar=eps, in1=pbE[0:1, 0:ntok], op0=ALU.add, op1=ALU.subtract),
        inc="sV", tag=f"{tagp}var")
    sd = P_.emit("act", lambda e: e.activation(
        out=T["stat2"].ap()[:, 0:ntok], in_=T["stat2"].ap()[:, 0:ntok],
        func=AF.Sqrt), waits=[("sV", vr)], inc="sA", tag=f"{tagp}sd")
    rs = P_.emit("dve", lambda e: e.reciprocal(
        out=T["stat2"].ap()[:, 0:ntok], in_=T["stat2"].ap()[:, 0:ntok]),
        waits=[("sA", sd)], inc="sV", tag=f"{tagp}rstd")
    mr = P_.emit("dve", lambda e: e.tensor_tensor(
        out=T["stat1"].ap()[:, 0:ntok], in0=T["stat1"].ap()[:, 0:ntok],
        in1=T["stat2"].ap()[:, 0:ntok], op=ALU.mult),
        inc="sV", tag=f"{tagp}mr")
    b1 = P_.emit("pe", lambda e: e.matmul(
        pbE[:, 0:ntok], T["ones1f"][:], T["stat2"].ap()[:, 0:ntok],
        start=True, stop=True), waits=[("sV", rs), ("sV", vr)], inc="sP",
        tag=f"{tagp}bc1")
    b2v = P_.emit("pe", lambda e: e.matmul(
        pbE[:, 512:512 + ntok], T["ones1f"][:], T["stat1"].ap()[:, 0:ntok],
        start=True, stop=True), waits=[("sV", mr)], inc="sP", tag=f"{tagp}bc2")
    last = None
    for k in range(KC):
        P_.emit("dve", lambda e, k=k: e.tensor_tensor(
            out=src(k), in0=src(k), in1=pbE[:, 0:ntok], op=ALU.mult),
            waits=[("sP", b2v)] if k == 0 else [], inc="sV", tag=f"{tagp}ap1.{k}")
        last = P_.emit("dve", lambda e, k=k: e.tensor_tensor(
            out=src(k), in0=src(k), in1=pbE[:, 512:512 + ntok], op=ALU.subtract),
            inc="sV", tag=f"{tagp}ap2.{k}")
    return last


def _slab(P_, T, src_cols, half, waits, tag):
    return P_.emit("sync", lambda e: e.dma_start(
        out=T["AR_W"].ap()[:, half * 2048:(half + 1) * 2048].rearrange(
            "p (c t) -> p c t", c=16),
        in_=src_cols.rearrange("(c p) m -> p c m", p=128)),
        waits=waits, inc="dS", tag=tag)


def _attn1(P_, T, DR, qstep):
    pbG, pbH = T["pbG"], T["pbH"]
    lnC = lambda k: T["AR_C"].ap()[:, k * 512:(k + 1) * 512]
    slab_free = {0: 0, 1: 0}
    for tb in range(4):
        xl = P_.emit("sync", lambda e, tb=tb: e.dma_start(
            out=T["AR_C"].ap().rearrange("p (c t) -> p c t", c=16),
            in_=DR["xT"].rearrange("(c p) t -> p c t", p=128)[
                :, :, tb * 512:(tb + 1) * 512]),
            waits=[("sP", P_.cnt["sP"]), ("sV", P_.cnt["sV"])],
            inc="dS", tag=f"a1_x{tb}")
        ln_done = _ln_block(P_, T, lnC, 512, LN_EPS, f"a1ln{tb}_",
                            first_waits=[xl])
        ev_prev = {}
        st_prev = {}
        vst_prev = {}
        for kind in ("k", "v", "q"):
            wname = {"k": "wkT", "v": "wvT", "q": "wqT"}[kind]
            for m in range(16):
                half = m % 2
                sl = _slab(P_, T, DR[wname][:, m * 128:(m + 1) * 128], half,
                           waits=[("sP", slab_free[half])], tag=f"a1w{kind}{tb}.{m}")
                qstep(kind == "k" and m == 0)
                bank = {"k": pbG[:, 0:512], "v": pbG[:, 512:1024],
                        "q": pbH[:, 0:256]}[kind]
                mm = None
                for k in range(KC):
                    ws = [sl, ("sV", ln_done)] if k == 0 else []
                    if k == 0 and kind in ev_prev:
                        ws.append(("sA", ev_prev[kind]))
                    if kind == "q":
                        rhs = (T["AR_C"].ap()[:, k * 512:(k + 1) * 512]
                               .rearrange("p (a b t) -> p a b t", a=2, b=2)
                               [:, :, 0:1, :])
                    else:
                        rhs = T["AR_C"].ap()[:, k * 512:(k + 1) * 512]
                    mm = P_.emit("pe", lambda e, bank=bank, half=half, k=k, rhs=rhs:
                                 e.matmul(bank,
                                          T["AR_W"].ap()[:, half * 2048 + k * 128:
                                                         half * 2048 + (k + 1) * 128],
                                          rhs, start=(k == 0), stop=(k == KC - 1)),
                                 waits=ws, inc="sP", tag=f"a1{kind}{tb}.{m}.{k}")
                slab_free[half] = mm
                if kind in ("k", "q"):
                    sh, n = (0, 512) if kind == "k" else (1, 256)
                    evw = [("sP", mm)]
                    if kind in st_prev:
                        evw.append(st_prev[kind])
                    ev = P_.emit("act", lambda e, bank=bank, sh=sh, n=n: e.activation(
                        out=T["stg"].ap()[:, sh * 512:sh * 512 + n], in_=bank,
                        func=AF.Copy), waits=evw, inc="sA",
                        tag=f"a1ev{kind}{tb}.{m}")
                    dst = (DR["k_scr"][m * 128:(m + 1) * 128, tb * 512:(tb + 1) * 512]
                           if kind == "k" else
                           DR["q_scr"][m * 128:(m + 1) * 128, tb * 256:(tb + 1) * 256])
                    stv = P_.emit("act", lambda e, dst=dst, sh=sh, n=n: e.dma_start(
                        out=dst, in_=T["stg"].ap()[:, sh * 512:sh * 512 + n]),
                        waits=[], inc="dC", tag=f"a1st{kind}{tb}.{m}")
                    st_prev[kind] = stv
                    ev_prev[kind] = ev
                else:
                    evw = [("sP", mm)]
                    if "v_tr" in st_prev:
                        evw.append(("sP", st_prev["v_tr"]))
                    ev = P_.emit("act", lambda e, bank=bank: e.activation(
                        out=T["stg"].ap()[:, 512:1024], in_=bank, func=AF.Copy),
                        waits=evw, inc="sA", tag=f"a1evv{tb}.{m}")
                    ev_prev[kind] = ev
                    tr_prev = None
                    for kt in range(4):
                        trb = pbH[:, 512:640]
                        ws = [("sA", ev)] + list(P_.consts_ok)
                        if tr_prev is not None:
                            ws.append(("sV", tr_prev))
                        elif vst_prev:
                            ws.append(("sV", vst_prev["last_te"]))
                        tr = P_.emit("pe", lambda e, trb=trb, kt=kt: e.transpose(
                            trb, T["stg"].ap()[:, 512 + kt * 128:512 + (kt + 1) * 128],
                            T["ident"][:]), waits=ws, inc="sP",
                            tag=f"a1tr{tb}.{m}.{kt}")
                        pcol = 128 * (kt % 2)
                        tvw = [("sP", tr)]
                        if (m, kt - 2) in vst_prev:
                            tvw.append(vst_prev[(m, kt - 2)])
                        elif (m - 1, kt + 2) in vst_prev:
                            tvw.append(vst_prev[(m - 1, kt + 2)])
                        tvv = P_.emit("dve", lambda e, trb=trb, pcol=pcol:
                                      e.tensor_copy(
                                          out=T["ptb"].ap()[:, pcol:pcol + 128],
                                          in_=trb),
                                      waits=tvw, inc="sV", tag=f"a1te{tb}.{m}.{kt}")
                        tr_prev = tvv
                        stv = P_.emit(
                            "act", lambda e, tb=tb, m=m, kt=kt, pcol=pcol:
                            e.dma_start(
                                out=DR["v_scr"][tb * 512 + kt * 128:
                                                tb * 512 + (kt + 1) * 128,
                                                m * 128:(m + 1) * 128],
                                in_=T["ptb"].ap()[:, pcol:pcol + 128]),
                            waits=[("sV", tvv)], inc="dC", tag=f"a1sv{tb}.{m}.{kt}")
                        vst_prev[(m, kt)] = stv
                        vst_prev["last_st"] = stv
                        vst_prev["last_te"] = tvv
                    st_prev["v_tr"] = tr


def _attn2(P_, T, DR):
    BAP = T["AR_B"].ap()
    kh = BAP[:, 0:2048]
    vT = BAP[:, 2048:4096]
    qh = BAP[:, 4096:5120]
    prob = BAP[:, 5120:7168]
    msk = BAP[:, 7168:7168 + MSK_TOT]
    pbE, pbF, pbG, pbH = T["pbE"], T["pbF"], T["pbG"], T["pbH"]

    a1_done = P_.snap("dC")
    quant_bar = P_.snap("dQ") + P_.snap("dG") + [("sG", P_.cnt["sG"]),
                                                 ("sV", P_.cnt["sV"]),
                                                 ("sA", P_.cnt["sA"])]
    ml = P_.emit("sync", lambda e: e.dma_start(out=msk, in_=DR["qmask"]),
                 waits=list(quant_bar), inc="dS", tag="a2_msk")
    last_mm_on = {"kh": 0, "vT": 0, "qh": 0}
    for h in range(H):
        ws = (list(a1_done) + [ml] + list(quant_bar)) if h == 0 else []
        lk = P_.emit("sync", lambda e, h=h: e.dma_start(
            out=kh, in_=DR["k_scr"][h * 128:(h + 1) * 128, :]),
            waits=ws + [("sP", last_mm_on["kh"])], inc="dS", tag=f"a2lk{h}")
        lv = P_.emit("sync", lambda e, h=h: e.dma_start(
            out=vT.rearrange("p (c t) -> p c t", c=16),
            in_=DR["v_scr"].rearrange("(c p) d -> p c d", p=128)[
                :, :, h * 128:(h + 1) * 128]),
            waits=[("sP", last_mm_on["vT"])] if h else list(a1_done),
            inc="dS", tag=f"a2lv{h}")
        lq = P_.emit("sync", lambda e, h=h: e.dma_start(
            out=qh, in_=DR["q_scr"][h * 128:(h + 1) * 128, :]),
            waits=[("sP", last_mm_on["qh"])] if h else list(a1_done),
            inc="dS", tag=f"a2lq{h}")
        for qt in range(8):
            W = W_QT[qt]
            scb = [(pbE, 0), (pbE, 512), (pbF, 0), (pbF, 512)]
            mm = None
            for kb in range(W // 512):
                t_, c_ = scb[kb]
                ws = []
                if kb == 0:
                    ws = [lk, lq, ("sA", P_.cnt["sA"])]
                mm = P_.emit("pe", lambda e, t_=t_, c_=c_, kb=kb, qt=qt: e.matmul(
                    t_[:, c_:c_ + 512],
                    qh[:, qt * 128:(qt + 1) * 128],
                    kh[:, kb * 512:(kb + 1) * 512],
                    start=True, stop=True), waits=ws, inc="sP",
                    tag=f"a2sc{h}.{qt}.{kb}")
            lo, hi = 256 * qt, W
            mvs = []
            off = MSK_OFF[qt]
            seg = lo
            while seg < hi:
                segend = min(hi, (seg // 1024 + 1) * 1024)
                t_ = pbE if seg < 1024 else pbF
                c_ = seg % 1024
                n = segend - seg
                moff = off + (seg - lo)
                mvs.append(P_.emit("dve", lambda e, t_=t_, c_=c_, n=n, moff=moff:
                                   e.tensor_tensor(
                                       out=t_[:, c_:c_ + n], in0=t_[:, c_:c_ + n],
                                       in1=msk[:, moff:moff + n], op=ALU.add),
                                   waits=[("sP", mm)], inc="sV",
                                   tag=f"a2mk{h}.{qt}.{seg}"))
                seg = segend
            if W <= 1024:
                rmx = P_.emit("dve", lambda e, W=W: e.tensor_reduce(
                    out=T["negmax"][:], in_=pbE[:, 0:W], axis=AX.X, op=ALU.max,
                    negate=True), waits=[("sV", mvs[-1])], inc="sV",
                    tag=f"a2mx{h}.{qt}")
            else:
                P_.emit("dve", lambda e: e.tensor_reduce(
                    out=T["negmax"][:], in_=pbE[:, 0:1024], axis=AX.X, op=ALU.max,
                    negate=True), waits=[("sV", mvs[-1])], inc="sV",
                    tag=f"a2mxa{h}.{qt}")
                P_.emit("dve", lambda e, W=W: e.tensor_reduce(
                    out=T["bias1"][:], in_=pbF[:, 0:W - 1024], axis=AX.X, op=ALU.max,
                    negate=True), inc="sV", tag=f"a2mxb{h}.{qt}")
                rmx = P_.emit("dve", lambda e: e.tensor_tensor(
                    out=T["negmax"][:], in0=T["negmax"][:], in1=T["bias1"][:],
                    op=ALU.min), inc="sV", tag=f"a2mxc{h}.{qt}")
            bs = P_.emit("dve", lambda e: e.tensor_scalar(
                out=T["bias1"][:], in0=T["negmax"][:], scalar1=SCL_ATT,
                scalar2=None, op0=ALU.mult), waits=[("sV", rmx)], inc="sV",
                tag=f"a2bs{h}.{qt}")
            if W <= 1024:
                ex = P_.emit("act", lambda e, W=W: e.activation(
                    out=prob[:, 0:W], in_=pbE[:, 0:W], func=AF.Exp,
                    bias=T["bias1"][:], scale=SCL_ATT, accum_out=T["rowsum"][:]),
                    waits=[("sV", bs)], inc="sA", tag=f"a2ex{h}.{qt}")
                rsv = P_.emit("dve", lambda e: e.reciprocal(
                    out=T["recip"][:], in_=T["rowsum"][:]),
                    waits=[("sA", ex)], inc="sV", tag=f"a2rc{h}.{qt}")
            else:
                P_.emit("act", lambda e: e.activation(
                    out=prob[:, 0:1024], in_=pbE[:, 0:1024], func=AF.Exp,
                    bias=T["bias1"][:], scale=SCL_ATT, accum_out=T["rowsum"][:]),
                    waits=[("sV", bs)], inc="sA", tag=f"a2exa{h}.{qt}")
                ex = P_.emit("act", lambda e, W=W: e.activation(
                    out=prob[:, 1024:W], in_=pbF[:, 0:W - 1024], func=AF.Exp,
                    bias=T["bias1"][:], scale=SCL_ATT, accum_out=T["negmax"][:]),
                    inc="sA", tag=f"a2exb{h}.{qt}")
                sm = P_.emit("dve", lambda e: e.tensor_tensor(
                    out=T["rowsum"][:], in0=T["rowsum"][:], in1=T["negmax"][:],
                    op=ALU.add), waits=[("sA", ex)], inc="sV", tag=f"a2sm{h}.{qt}")
                rsv = P_.emit("dve", lambda e: e.reciprocal(
                    out=T["recip"][:], in_=T["rowsum"][:]), inc="sV",
                    tag=f"a2rc{h}.{qt}")
            pn = P_.emit("dve", lambda e, W=W: e.tensor_scalar(
                out=prob[:, 0:W], in0=prob[:, 0:W], scalar1=T["recip"][:],
                scalar2=None, op0=ALU.mult), waits=[("sV", rsv)], inc="sV",
                tag=f"a2pn{h}.{qt}")
            abank = pbH[:, 0:128] if qt % 2 == 0 else pbH[:, 512:640]
            te_prev = {}
            av_prev = {}
            mm_av = None
            for kc in range(W // 128):
                trb = pbG[:, 0:128] if kc % 2 == 0 else pbG[:, 512:640]
                ws = [("sV", pn)] + list(P_.consts_ok)
                if kc >= 2:
                    ws.append(("sV", te_prev[kc % 2]))
                tr = P_.emit("pe", lambda e, trb=trb, kc=kc: e.transpose(
                    trb, prob[:, kc * 128:(kc + 1) * 128], T["ident"][:]),
                    waits=ws, inc="sP", tag=f"a2tr{h}.{qt}.{kc}")
                pcol = 128 * (kc % 2)
                tew = [("sP", tr)]
                if kc - 2 in av_prev:
                    tew.append(("sP", av_prev[kc - 2]))
                te = P_.emit("dve", lambda e, trb=trb, pcol=pcol: e.tensor_copy(
                    out=T["ptb"].ap()[:, pcol:pcol + 128], in_=trb),
                    waits=tew, inc="sV", tag=f"a2te{h}.{qt}.{kc}")
                te_prev[kc % 2] = te
                ws2 = [("sV", te), lv]
                mm_av = P_.emit("pe", lambda e, kc=kc, pcol=pcol, abank=abank, W=W:
                                e.matmul(abank, vT[:, kc * 128:(kc + 1) * 128],
                                         T["ptb"].ap()[:, pcol:pcol + 128],
                                         start=(kc == 0), stop=(kc == W // 128 - 1)),
                                waits=ws2, inc="sP", tag=f"a2av{h}.{qt}.{kc}")
                av_prev[kc] = mm_av
            at_ = T["AR_C"] if h < 8 else T["AR_D"]
            hc = h % 8
            P_.emit("act", lambda e, hc=hc, qt=qt, abank=abank, at_=at_:
                    e.activation(
                        out=at_.ap()[:, hc * 1024 + qt * 128:
                                     hc * 1024 + (qt + 1) * 128],
                        in_=abank, func=AF.Copy), waits=[("sP", mm_av)], inc="sA",
                    tag=f"a2ae{h}.{qt}")
        last_mm_on = {"kh": P_.cnt["sP"], "vT": P_.cnt["sP"], "qh": P_.cnt["sP"]}


def _attn3(P_, T, DR):
    pbG = T["pbG"]
    oB = lambda m: T["AR_B"].ap()[:, m * 512:(m + 1) * 512]

    def a_ap(k, tb):
        at_ = T["AR_C"] if k < 8 else T["AR_D"]
        return at_.ap()[:, (k % 8) * 1024 + tb * 512:(k % 8) * 1024 + (tb + 1) * 512]

    slab_free = {0: P_.cnt["sP"], 1: P_.cnt["sP"]}
    for tb in range(2):
        a2_done = [("sA", P_.cnt["sA"])] if tb == 0 else P_.snap("dC")
        ev_prev = {}
        for m in range(16):
            half = m % 2
            sl = _slab(P_, T, DR["woT"][:, m * 128:(m + 1) * 128], half,
                       waits=[("sP", slab_free[half])], tag=f"a3w{tb}.{m}")
            bank = (pbG[:, 0:512], pbG[:, 512:1024])[m % 2]
            mm = None
            for k in range(KC):
                ws = ([sl] + (list(a2_done) if m == 0 else [])) if k == 0 else []
                if k == 0 and (m % 2) in ev_prev:
                    ws.append(("sA", ev_prev[m % 2]))
                if k == 0 and m == 0 and tb == 1:
                    ws.extend(P_.snap("dC"))
                mm = P_.emit("pe", lambda e, bank=bank, half=half, k=k, tb=tb:
                             e.matmul(bank,
                                      T["AR_W"].ap()[:, half * 2048 + k * 128:
                                                     half * 2048 + (k + 1) * 128],
                                      a_ap(k, tb),
                                      start=(k == 0), stop=(k == KC - 1)),
                             waits=ws, inc="sP", tag=f"a3o{tb}.{m}.{k}")
            slab_free[half] = mm
            ev_prev[m % 2] = P_.emit("act", lambda e, bank=bank, m=m: e.activation(
                out=oB(m), in_=bank, func=AF.Copy), waits=[("sP", mm)], inc="sA",
                tag=f"a3ev{tb}.{m}")
        ln_done = _ln_block(P_, T, oB, 512, LN_EPS, f"a3ln{tb}_",
                            first_waits=[("sA", P_.cnt["sA"])])
        ad_prev = {}
        for k in range(KC):
            sh = k % 2
            xlw = ([("sV", P_.cnt["sV"])] + P_.snap("dC")) if k < 2 else \
                  [("sV", ad_prev[k - 2])]
            xl = P_.emit("sync", lambda e, k=k, tb=tb, sh=sh: e.dma_start(
                out=T["stg"].ap()[:, sh * 512:(sh + 1) * 512].rearrange(
                    "p (a b t) -> p a b t", a=4, b=1),
                in_=DR["xT"][k * 128:(k + 1) * 128, :].rearrange(
                    "p (a b t) -> p a b t", a=8, b=2)[:, 4 * tb:4 * tb + 4, 0:1, :]),
                waits=xlw, inc="dS", tag=f"a3x{tb}.{k}")
            ad = P_.emit("dve", lambda e, k=k, sh=sh: e.tensor_tensor(
                out=oB(k), in0=oB(k), in1=T["stg"].ap()[:, sh * 512:(sh + 1) * 512],
                op=ALU.add),
                waits=[xl, ("sV", ln_done)], inc="sV", tag=f"a3x1{tb}.{k}")
            ad_prev[k] = ad
            P_.emit("act", lambda e, k=k, tb=tb: e.dma_start(
                out=DR["x1_scr"][k * 128:(k + 1) * 128, tb * 512:(tb + 1) * 512],
                in_=oB(k)), waits=[("sV", ad)], inc="dC", tag=f"a3s1{tb}.{k}")


def _ffn(P_, T, DR):
    pbE, pbF, pbG, pbH = T["pbE"], T["pbF"], T["pbG"], T["pbH"]
    lnD = lambda k: T["AR_D"].ap()[:, k * 256:(k + 1) * 256]
    lnY = lambda k: T["AR_D"].ap()[:, 4096 + k * 256: 4096 + (k + 1) * 256]
    l2h = lambda k: T["LN2H"].ap()[:, k * 256:(k + 1) * 256]
    l2l = lambda k: T["LN2L"].ap()[:, k * 256:(k + 1) * 256]
    hh = lambda c: T["HH"].ap()[:, c * 256:(c + 1) * 256]
    hl = lambda c: T["HL"].ap()[:, c * 256:(c + 1) * 256]
    wb = lambda q: T["AR_WB"].ap()[:, q * 2048:(q + 1) * 2048]

    x1_ok = P_.snap("dC") + P_.snap("dQ")
    for fb in range(4):
        xl = P_.emit("sync", lambda e, fb=fb: e.dma_start(
            out=T["AR_D"].ap()[:, 0:4096].rearrange("p (c t) -> p c t", c=16),
            in_=DR["x1_scr"].rearrange("(c p) t -> p c t", p=128)[
                :, :, fb * 256:(fb + 1) * 256]),
            waits=list(x1_ok) + [("sV", P_.cnt["sV"]), ("sP", P_.cnt["sP"])]
                  + P_.snap("dO"),
            inc="dS", tag=f"f_x1{fb}")
        ln2_done = _ln_block(P_, T, lnD, 256, LN_EPS, f"fln{fb}_",
                             first_waits=[xl])
        sp = None
        for k in range(KC):
            ws = [("sV", ln2_done), ("sP", P_.cnt["sP"])] if k == 0 else []
            P_.emit("dve", lambda e, k=k: e.tensor_copy(out=l2h(k), in_=lnD(k)),
                    waits=ws, inc="sV", tag=f"fsh{fb}.{k}")
            sp = P_.emit("dve", lambda e, k=k: e.tensor_tensor(
                out=l2l(k), in0=lnD(k), in1=l2h(k), op=ALU.subtract),
                inc="sV", tag=f"fsl{fb}.{k}")
        slabs = {}
        hsq_mm = None
        for c in range(64):
            r = c % 2
            gbank = (pbG[:, 0:256], pbG[:, 512:768])[r]
            ubank = (pbH[:, 0:256], pbH[:, 512:768])[r]
            wsl = [("sP", slabs[r])] if r in slabs else                 [("sP", P_.cnt["sP"])] + list(x1_ok)
            sl_g = P_.emit("sync", lambda e, c=c, r=r: e.dma_start(
                out=wb(2 * r).rearrange("p (kk t) -> p kk t", kk=16),
                in_=DR["wgu_hat"].rearrange("(kk p) m -> p kk m", p=128)[
                    :, :, c * 128:(c + 1) * 128]),
                waits=wsl, inc="dS", tag=f"fwg{fb}.{c}")
            sl_u = P_.emit("sync", lambda e, c=c, r=r: e.dma_start(
                out=wb(2 * r + 1).rearrange("p (kk t) -> p kk t", kk=16),
                in_=DR["wgu_hat"].rearrange("(kk p) m -> p kk m", p=128)[
                    :, :, 8192 + c * 128:8192 + (c + 1) * 128]),
                inc="dS", tag=f"fwu{fb}.{c}")
            mmu = None
            for k in range(KC):
                ws = [sl_g, sl_u, ("sV", sp)] if k == 0 else []
                for p, rhs_f in ((0, l2h), (1, l2l)):
                    P_.emit("pe", lambda e, gbank=gbank, r=r, k=k, rhs_f=rhs_f, p=p:
                            e.matmul(gbank, wb(2 * r)[:, k * 128:(k + 1) * 128],
                                     rhs_f(k), start=(k == 0 and p == 0),
                                     stop=(k == KC - 1 and p == 1)),
                            waits=ws if p == 0 else [], inc="sP",
                            tag=f"fg{fb}.{c}.{k}.{p}")
                    ws = []
                    mmu = P_.emit("pe", lambda e, ubank=ubank, r=r, k=k, rhs_f=rhs_f,
                                  p=p: e.matmul(
                                      ubank, wb(2 * r + 1)[:, k * 128:(k + 1) * 128],
                                      rhs_f(k), start=(k == 0 and p == 0),
                                      stop=(k == KC - 1 and p == 1)),
                                  inc="sP", tag=f"fu{fb}.{c}.{k}.{p}")
            slabs[r] = mmu
            sv = P_.emit("act", lambda e, gbank=gbank: e.activation(
                out=T["tmp"].ap()[:, 0:256], in_=gbank, func=AF.Silu),
                waits=[("sP", mmu)], inc="sA", tag=f"fsilu{fb}.{c}")
            ht = P_.emit("dve", lambda e, ubank=ubank: e.tensor_tensor(
                out=T["tmp"].ap()[:, 256:512], in0=T["tmp"].ap()[:, 0:256],
                in1=ubank, op=ALU.mult),
                waits=[("sA", sv)], inc="sV", tag=f"fht{fb}.{c}")
            P_.emit("dve", lambda e, c=c: e.tensor_copy(
                out=hh(c), in_=T["tmp"].ap()[:, 256:512]), inc="sV",
                tag=f"fhh{fb}.{c}")
            hv = P_.emit("dve", lambda e, c=c: e.tensor_tensor(
                out=hl(c), in0=T["tmp"].ap()[:, 256:512], in1=hh(c),
                op=ALU.subtract), inc="sV", tag=f"fhl{fb}.{c}")
            hq = P_.emit("act", lambda e: e.activation(
                out=T["LNXB"].ap()[:, 0:256], in_=T["tmp"].ap()[:, 256:512],
                func=AF.Square),
                waits=[("sV", ht)], inc="sA", tag=f"fhsq{fb}.{c}")
            hsq_mm = P_.emit("pe", lambda e, c=c: e.matmul(
                pbE[0:1, 0:256], T["ones1"][:], T["LNXB"].ap()[:, 0:256],
                start=(c == 0), stop=(c == 63)),
                waits=[("sA", hq)], inc="sP", tag=f"fhs{fb}.{c}")
        r1 = P_.emit("dve", lambda e: e.tensor_scalar(
            out=T["stat2"].ap()[:, 0:256], in0=pbE[0:1, 0:256], scalar1=1.0 / HF,
            scalar2=RMS_EPS, op0=ALU.mult, op1=ALU.add),
            waits=[("sP", hsq_mm)], inc="sV", tag=f"fr1{fb}")
        r2 = P_.emit("act", lambda e: e.activation(
            out=T["stat2"].ap()[:, 0:256], in_=T["stat2"].ap()[:, 0:256],
            func=AF.Sqrt), waits=[("sV", r1)], inc="sA", tag=f"fr2{fb}")
        r3 = P_.emit("dve", lambda e: e.reciprocal(
            out=T["stat2"].ap()[:, 0:256], in_=T["stat2"].ap()[:, 0:256]),
            waits=[("sA", r2)], inc="sV", tag=f"fr3{fb}")
        r4 = P_.emit("pe", lambda e: e.matmul(
            pbE[:, 512:768], T["ones1f"][:], T["stat2"].ap()[:, 0:256],
            start=True, stop=True), waits=[("sV", r3)], inc="sP", tag=f"fr4{fb}")
        r5 = P_.emit("dve", lambda e: e.tensor_copy(
            out=T["tmp"].ap()[:, 0:256], in_=pbE[:, 512:768]),
            waits=[("sP", r4)], inc="sV", tag=f"fr5{fb}")
        banks = [pbG[:, 0:256], pbG[:, 512:768], pbH[:, 0:256], pbH[:, 512:768],
                 pbE[:, 0:256], pbE[:, 512:768], pbF[:, 0:256], pbF[:, 512:768]]
        yv = {}
        for sw in range(2):
            for dm in range(8):
                dc = sw * 8 + dm
                bank = banks[dm]
                mm = None
                for ss in range(4):
                    half = ss % 2
                    wsl = [("sP", P_.cnt["sP"] if (sw or dm or ss) else 0)]
                    if dc == 0 and ss == 0:
                        wsl = [("sV", r5), ("sP", P_.cnt["sP"])]
                    sl = P_.emit("sync", lambda e, dc=dc, ss=ss, half=half:
                                 e.dma_start(
                                     out=wb(half).rearrange(
                                         "p (kk t) -> p kk t", kk=16),
                                     in_=DR["wp_hat"].rearrange(
                                         "(kk p) m -> p kk m", p=128)[
                                         :, ss * 16:(ss + 1) * 16,
                                         dc * 128:(dc + 1) * 128]),
                                 waits=wsl, inc="dS", tag=f"fpw{fb}.{dc}.{ss}")
                    for kk in range(16):
                        hk = ss * 16 + kk
                        ws = [sl] if kk == 0 else []
                        if dm == 0 and ss == 0 and kk == 0 and sw == 1:
                            ws.append(("sV", yv[7]))
                        for p, h_f in ((0, hh), (1, hl)):
                            mm = P_.emit("pe", lambda e, bank=bank, half=half,
                                         kk=kk, hk=hk, h_f=h_f, p=p: e.matmul(
                                             bank, wb(half)[:, kk * 128:
                                                            (kk + 1) * 128],
                                             h_f(hk), start=(hk == 0 and p == 0),
                                             stop=(hk == 63 and p == 1)),
                                         waits=ws if p == 0 else [], inc="sP",
                                         tag=f"fp{fb}.{dc}.{hk}.{p}")
                            ws = []
                yv[dm] = P_.emit("dve", lambda e, bank=bank, dc=dc: e.tensor_tensor(
                    out=lnY(dc), in0=bank, in1=T["tmp"].ap()[:, 0:256],
                    op=ALU.mult), waits=[("sP", mm)], inc="sV", tag=f"fy{fb}.{dc}")
        ln_done = _ln_block(P_, T, lnY, 256, LN_EPS, f"fyln{fb}_",
                            first_waits=[("sV", yv[7])])
        ad_prev = {}
        for k in range(KC):
            sh = k % 2
            xlw = ([("sV", P_.cnt["sV"])] + P_.snap("dC")) if k < 2 else \
                  [("sV", ad_prev[k - 2])]
            xl = P_.emit("sync", lambda e, k=k, fb=fb, sh=sh: e.dma_start(
                out=T["stg"].ap()[:, sh * 512:sh * 512 + 256],
                in_=DR["x1_scr"][k * 128:(k + 1) * 128, fb * 256:(fb + 1) * 256]),
                waits=xlw, inc="dS", tag=f"fx1{fb}.{k}")
            ad = P_.emit("dve", lambda e, k=k, sh=sh: e.tensor_tensor(
                out=lnY(k), in0=lnY(k), in1=T["stg"].ap()[:, sh * 512:sh * 512 + 256],
                op=ALU.add), waits=[xl, ("sV", ln_done)], inc="sV",
                tag=f"fx2{fb}.{k}")
            ad_prev[k] = ad
            P_.emit("sync", lambda e, k=k, fb=fb: e.dma_start(
                out=DR["out"][k * 128:(k + 1) * 128, fb * 256:(fb + 1) * 256],
                in_=lnY(k)), waits=[("sV", ad)], inc="dO", tag=f"fo{fb}.{k}")


def build_program():
    nc = bass.Bass("TRN2", target_bir_lowering=False)
    T, DR = _alloc(nc)
    P_ = Prog()
    phases = _DBG_PHASES or ("quant", "attn1", "attn2", "attn3", "ffn")

    import contextlib
    with contextlib.ExitStack() as stack:
        sems = {nm: stack.enter_context(nc.semaphore(nm)) for nm in Prog.SEMS}
        block = stack.enter_context(nc.Block())
        for nm in ("b2", "c2", "ones1", "ones1f", "ident"):
            P_.emit("sync", lambda e, nm=nm: e.dma_start(out=T[nm][:], in_=DR[nm]),
                    inc="dS", tag=f"c_{nm}")
        P_.consts_ok = P_.snap("dS")
        qg = iter(())
        if "quant" in phases:
            qg = _quant_gen(P_, T, DR, P_.consts_ok)
            next(qg, None)

        def qstep(bar=False):
            try:
                qg.send(bar)
            except StopIteration:
                pass

        if "attn1" in phases:
            _attn1(P_, T, DR, qstep)
        for _ in qg:
            pass
        if "attn2" in phases:
            _attn2(P_, T, DR)
        if "attn3" in phases:
            _attn3(P_, T, DR)
        if "ffn" in phases:
            _ffn(P_, T, DR)
        fin = []
        for cls in Prog.DCLS:
            fin.extend(P_.snap(cls))
        P_.emit("sync", None, waits=fin, tag="fin")
        P_.render(block, sems)
    return nc


def _host_inputs(inputs):
    x = np.asarray(inputs["L0_attn_x"], dtype=np.float32)
    wq = np.asarray(inputs["L0_attn_wq"], dtype=np.float32)
    wk = np.asarray(inputs["L0_attn_wk"], dtype=np.float32)
    wv = np.asarray(inputs["L0_attn_wv"], dtype=np.float32)
    wo = np.asarray(inputs["L0_attn_wo"], dtype=np.float32)
    wgu = np.asarray(inputs["L0_ffn_w_gate_up"], dtype=np.float32)
    wp = np.asarray(inputs["L0_ffn_w_proj"], dtype=np.float32)

    wqT = np.ascontiguousarray(wq.T)
    wkT = np.ascontiguousarray(wk.T)
    wvT = np.ascontiguousarray(wv.T)
    woT = np.ascontiguousarray(wo.T)
    wguT = np.ascontiguousarray(wgu.T)
    wpT = np.ascontiguousarray(wp.T)

    b2 = np.zeros((128, 2), dtype=bfloat16)
    b2[:64, 0] = 1
    b2[64:, 1] = 1
    c2 = np.ascontiguousarray(b2.T)
    ones1 = np.ones((128, 1), dtype=bfloat16)
    ones1f = np.ones((1, 128), dtype=np.float32)
    ident = np.eye(128, dtype=np.float32)

    in_maps = []
    for c in range(8):
        b, hf = c // 2, c % 2
        xt = np.ascontiguousarray(x[b].T)
        if hf == 1:
            xt = np.ascontiguousarray(
                xt.reshape(D, S // 256, 2, 128)[:, :, ::-1, :].reshape(D, S))
        qmask = np.zeros((128, MSK_TOT), dtype=np.float32)
        for i in range(8):
            grow = 256 * i + 128 * hf + np.arange(128)[:, None]
            lk = 256 * i + np.arange(MSK_W[i])[None, :]
            gk = ((lk // 128) ^ hf) * 128 + lk % 128
            qmask[:, MSK_OFF[i]:MSK_OFF[i + 1]] = np.where(gk <= grow, 0.0, NEG_BIG)
        in_maps.append(dict(
            xT=xt, wqT=wqT, wkT=wkT, wvT=wvT, woT=woT, wguT=wguT, wpT=wpT,
            qmask=qmask, b2=b2, c2=c2, ones1=ones1, ones1f=ones1f, ident=ident))
    return in_maps


def _gather(results):
    out = np.empty((B, S, D), dtype=np.float32)
    for c in range(8):
        b, hf = c // 2, c % 2
        o = results[c]["out"]
        ot = o.T
        for i in range(8):
            out[b, 256 * i + 128 * hf:256 * i + 128 * hf + 128, :] = \
                ot[i * 128:(i + 1) * 128, :]
    return out


def kernel(**inputs):
    in_maps = _host_inputs(inputs)
    nc = build_program()
    res = run_bass_kernel_spmd(nc, in_maps, core_ids=list(range(8)))
    return _gather(res.results)


# version 18 (final)
# speedup vs baseline: 1.2673x; 1.0311x over previous
import numpy as np
import ml_dtypes
import concourse.bass as bass
import concourse.mybir as mybir
from concourse.bass_utils import run_bass_kernel_spmd

F32 = mybir.dt.float32
BF16 = mybir.dt.bfloat16
I16 = mybir.dt.int16
ALU = mybir.AluOpType
AF = mybir.ActivationFunctionType
AX = mybir.AxisListType
bfloat16 = ml_dtypes.bfloat16

B, S, D, H, HD = 4, 2048, 2048, 16, 128
TOK = 1024
KC = D // 128
GU, HF = 16384, 8192
M_THR = 0.501953125
EPS8 = float(np.float32(bfloat16(1e-8)))
LN_EPS = 1e-5
RMS_EPS = float(np.float32(np.finfo(np.float32).eps))
NEG_BIG = -1.0e30
SCL_ATT = 1.0 / float(np.sqrt(HD))
W_QT = [512, 1024, 1024, 1536, 1536, 2048, 2048, 2048]
MSK_W = [W_QT[i] - 256 * i for i in range(8)]
MSK_OFF = np.cumsum([0] + MSK_W).tolist()
MSK_TOT = MSK_OFF[-1]

_DBG_OUTS = ()
_DBG_PHASES = None
_DBG_QSTORE = None
_DBG_QLIMIT = None


class Prog:
    ENGS = ("pe", "dve", "act", "gp", "sync")
    NROT = 4
    DCLS = ("dS", "dG", "dC", "dO", "dQ")
    SEMS = ("sP", "sV", "sA", "sG") + tuple(
        f"{c}{r}" for c in ("dS", "dG", "dC", "dO", "dQ") for r in range(4))

    def __init__(self):
        self.recs = {e: [] for e in self.ENGS}
        self.cnt = {s: 0 for s in self.SEMS}
        self.rr = {c: 0 for c in self.DCLS}
        self.consts_ok = []

    def snap(self, cls):
        return [(f"{cls}{r}", self.cnt[f"{cls}{r}"]) for r in range(self.NROT)
                if self.cnt[f"{cls}{r}"] > 0]

    def emit(self, eng, fn, waits=(), inc=None, tag=""):
        waits = list(waits)
        if inc is not None and inc in self.DCLS:
            phys = f"{inc}{self.rr[inc] % self.NROT}"
            self.rr[inc] += 1
            if self.cnt[phys] > 0:
                waits.append((phys, self.cnt[phys]))
            inc = phys
        for (sem, v) in waits:
            assert v <= self.cnt[sem], f"wait on future {sem}={v}>{self.cnt[sem]} at {tag}"
        post = None
        if inc is not None:
            self.cnt[inc] += 16 if inc.startswith("d") else 1
            post = self.cnt[inc]
        self.recs[eng].append((tuple(waits), fn, inc, post, tag))
        return (inc, post) if inc is not None and inc.startswith("d") else post

    def render(self, block, sems):
        def rnd(eng_obj, recs, own_sem=None):
            last_own = 0
            for waits, fn, inc, post, tag in recs:
                if own_sem is not None and last_own > 0 and fn is not None:
                    eng_obj.wait_ge(sems[own_sem], last_own)
                for (sem, v) in waits:
                    if v > 0:
                        eng_obj.wait_ge(sems[sem], v)
                if fn is not None:
                    ins = fn(eng_obj)
                    if inc is not None:
                        ins.then_inc(sems[inc], 16 if inc.startswith("d") else 1)
                if inc == own_sem and post is not None:
                    last_own = post

        @block.tensor
        def _(t):
            rnd(t, self.recs["pe"])

        @block.vector
        def _(v):
            rnd(v, self.recs["dve"], own_sem="sV")

        @block.scalar
        def _(s):
            rnd(s, self.recs["act"], own_sem="sA")

        @block.gpsimd
        def _(g):
            rnd(g, self.recs["gp"], own_sem="sG")

        @block.sync
        def _(s):
            rnd(s, self.recs["sync"])


def _alloc(nc):
    T, DR = {}, {}
    DR["xT"] = nc.dram_tensor("xT", [D, S], F32, kind="ExternalInput").ap()
    for w in ("wqT", "wkT", "wvT", "woT"):
        DR[w] = nc.dram_tensor(w, [D, D], F32, kind="ExternalInput").ap()
    DR["wguT"] = nc.dram_tensor("wguT", [D, GU], F32, kind="ExternalInput").ap()
    DR["wpT"] = nc.dram_tensor("wpT", [HF, D], F32, kind="ExternalInput").ap()
    DR["qmask"] = nc.dram_tensor("qmask", [128, MSK_TOT], F32, kind="ExternalInput").ap()
    DR["b2"] = nc.dram_tensor("b2", [128, 2], BF16, kind="ExternalInput").ap()
    DR["c2"] = nc.dram_tensor("c2", [2, 128], BF16, kind="ExternalInput").ap()
    DR["ones1"] = nc.dram_tensor("ones1", [128, 1], BF16, kind="ExternalInput").ap()
    DR["ones1f"] = nc.dram_tensor("ones1f", [1, 128], F32, kind="ExternalInput").ap()
    DR["ident"] = nc.dram_tensor("ident", [128, 128], F32, kind="ExternalInput").ap()
    DR["out"] = nc.dram_tensor("out", [D, TOK], F32, kind="ExternalOutput").ap()
    for nm, shape, dt_ in (("wgu_hat", [D, GU], BF16), ("wp_hat", [HF, D], BF16),
                           ("k_scr", [D, S], F32), ("v_scr", [S, D], F32),
                           ("q_scr", [D, TOK], F32), ("x1_scr", [D, TOK], F32)):
        if nm in _DBG_OUTS:
            DR[nm] = nc.dram_tensor(nm, shape, dt_, kind="ExternalOutput").ap()
        else:
            DR[nm] = nc.dram_tensor(nm, shape, dt_).ap()

    al = nc.alloc_sbuf_tensor
    T["b2"] = al("b2_t", [128, 2], BF16)
    T["c2"] = al("c2_t", [2, 128], BF16)
    T["ones1"] = al("ones1_t", [128, 1], BF16)
    T["ones1f"] = al("ones1f_t", [1, 128], F32)
    T["negmax"] = al("negmax", [128, 1], F32)
    T["bias1"] = al("bias1", [128, 1], F32)
    T["rowsum"] = al("rowsum", [128, 1], F32)
    T["recip"] = al("recip", [128, 1], F32)

    BASE = 18432
    at = lambda nm, shape, dt_, offset: nc.alloc_sbuf_tensor_at(
        nm, shape, dt_, offset=BASE + offset)
    T["QW"] = at("QW", [128, 8192], BF16, 0)
    T["QABS"] = at("QABS", [128, 4096], BF16, 16384)
    T["QSG"] = at("QSG", [128, 4096], BF16, 24576)
    T["QT"] = at("QT", [128, 2048], BF16, 32768)
    T["QBB"] = at("QBB", [128, 2560], F32, 36864)
    T["QSCL"] = at("QSCL", [2, 4096], BF16, 47104)
    T["QLND"] = at("QLND", [128, 4096], I16, 55296)
    T["AR_A"] = at("AR_A", [128, 16384], F32, 0)
    T["AR_B"] = at("AR_B", [128, 12288], F32, 65536)
    T["AR_C"] = at("AR_C", [128, 8192], F32, 114688)
    T["AR_D"] = at("AR_D", [128, 8192], F32, 147456)
    T["AR_W"] = at("AR_W", [128, 4096], F32, 180224)
    T["LNXB"] = at("LNXB", [128, 1024], BF16, 196608)
    T["stg"] = at("stg", [128, 1024], F32, 198656)
    T["tmp"] = at("tmp", [128, 512], F32, 202752)
    T["ptb"] = at("ptb", [128, 256], F32, 204800)
    T["ident"] = at("ident_t", [128, 128], F32, 205824)
    T["stat1"] = at("stat1", [1, 512], F32, 206336)
    T["stat2"] = at("stat2", [1, 512], F32, 208384)
    T["HH"] = at("HH", [128, 16384], BF16, 0)
    T["HL"] = at("HL", [128, 16384], BF16, 32768)
    T["LN2H"] = at("LN2H", [128, 4096], BF16, 65536)
    T["LN2L"] = at("LN2L", [128, 4096], BF16, 65536 + 8192)
    T["AR_WB"] = at("AR_WB", [128, 8192], BF16, 180224)

    T["pbE"] = nc.alloc_psum_tensor("pbE", [128, 1024], F32)
    T["pbF"] = nc.alloc_psum_tensor("pbF", [128, 1024], F32)
    T["pbG"] = nc.alloc_psum_tensor("pbG", [128, 1024], F32)
    T["pbH"] = nc.alloc_psum_tensor("pbH", [128, 1024], F32)
    return T, DR


def _qtiles(T, i):
    s4 = slice((i % 4) * 2048, (i % 4) * 2048 + 2048)
    s2 = slice((i % 2) * 2048, (i % 2) * 2048 + 2048)
    return (T["QW"].ap()[:, s4], T["QABS"].ap()[:, s2], T["QSG"].ap()[:, s2],
            T["QT"].ap()[:, 0:2048], T["QSCL"].ap()[:, s2],
            T["QLND"].ap().bitcast(BF16)[:, s2])


def _quant_gen(P_, T, DR, consts_ok):
    FQ = 2048
    jobs = []
    for d0 in range(0, D, 128):
        for h0 in range(0, GU, FQ):
            jobs.append((DR["wguT"][d0:d0 + 128, h0:h0 + FQ],
                         DR["wgu_hat"][d0:d0 + 128, h0:h0 + FQ]))
    for h0 in range(0, HF, 128):
        for d0 in range(0, D, FQ):
            jobs.append((DR["wpT"][h0:h0 + 128, d0:d0 + FQ],
                         DR["wp_hat"][h0:h0 + 128, d0:d0 + FQ]))
    if _DBG_QLIMIT:
        jobs = jobs[:_DBG_QLIMIT]
    N = len(jobs)
    pbE, pbF = T["pbE"], T["pbF"]
    bcreg = [pbF[:, 0:512], pbF[:, 512:1024], pbE[:, 512:1024]]
    ld_tok, ab_tok, sg_tok = {}, {}, {}
    sc_tok, t_tok, wh_tok, sum_tok = {}, {}, {}, {}
    d_tok, wf_tok, st_tok, bc3_tok = {}, {}, {}, {}

    def do_ld(j):
        if j >= N:
            return
        w = _qtiles(T, j)[0]
        ld_tok[j] = P_.emit("gp", lambda e: e.dma_start(
            out=w, in_=jobs[j][0]), inc="dG", tag=f"q_ld{j}")

    def do_ab(j):
        if j >= N:
            return
        w, absw = _qtiles(T, j)[0:2]
        ws = [ld_tok[j]]
        if (j - 2, 3) in t_tok:
            ws += [("sV", t_tok[(j - 2, 3)]), ("sP", sum_tok[j - 2])]
        ab_tok[j] = P_.emit("act", lambda e: e.activation(
            out=absw, in_=w, func=AF.Abs),
            waits=ws, inc="sA", tag=f"q_ab{j}")

    def do_sg(j):
        if j >= N:
            return
        w, _, sg = _qtiles(T, j)[0:3]
        ws = [ld_tok[j]]
        if (j - 2) in wf_tok:
            ws.append(("sG", wf_tok[j - 2]))
        sg_tok[j] = P_.emit("act", lambda e: e.activation(
            out=sg, in_=w, func=AF.Sign), waits=ws, inc="sA", tag=f"q_sg{j}")

    def chain(i, q, bc):
        w, absw, sg, tt, scl, laund = _qtiles(T, i)
        tw = [("sP", bc)]
        if q == 0 and (i - 1) in d_tok:
            tw.append(("sG", d_tok[i - 1]))
        t_tok[(i, q)] = P_.emit("dve", lambda e, q=q: e.scalar_tensor_tensor(
            out=tt[:, q * 512:(q + 1) * 512], in0=bcreg[q % 3][:],
            scalar=M_THR, in1=absw[:, q * 512:(q + 1) * 512],
            op0=ALU.mult, op1=ALU.is_lt),
            waits=tw, inc="sV", tag=f"q_t{i}.{q}")
        g1w = [("sA", sg_tok[i])] if q == 0 else []
        P_.emit("dve", lambda e, q=q: e.tensor_tensor(
            out=tt[:, q * 512:(q + 1) * 512],
            in0=tt[:, q * 512:(q + 1) * 512],
            in1=sg[:, q * 512:(q + 1) * 512], op=ALU.mult),
            waits=g1w, inc="sV", tag=f"q_tsg{i}.{q}")
        wh_tok[(i, q)] = P_.emit("dve", lambda e, q=q: e.tensor_tensor(
            out=tt[:, q * 512:(q + 1) * 512],
            in0=tt[:, q * 512:(q + 1) * 512],
            in1=bcreg[q % 3][:], op=ALU.mult),
            inc="sV", tag=f"q_wh{i}.{q}")

    def do_bc(i, q, bar):
        w, absw, sg, tt, scl, laund = _qtiles(T, i)
        ws = [("sV", sc_tok[(i, q)])]
        if i == 0 and q == 0:
            ws += list(consts_ok)
        if bar:
            ws.append(("sV", P_.cnt["sV"]))
        prev = (i, 0) if q == 3 else (i - 1, (3, 1, 2)[q])
        if prev in wh_tok:
            ws.append(("sV", wh_tok[prev]))
        bc = P_.emit("pe", lambda e, q=q: e.matmul(
            bcreg[q % 3][:], T["c2"][:], scl[:, q * 512:(q + 1) * 512],
            start=True, stop=True), waits=ws, inc="sP", tag=f"q_bc{i}.{q}")
        if q == 3:
            bc3_tok[i] = bc
        chain(i, q, bc)

    def stage_a(i, bar=False):
        w, absw, sg, tt, scl, laund = _qtiles(T, i)
        for q in range(4):
            base = (0, 32, 64, 0)[q]
            ws = [("sA", ab_tok[i])] if q == 0 else []
            if i == 0 and q == 0:
                ws += list(consts_ok)
            if bar and q == 0:
                ws.append(("sV", P_.cnt["sV"]))
            prevq = (i, 0) if q == 3 else (i - 1, (3, 1, 2)[q])
            if prevq in sc_tok:
                ws.append(("sV", sc_tok[prevq]))
            mm = P_.emit("pe", lambda e, q=q, base=base: e.matmul(
                pbE[base:base + 2, 0:512], T["b2"][:],
                absw[:, q * 512:(q + 1) * 512], start=True, stop=True),
                waits=ws, inc="sP", tag=f"q_sum{i}.{q}")
            sum_tok[i] = mm
            ws2 = [("sP", mm)]
            if (i - 2) in bc3_tok:
                ws2.append(("sP", bc3_tok[i - 2]))
            sc_tok[(i, q)] = P_.emit("dve", lambda e, q=q, base=base:
                e.tensor_scalar(
                    out=scl[:, q * 512:(q + 1) * 512],
                    in0=pbE[base:base + 2, 0:512],
                    scalar1=1.0 / 64, scalar2=EPS8, op0=ALU.mult, op1=ALU.max),
                waits=ws2, inc="sV", tag=f"q_sc{i}.{q}")

    def stage_c(i, dst):
        w, absw, sg, tt, scl, laund = _qtiles(T, i)
        d_tok[i] = P_.emit("gp", lambda e: e.tensor_tensor(
            out=sg, in0=tt, in1=w, op=ALU.subtract),
            waits=[("sV", wh_tok[(i, 3)])], inc="sG", tag=f"q_d{i}")
        ws = [st_tok[i - 2]] if (i - 2) in st_tok else []
        wf_tok[i] = P_.emit("gp", lambda e: e.tensor_tensor(
            out=laund, in0=w, in1=sg, op=ALU.add),
            waits=ws, inc="sG", tag=f"q_wf{i}")
        st_tok[i] = P_.emit("gp", lambda e: e.dma_start(
            out=dst, in_=laund), waits=[("sG", wf_tok[i])], inc="dQ",
            tag=f"q_st{i}")

    bar = False
    for i in range(N + 1):
        if i == 0:
            do_ld(0)
            do_ld(1)
            do_ab(0)
        do_ld(i + 2)
        if i < N:
            stage_a(i, bar)
        if i >= 1:
            for q in range(3):
                do_bc(i - 1, q, bar and q == 2)
            do_bc(i - 1, 3, False)
            stage_c(i - 1, jobs[i - 1][1])
        do_ab(i + 1)
        do_sg(i + 1 if i else 0)
        if i == 0:
            do_sg(1)
        if i < N:
            bar = yield


def _ln_block(P_, T, src, ntok, eps, tagp, first_waits=()):
    assert ntok <= 512
    pbE = T["pbE"]
    mm1 = mm2 = None
    for k in range(KC):
        ws = list(first_waits) if k == 0 else [("sP", mm2)]
        if k == 0:
            ws.append(("sP", P_.cnt["sP"]))
            ws.append(("sV", P_.cnt["sV"]))
            ws.extend(P_.consts_ok)
        cb = P_.emit("dve", lambda e, k=k: e.tensor_copy(
            out=T["LNXB"].ap()[:, 0:ntok], in_=src(k)), waits=ws,
            inc="sV", tag=f"{tagp}cb{k}")
        sq = P_.emit("act", lambda e, k=k: e.activation(
            out=T["LNXB"].ap()[:, 512:512 + ntok], in_=src(k), func=AF.Square),
            waits=(list(first_waits) if k == 0 else [("sP", mm2)]),
            inc="sA", tag=f"{tagp}sq{k}")
        mm1 = P_.emit("pe", lambda e, k=k: e.matmul(
            pbE[0:1, 0:ntok], T["ones1"][:], T["LNXB"].ap()[:, 0:ntok],
            start=(k == 0), stop=(k == KC - 1)),
            waits=[("sV", cb)], inc="sP", tag=f"{tagp}m1{k}")
        mm2 = P_.emit("pe", lambda e, k=k: e.matmul(
            pbE[0:1, 512:512 + ntok], T["ones1"][:],
            T["LNXB"].ap()[:, 512:512 + ntok],
            start=(k == 0), stop=(k == KC - 1)),
            waits=[("sA", sq)] + ([("sV", cb)] if k == 0 else []),
            inc="sP", tag=f"{tagp}m2{k}")
    a1 = P_.emit("dve", lambda e: e.tensor_scalar(
        out=T["stat1"].ap()[:, 0:ntok], in0=pbE[0:1, 0:ntok],
        scalar1=1.0 / D, scalar2=None, op0=ALU.mult),
        waits=[("sP", mm2)], inc="sV", tag=f"{tagp}mean")
    a2 = P_.emit("dve", lambda e: e.tensor_scalar(
        out=T["stat2"].ap()[:, 0:ntok], in0=pbE[0:1, 512:512 + ntok],
        scalar1=1.0 / D, scalar2=None, op0=ALU.mult),
        inc="sV", tag=f"{tagp}msq")
    m2 = P_.emit("dve", lambda e: e.tensor_tensor(
        out=pbE[0:1, 0:ntok], in0=T["stat1"].ap()[:, 0:ntok],
        in1=T["stat1"].ap()[:, 0:ntok], op=ALU.mult),
        inc="sV", tag=f"{tagp}m2")
    vr = P_.emit("dve", lambda e, eps=eps: e.scalar_tensor_tensor(
        out=T["stat2"].ap()[:, 0:ntok], in0=T["stat2"].ap()[:, 0:ntok],
        scalar=eps, in1=pbE[0:1, 0:ntok], op0=ALU.add, op1=ALU.subtract),
        inc="sV", tag=f"{tagp}var")
    sd = P_.emit("act", lambda e: e.activation(
        out=T["stat2"].ap()[:, 0:ntok], in_=T["stat2"].ap()[:, 0:ntok],
        func=AF.Sqrt), waits=[("sV", vr)], inc="sA", tag=f"{tagp}sd")
    rs = P_.emit("dve", lambda e: e.reciprocal(
        out=T["stat2"].ap()[:, 0:ntok], in_=T["stat2"].ap()[:, 0:ntok]),
        waits=[("sA", sd)], inc="sV", tag=f"{tagp}rstd")
    mr = P_.emit("dve", lambda e: e.tensor_tensor(
        out=T["stat1"].ap()[:, 0:ntok], in0=T["stat1"].ap()[:, 0:ntok],
        in1=T["stat2"].ap()[:, 0:ntok], op=ALU.mult),
        inc="sV", tag=f"{tagp}mr")
    b1 = P_.emit("pe", lambda e: e.matmul(
        pbE[:, 0:ntok], T["ones1f"][:], T["stat2"].ap()[:, 0:ntok],
        start=True, stop=True), waits=[("sV", rs), ("sV", vr)], inc="sP",
        tag=f"{tagp}bc1")
    b2v = P_.emit("pe", lambda e: e.matmul(
        pbE[:, 512:512 + ntok], T["ones1f"][:], T["stat1"].ap()[:, 0:ntok],
        start=True, stop=True), waits=[("sV", mr)], inc="sP", tag=f"{tagp}bc2")
    last = None
    for k in range(KC):
        P_.emit("dve", lambda e, k=k: e.tensor_tensor(
            out=src(k), in0=src(k), in1=pbE[:, 0:ntok], op=ALU.mult),
            waits=[("sP", b2v)] if k == 0 else [], inc="sV", tag=f"{tagp}ap1.{k}")
        last = P_.emit("dve", lambda e, k=k: e.tensor_tensor(
            out=src(k), in0=src(k), in1=pbE[:, 512:512 + ntok], op=ALU.subtract),
            inc="sV", tag=f"{tagp}ap2.{k}")
    return last


def _slab(P_, T, src_cols, half, waits, tag):
    return P_.emit("sync", lambda e: e.dma_start(
        out=T["AR_W"].ap()[:, half * 2048:(half + 1) * 2048].rearrange(
            "p (c t) -> p c t", c=16),
        in_=src_cols.rearrange("(c p) m -> p c m", p=128)),
        waits=waits, inc="dS", tag=tag)


def _attn1(P_, T, DR, qstep):
    pbG, pbH = T["pbG"], T["pbH"]
    lnC = lambda k: T["AR_C"].ap()[:, k * 512:(k + 1) * 512]
    slab_free = {0: 0, 1: 0}
    for tb in range(4):
        xl = P_.emit("sync", lambda e, tb=tb: e.dma_start(
            out=T["AR_C"].ap().rearrange("p (c t) -> p c t", c=16),
            in_=DR["xT"].rearrange("(c p) t -> p c t", p=128)[
                :, :, tb * 512:(tb + 1) * 512]),
            waits=[("sP", P_.cnt["sP"]), ("sV", P_.cnt["sV"])],
            inc="dS", tag=f"a1_x{tb}")
        ln_done = _ln_block(P_, T, lnC, 512, LN_EPS, f"a1ln{tb}_",
                            first_waits=[xl])
        ev_prev = {}
        st_prev = {}
        vst_prev = {}
        for kind in ("k", "v", "q"):
            wname = {"k": "wkT", "v": "wvT", "q": "wqT"}[kind]
            for m in range(16):
                half = m % 2
                sl = _slab(P_, T, DR[wname][:, m * 128:(m + 1) * 128], half,
                           waits=[("sP", slab_free[half])], tag=f"a1w{kind}{tb}.{m}")
                qstep(kind == "k" and m == 0)
                bank = {"k": pbG[:, 0:512], "v": pbG[:, 512:1024],
                        "q": pbH[:, 0:256]}[kind]
                mm = None
                for k in range(KC):
                    ws = [sl, ("sV", ln_done)] if k == 0 else []
                    if k == 0 and kind in ev_prev:
                        ws.append(("sA", ev_prev[kind]))
                    if kind == "q":
                        rhs = (T["AR_C"].ap()[:, k * 512:(k + 1) * 512]
                               .rearrange("p (a b t) -> p a b t", a=2, b=2)
                               [:, :, 0:1, :])
                    else:
                        rhs = T["AR_C"].ap()[:, k * 512:(k + 1) * 512]
                    mm = P_.emit("pe", lambda e, bank=bank, half=half, k=k, rhs=rhs:
                                 e.matmul(bank,
                                          T["AR_W"].ap()[:, half * 2048 + k * 128:
                                                         half * 2048 + (k + 1) * 128],
                                          rhs, start=(k == 0), stop=(k == KC - 1)),
                                 waits=ws, inc="sP", tag=f"a1{kind}{tb}.{m}.{k}")
                slab_free[half] = mm
                if kind in ("k", "q"):
                    sh, n = (0, 512) if kind == "k" else (1, 256)
                    evw = [("sP", mm)]
                    if kind in st_prev:
                        evw.append(st_prev[kind])
                    ev = P_.emit("act", lambda e, bank=bank, sh=sh, n=n: e.activation(
                        out=T["stg"].ap()[:, sh * 512:sh * 512 + n], in_=bank,
                        func=AF.Copy), waits=evw, inc="sA",
                        tag=f"a1ev{kind}{tb}.{m}")
                    dst = (DR["k_scr"][m * 128:(m + 1) * 128, tb * 512:(tb + 1) * 512]
                           if kind == "k" else
                           DR["q_scr"][m * 128:(m + 1) * 128, tb * 256:(tb + 1) * 256])
                    stv = P_.emit("act", lambda e, dst=dst, sh=sh, n=n: e.dma_start(
                        out=dst, in_=T["stg"].ap()[:, sh * 512:sh * 512 + n]),
                        waits=[], inc="dC", tag=f"a1st{kind}{tb}.{m}")
                    st_prev[kind] = stv
                    ev_prev[kind] = ev
                else:
                    evw = [("sP", mm)]
                    if "v_tr" in st_prev:
                        evw.append(("sP", st_prev["v_tr"]))
                    ev = P_.emit("act", lambda e, bank=bank: e.activation(
                        out=T["stg"].ap()[:, 512:1024], in_=bank, func=AF.Copy),
                        waits=evw, inc="sA", tag=f"a1evv{tb}.{m}")
                    ev_prev[kind] = ev
                    tr_prev = None
                    for kt in range(4):
                        trb = pbH[:, 512:640]
                        ws = [("sA", ev)] + list(P_.consts_ok)
                        if tr_prev is not None:
                            ws.append(("sV", tr_prev))
                        elif vst_prev:
                            ws.append(("sV", vst_prev["last_te"]))
                        tr = P_.emit("pe", lambda e, trb=trb, kt=kt: e.transpose(
                            trb, T["stg"].ap()[:, 512 + kt * 128:512 + (kt + 1) * 128],
                            T["ident"][:]), waits=ws, inc="sP",
                            tag=f"a1tr{tb}.{m}.{kt}")
                        pcol = 128 * (kt % 2)
                        tvw = [("sP", tr)]
                        if (m, kt - 2) in vst_prev:
                            tvw.append(vst_prev[(m, kt - 2)])
                        elif (m - 1, kt + 2) in vst_prev:
                            tvw.append(vst_prev[(m - 1, kt + 2)])
                        tvv = P_.emit("dve", lambda e, trb=trb, pcol=pcol:
                                      e.tensor_copy(
                                          out=T["ptb"].ap()[:, pcol:pcol + 128],
                                          in_=trb),
                                      waits=tvw, inc="sV", tag=f"a1te{tb}.{m}.{kt}")
                        tr_prev = tvv
                        stv = P_.emit(
                            "act", lambda e, tb=tb, m=m, kt=kt, pcol=pcol:
                            e.dma_start(
                                out=DR["v_scr"][tb * 512 + kt * 128:
                                                tb * 512 + (kt + 1) * 128,
                                                m * 128:(m + 1) * 128],
                                in_=T["ptb"].ap()[:, pcol:pcol + 128]),
                            waits=[("sV", tvv)], inc="dC", tag=f"a1sv{tb}.{m}.{kt}")
                        vst_prev[(m, kt)] = stv
                        vst_prev["last_st"] = stv
                        vst_prev["last_te"] = tvv
                    st_prev["v_tr"] = tr


def _attn2(P_, T, DR):
    BAP = T["AR_B"].ap()
    kh = BAP[:, 0:2048]
    vT = BAP[:, 2048:4096]
    qh = BAP[:, 4096:5120]
    prob = BAP[:, 5120:7168]
    msk = BAP[:, 7168:7168 + MSK_TOT]
    pbE, pbF, pbG, pbH = T["pbE"], T["pbF"], T["pbG"], T["pbH"]

    a1_done = P_.snap("dC")
    quant_bar = P_.snap("dQ") + P_.snap("dG") + [("sG", P_.cnt["sG"]),
                                                 ("sV", P_.cnt["sV"]),
                                                 ("sA", P_.cnt["sA"])]
    ml = P_.emit("sync", lambda e: e.dma_start(out=msk, in_=DR["qmask"]),
                 waits=list(quant_bar), inc="dS", tag="a2_msk")
    last_mm_on = {"kh": 0, "vT": 0, "qh": 0}
    for h in range(H):
        ws = (list(a1_done) + [ml] + list(quant_bar)) if h == 0 else []
        lk = P_.emit("sync", lambda e, h=h: e.dma_start(
            out=kh, in_=DR["k_scr"][h * 128:(h + 1) * 128, :]),
            waits=ws + [("sP", last_mm_on["kh"])], inc="dS", tag=f"a2lk{h}")
        lv = P_.emit("sync", lambda e, h=h: e.dma_start(
            out=vT.rearrange("p (c t) -> p c t", c=16),
            in_=DR["v_scr"].rearrange("(c p) d -> p c d", p=128)[
                :, :, h * 128:(h + 1) * 128]),
            waits=[("sP", last_mm_on["vT"])] if h else list(a1_done),
            inc="dS", tag=f"a2lv{h}")
        lq = P_.emit("sync", lambda e, h=h: e.dma_start(
            out=qh, in_=DR["q_scr"][h * 128:(h + 1) * 128, :]),
            waits=[("sP", last_mm_on["qh"])] if h else list(a1_done),
            inc="dS", tag=f"a2lq{h}")
        for qt in range(8):
            W = W_QT[qt]
            scb = [(pbE, 0), (pbE, 512), (pbF, 0), (pbF, 512)]
            mm = None
            for kb in range(W // 512):
                t_, c_ = scb[kb]
                ws = []
                if kb == 0:
                    ws = [lk, lq, ("sA", P_.cnt["sA"])]
                mm = P_.emit("pe", lambda e, t_=t_, c_=c_, kb=kb, qt=qt: e.matmul(
                    t_[:, c_:c_ + 512],
                    qh[:, qt * 128:(qt + 1) * 128],
                    kh[:, kb * 512:(kb + 1) * 512],
                    start=True, stop=True), waits=ws, inc="sP",
                    tag=f"a2sc{h}.{qt}.{kb}")
            lo, hi = 256 * qt, W
            mvs = []
            off = MSK_OFF[qt]
            seg = lo
            while seg < hi:
                segend = min(hi, (seg // 1024 + 1) * 1024)
                t_ = pbE if seg < 1024 else pbF
                c_ = seg % 1024
                n = segend - seg
                moff = off + (seg - lo)
                mvs.append(P_.emit("dve", lambda e, t_=t_, c_=c_, n=n, moff=moff:
                                   e.tensor_tensor(
                                       out=t_[:, c_:c_ + n], in0=t_[:, c_:c_ + n],
                                       in1=msk[:, moff:moff + n], op=ALU.add),
                                   waits=[("sP", mm)], inc="sV",
                                   tag=f"a2mk{h}.{qt}.{seg}"))
                seg = segend
            if W <= 1024:
                rmx = P_.emit("dve", lambda e, W=W: e.tensor_reduce(
                    out=T["negmax"][:], in_=pbE[:, 0:W], axis=AX.X, op=ALU.max,
                    negate=True), waits=[("sV", mvs[-1])], inc="sV",
                    tag=f"a2mx{h}.{qt}")
            else:
                P_.emit("dve", lambda e: e.tensor_reduce(
                    out=T["negmax"][:], in_=pbE[:, 0:1024], axis=AX.X, op=ALU.max,
                    negate=True), waits=[("sV", mvs[-1])], inc="sV",
                    tag=f"a2mxa{h}.{qt}")
                P_.emit("dve", lambda e, W=W: e.tensor_reduce(
                    out=T["bias1"][:], in_=pbF[:, 0:W - 1024], axis=AX.X, op=ALU.max,
                    negate=True), inc="sV", tag=f"a2mxb{h}.{qt}")
                rmx = P_.emit("dve", lambda e: e.tensor_tensor(
                    out=T["negmax"][:], in0=T["negmax"][:], in1=T["bias1"][:],
                    op=ALU.min), inc="sV", tag=f"a2mxc{h}.{qt}")
            bs = P_.emit("dve", lambda e: e.tensor_scalar(
                out=T["bias1"][:], in0=T["negmax"][:], scalar1=SCL_ATT,
                scalar2=None, op0=ALU.mult), waits=[("sV", rmx)], inc="sV",
                tag=f"a2bs{h}.{qt}")
            if W <= 1024:
                ex = P_.emit("act", lambda e, W=W: e.activation(
                    out=prob[:, 0:W], in_=pbE[:, 0:W], func=AF.Exp,
                    bias=T["bias1"][:], scale=SCL_ATT, accum_out=T["rowsum"][:]),
                    waits=[("sV", bs)], inc="sA", tag=f"a2ex{h}.{qt}")
                rsv = P_.emit("dve", lambda e: e.reciprocal(
                    out=T["recip"][:], in_=T["rowsum"][:]),
                    waits=[("sA", ex)], inc="sV", tag=f"a2rc{h}.{qt}")
            else:
                P_.emit("act", lambda e: e.activation(
                    out=prob[:, 0:1024], in_=pbE[:, 0:1024], func=AF.Exp,
                    bias=T["bias1"][:], scale=SCL_ATT, accum_out=T["rowsum"][:]),
                    waits=[("sV", bs)], inc="sA", tag=f"a2exa{h}.{qt}")
                ex = P_.emit("act", lambda e, W=W: e.activation(
                    out=prob[:, 1024:W], in_=pbF[:, 0:W - 1024], func=AF.Exp,
                    bias=T["bias1"][:], scale=SCL_ATT, accum_out=T["negmax"][:]),
                    inc="sA", tag=f"a2exb{h}.{qt}")
                sm = P_.emit("dve", lambda e: e.tensor_tensor(
                    out=T["rowsum"][:], in0=T["rowsum"][:], in1=T["negmax"][:],
                    op=ALU.add), waits=[("sA", ex)], inc="sV", tag=f"a2sm{h}.{qt}")
                rsv = P_.emit("dve", lambda e: e.reciprocal(
                    out=T["recip"][:], in_=T["rowsum"][:]), inc="sV",
                    tag=f"a2rc{h}.{qt}")
            pn = P_.emit("dve", lambda e, W=W: e.tensor_scalar(
                out=prob[:, 0:W], in0=prob[:, 0:W], scalar1=T["recip"][:],
                scalar2=None, op0=ALU.mult), waits=[("sV", rsv)], inc="sV",
                tag=f"a2pn{h}.{qt}")
            abank = pbH[:, 0:128] if qt % 2 == 0 else pbH[:, 512:640]
            te_prev = {}
            av_prev = {}
            mm_av = None
            for kc in range(W // 128):
                trb = pbG[:, 0:128] if kc % 2 == 0 else pbG[:, 512:640]
                ws = [("sV", pn)] + list(P_.consts_ok)
                if kc >= 2:
                    ws.append(("sV", te_prev[kc % 2]))
                tr = P_.emit("pe", lambda e, trb=trb, kc=kc: e.transpose(
                    trb, prob[:, kc * 128:(kc + 1) * 128], T["ident"][:]),
                    waits=ws, inc="sP", tag=f"a2tr{h}.{qt}.{kc}")
                pcol = 128 * (kc % 2)
                tew = [("sP", tr)]
                if kc - 2 in av_prev:
                    tew.append(("sP", av_prev[kc - 2]))
                te = P_.emit("dve", lambda e, trb=trb, pcol=pcol: e.tensor_copy(
                    out=T["ptb"].ap()[:, pcol:pcol + 128], in_=trb),
                    waits=tew, inc="sV", tag=f"a2te{h}.{qt}.{kc}")
                te_prev[kc % 2] = te
                ws2 = [("sV", te), lv]
                mm_av = P_.emit("pe", lambda e, kc=kc, pcol=pcol, abank=abank, W=W:
                                e.matmul(abank, vT[:, kc * 128:(kc + 1) * 128],
                                         T["ptb"].ap()[:, pcol:pcol + 128],
                                         start=(kc == 0), stop=(kc == W // 128 - 1)),
                                waits=ws2, inc="sP", tag=f"a2av{h}.{qt}.{kc}")
                av_prev[kc] = mm_av
            at_ = T["AR_C"] if h < 8 else T["AR_D"]
            hc = h % 8
            P_.emit("act", lambda e, hc=hc, qt=qt, abank=abank, at_=at_:
                    e.activation(
                        out=at_.ap()[:, hc * 1024 + qt * 128:
                                     hc * 1024 + (qt + 1) * 128],
                        in_=abank, func=AF.Copy), waits=[("sP", mm_av)], inc="sA",
                    tag=f"a2ae{h}.{qt}")
        last_mm_on = {"kh": P_.cnt["sP"], "vT": P_.cnt["sP"], "qh": P_.cnt["sP"]}


def _attn3(P_, T, DR):
    pbG = T["pbG"]
    oB = lambda m: T["AR_B"].ap()[:, m * 512:(m + 1) * 512]

    def a_ap(k, tb):
        at_ = T["AR_C"] if k < 8 else T["AR_D"]
        return at_.ap()[:, (k % 8) * 1024 + tb * 512:(k % 8) * 1024 + (tb + 1) * 512]

    slab_free = {0: P_.cnt["sP"], 1: P_.cnt["sP"]}
    for tb in range(2):
        a2_done = [("sA", P_.cnt["sA"])] if tb == 0 else P_.snap("dC")
        ev_prev = {}
        for m in range(16):
            half = m % 2
            sl = _slab(P_, T, DR["woT"][:, m * 128:(m + 1) * 128], half,
                       waits=[("sP", slab_free[half])], tag=f"a3w{tb}.{m}")
            bank = (pbG[:, 0:512], pbG[:, 512:1024])[m % 2]
            mm = None
            for k in range(KC):
                ws = ([sl] + (list(a2_done) if m == 0 else [])) if k == 0 else []
                if k == 0 and (m % 2) in ev_prev:
                    ws.append(("sA", ev_prev[m % 2]))
                if k == 0 and m == 0 and tb == 1:
                    ws.extend(P_.snap("dC"))
                mm = P_.emit("pe", lambda e, bank=bank, half=half, k=k, tb=tb:
                             e.matmul(bank,
                                      T["AR_W"].ap()[:, half * 2048 + k * 128:
                                                     half * 2048 + (k + 1) * 128],
                                      a_ap(k, tb),
                                      start=(k == 0), stop=(k == KC - 1)),
                             waits=ws, inc="sP", tag=f"a3o{tb}.{m}.{k}")
            slab_free[half] = mm
            ev_prev[m % 2] = P_.emit("act", lambda e, bank=bank, m=m: e.activation(
                out=oB(m), in_=bank, func=AF.Copy), waits=[("sP", mm)], inc="sA",
                tag=f"a3ev{tb}.{m}")
        ln_done = _ln_block(P_, T, oB, 512, LN_EPS, f"a3ln{tb}_",
                            first_waits=[("sA", P_.cnt["sA"])])
        ad_prev = {}
        for k in range(KC):
            sh = k % 2
            xlw = ([("sV", P_.cnt["sV"])] + P_.snap("dC")) if k < 2 else \
                  [("sV", ad_prev[k - 2])]
            xl = P_.emit("sync", lambda e, k=k, tb=tb, sh=sh: e.dma_start(
                out=T["stg"].ap()[:, sh * 512:(sh + 1) * 512].rearrange(
                    "p (a b t) -> p a b t", a=4, b=1),
                in_=DR["xT"][k * 128:(k + 1) * 128, :].rearrange(
                    "p (a b t) -> p a b t", a=8, b=2)[:, 4 * tb:4 * tb + 4, 0:1, :]),
                waits=xlw, inc="dS", tag=f"a3x{tb}.{k}")
            ad = P_.emit("dve", lambda e, k=k, sh=sh: e.tensor_tensor(
                out=oB(k), in0=oB(k), in1=T["stg"].ap()[:, sh * 512:(sh + 1) * 512],
                op=ALU.add),
                waits=[xl, ("sV", ln_done)], inc="sV", tag=f"a3x1{tb}.{k}")
            ad_prev[k] = ad
            P_.emit("act", lambda e, k=k, tb=tb: e.dma_start(
                out=DR["x1_scr"][k * 128:(k + 1) * 128, tb * 512:(tb + 1) * 512],
                in_=oB(k)), waits=[("sV", ad)], inc="dC", tag=f"a3s1{tb}.{k}")


def _ffn(P_, T, DR):
    pbE, pbF, pbG, pbH = T["pbE"], T["pbF"], T["pbG"], T["pbH"]
    lnD = lambda k: T["AR_D"].ap()[:, k * 256:(k + 1) * 256]
    lnY = lambda k: T["AR_D"].ap()[:, 4096 + k * 256: 4096 + (k + 1) * 256]
    l2h = lambda k: T["LN2H"].ap()[:, k * 256:(k + 1) * 256]
    l2l = lambda k: T["LN2L"].ap()[:, k * 256:(k + 1) * 256]
    hh = lambda c: T["HH"].ap()[:, c * 256:(c + 1) * 256]
    hl = lambda c: T["HL"].ap()[:, c * 256:(c + 1) * 256]
    wb = lambda q: T["AR_WB"].ap()[:, q * 2048:(q + 1) * 2048]

    x1_ok = P_.snap("dC") + P_.snap("dQ")
    for fb in range(4):
        xl = P_.emit("sync", lambda e, fb=fb: e.dma_start(
            out=T["AR_D"].ap()[:, 0:4096].rearrange("p (c t) -> p c t", c=16),
            in_=DR["x1_scr"].rearrange("(c p) t -> p c t", p=128)[
                :, :, fb * 256:(fb + 1) * 256]),
            waits=list(x1_ok) + [("sV", P_.cnt["sV"]), ("sP", P_.cnt["sP"])]
                  + P_.snap("dO"),
            inc="dS", tag=f"f_x1{fb}")
        ln2_done = _ln_block(P_, T, lnD, 256, LN_EPS, f"fln{fb}_",
                             first_waits=[xl])
        sp = None
        for k in range(KC):
            ws = [("sV", ln2_done), ("sP", P_.cnt["sP"])] if k == 0 else []
            P_.emit("dve", lambda e, k=k: e.tensor_copy(out=l2h(k), in_=lnD(k)),
                    waits=ws, inc="sV", tag=f"fsh{fb}.{k}")
            sp = P_.emit("dve", lambda e, k=k: e.tensor_tensor(
                out=l2l(k), in0=lnD(k), in1=l2h(k), op=ALU.subtract),
                inc="sV", tag=f"fsl{fb}.{k}")
        slabs = {}
        hsq_mm = None
        for c in range(64):
            r = c % 2
            gbank = (pbG[:, 0:256], pbG[:, 512:768])[r]
            ubank = (pbH[:, 0:256], pbH[:, 512:768])[r]
            wsl = [("sP", slabs[r])] if r in slabs else                 [("sP", P_.cnt["sP"])] + list(x1_ok)
            sl_g = P_.emit("sync", lambda e, c=c, r=r: e.dma_start(
                out=wb(2 * r).rearrange("p (kk t) -> p kk t", kk=16),
                in_=DR["wgu_hat"].rearrange("(kk p) m -> p kk m", p=128)[
                    :, :, c * 128:(c + 1) * 128]),
                waits=wsl, inc="dS", tag=f"fwg{fb}.{c}")
            sl_u = P_.emit("sync", lambda e, c=c, r=r: e.dma_start(
                out=wb(2 * r + 1).rearrange("p (kk t) -> p kk t", kk=16),
                in_=DR["wgu_hat"].rearrange("(kk p) m -> p kk m", p=128)[
                    :, :, 8192 + c * 128:8192 + (c + 1) * 128]),
                inc="dS", tag=f"fwu{fb}.{c}")
            mmu = None
            for k in range(KC):
                ws = [sl_g, sl_u, ("sV", sp)] if k == 0 else []
                for p, rhs_f in ((0, l2h), (1, l2l)):
                    P_.emit("pe", lambda e, gbank=gbank, r=r, k=k, rhs_f=rhs_f, p=p:
                            e.matmul(gbank, wb(2 * r)[:, k * 128:(k + 1) * 128],
                                     rhs_f(k), start=(k == 0 and p == 0),
                                     stop=(k == KC - 1 and p == 1)),
                            waits=ws if p == 0 else [], inc="sP",
                            tag=f"fg{fb}.{c}.{k}.{p}")
                    ws = []
                    mmu = P_.emit("pe", lambda e, ubank=ubank, r=r, k=k, rhs_f=rhs_f,
                                  p=p: e.matmul(
                                      ubank, wb(2 * r + 1)[:, k * 128:(k + 1) * 128],
                                      rhs_f(k), start=(k == 0 and p == 0),
                                      stop=(k == KC - 1 and p == 1)),
                                  inc="sP", tag=f"fu{fb}.{c}.{k}.{p}")
            slabs[r] = mmu
            sv = P_.emit("act", lambda e, gbank=gbank: e.activation(
                out=T["tmp"].ap()[:, 0:256], in_=gbank, func=AF.Silu),
                waits=[("sP", mmu)], inc="sA", tag=f"fsilu{fb}.{c}")
            ht = P_.emit("dve", lambda e, ubank=ubank: e.tensor_tensor(
                out=T["tmp"].ap()[:, 256:512], in0=T["tmp"].ap()[:, 0:256],
                in1=ubank, op=ALU.mult),
                waits=[("sA", sv)], inc="sV", tag=f"fht{fb}.{c}")
            P_.emit("dve", lambda e, c=c: e.tensor_copy(
                out=hh(c), in_=T["tmp"].ap()[:, 256:512]), inc="sV",
                tag=f"fhh{fb}.{c}")
            hv = P_.emit("dve", lambda e, c=c: e.tensor_tensor(
                out=hl(c), in0=T["tmp"].ap()[:, 256:512], in1=hh(c),
                op=ALU.subtract), inc="sV", tag=f"fhl{fb}.{c}")
            hq = P_.emit("act", lambda e: e.activation(
                out=T["LNXB"].ap()[:, 0:256], in_=T["tmp"].ap()[:, 256:512],
                func=AF.Square),
                waits=[("sV", ht)], inc="sA", tag=f"fhsq{fb}.{c}")
            hsq_mm = P_.emit("pe", lambda e, c=c: e.matmul(
                pbE[0:1, 0:256], T["ones1"][:], T["LNXB"].ap()[:, 0:256],
                start=(c == 0), stop=(c == 63)),
                waits=[("sA", hq)], inc="sP", tag=f"fhs{fb}.{c}")
        r1 = P_.emit("dve", lambda e: e.tensor_scalar(
            out=T["stat2"].ap()[:, 0:256], in0=pbE[0:1, 0:256], scalar1=1.0 / HF,
            scalar2=RMS_EPS, op0=ALU.mult, op1=ALU.add),
            waits=[("sP", hsq_mm)], inc="sV", tag=f"fr1{fb}")
        r2 = P_.emit("act", lambda e: e.activation(
            out=T["stat2"].ap()[:, 0:256], in_=T["stat2"].ap()[:, 0:256],
            func=AF.Sqrt), waits=[("sV", r1)], inc="sA", tag=f"fr2{fb}")
        r3 = P_.emit("dve", lambda e: e.reciprocal(
            out=T["stat2"].ap()[:, 0:256], in_=T["stat2"].ap()[:, 0:256]),
            waits=[("sA", r2)], inc="sV", tag=f"fr3{fb}")
        r4 = P_.emit("pe", lambda e: e.matmul(
            pbE[:, 512:768], T["ones1f"][:], T["stat2"].ap()[:, 0:256],
            start=True, stop=True), waits=[("sV", r3)], inc="sP", tag=f"fr4{fb}")
        r5 = P_.emit("dve", lambda e: e.tensor_copy(
            out=T["tmp"].ap()[:, 0:256], in_=pbE[:, 512:768]),
            waits=[("sP", r4)], inc="sV", tag=f"fr5{fb}")
        banks = [pbG[:, 0:256], pbG[:, 512:768], pbH[:, 0:256], pbH[:, 512:768],
                 pbE[:, 0:256], pbE[:, 512:768], pbF[:, 0:256], pbF[:, 512:768]]
        yv = {}
        for sw in range(2):
            for dm in range(8):
                dc = sw * 8 + dm
                bank = banks[dm]
                mm = None
                for ss in range(4):
                    half = ss % 2
                    wsl = [("sP", P_.cnt["sP"] if (sw or dm or ss) else 0)]
                    if dc == 0 and ss == 0:
                        wsl = [("sV", r5), ("sP", P_.cnt["sP"])]
                    sl = P_.emit("sync", lambda e, dc=dc, ss=ss, half=half:
                                 e.dma_start(
                                     out=wb(half).rearrange(
                                         "p (kk t) -> p kk t", kk=16),
                                     in_=DR["wp_hat"].rearrange(
                                         "(kk p) m -> p kk m", p=128)[
                                         :, ss * 16:(ss + 1) * 16,
                                         dc * 128:(dc + 1) * 128]),
                                 waits=wsl, inc="dS", tag=f"fpw{fb}.{dc}.{ss}")
                    for kk in range(16):
                        hk = ss * 16 + kk
                        ws = [sl] if kk == 0 else []
                        if dm == 0 and ss == 0 and kk == 0 and sw == 1:
                            ws.append(("sV", yv[7]))
                        for p, h_f in ((0, hh), (1, hl)):
                            mm = P_.emit("pe", lambda e, bank=bank, half=half,
                                         kk=kk, hk=hk, h_f=h_f, p=p: e.matmul(
                                             bank, wb(half)[:, kk * 128:
                                                            (kk + 1) * 128],
                                             h_f(hk), start=(hk == 0 and p == 0),
                                             stop=(hk == 63 and p == 1)),
                                         waits=ws if p == 0 else [], inc="sP",
                                         tag=f"fp{fb}.{dc}.{hk}.{p}")
                            ws = []
                yv[dm] = P_.emit("dve", lambda e, bank=bank, dc=dc: e.tensor_tensor(
                    out=lnY(dc), in0=bank, in1=T["tmp"].ap()[:, 0:256],
                    op=ALU.mult), waits=[("sP", mm)], inc="sV", tag=f"fy{fb}.{dc}")
        ln_done = _ln_block(P_, T, lnY, 256, LN_EPS, f"fyln{fb}_",
                            first_waits=[("sV", yv[7])])
        ad_prev = {}
        for k in range(KC):
            sh = k % 2
            xlw = ([("sV", P_.cnt["sV"])] + P_.snap("dC")) if k < 2 else \
                  [("sV", ad_prev[k - 2])]
            xl = P_.emit("sync", lambda e, k=k, fb=fb, sh=sh: e.dma_start(
                out=T["stg"].ap()[:, sh * 512:sh * 512 + 256],
                in_=DR["x1_scr"][k * 128:(k + 1) * 128, fb * 256:(fb + 1) * 256]),
                waits=xlw, inc="dS", tag=f"fx1{fb}.{k}")
            ad = P_.emit("dve", lambda e, k=k, sh=sh: e.tensor_tensor(
                out=lnY(k), in0=lnY(k), in1=T["stg"].ap()[:, sh * 512:sh * 512 + 256],
                op=ALU.add), waits=[xl, ("sV", ln_done)], inc="sV",
                tag=f"fx2{fb}.{k}")
            ad_prev[k] = ad
            P_.emit("sync", lambda e, k=k, fb=fb: e.dma_start(
                out=DR["out"][k * 128:(k + 1) * 128, fb * 256:(fb + 1) * 256],
                in_=lnY(k)), waits=[("sV", ad)], inc="dO", tag=f"fo{fb}.{k}")


def build_program():
    nc = bass.Bass("TRN2", target_bir_lowering=False)
    T, DR = _alloc(nc)
    P_ = Prog()
    phases = _DBG_PHASES or ("quant", "attn1", "attn2", "attn3", "ffn")

    import contextlib
    with contextlib.ExitStack() as stack:
        sems = {nm: stack.enter_context(nc.semaphore(nm)) for nm in Prog.SEMS}
        block = stack.enter_context(nc.Block())
        for nm in ("b2", "c2", "ones1", "ones1f", "ident"):
            P_.emit("sync", lambda e, nm=nm: e.dma_start(out=T[nm][:], in_=DR[nm]),
                    inc="dS", tag=f"c_{nm}")
        P_.consts_ok = P_.snap("dS")
        qg = iter(())
        if "quant" in phases:
            qg = _quant_gen(P_, T, DR, P_.consts_ok)
            next(qg, None)

        def qstep(bar=False):
            try:
                qg.send(bar)
            except StopIteration:
                pass

        if "attn1" in phases:
            _attn1(P_, T, DR, qstep)
        for _ in qg:
            pass
        if "attn2" in phases:
            _attn2(P_, T, DR)
        if "attn3" in phases:
            _attn3(P_, T, DR)
        if "ffn" in phases:
            _ffn(P_, T, DR)
        fin = []
        for cls in Prog.DCLS:
            fin.extend(P_.snap(cls))
        P_.emit("sync", None, waits=fin, tag="fin")
        P_.render(block, sems)
    return nc


def _host_inputs(inputs):
    x = np.asarray(inputs["L0_attn_x"], dtype=np.float32)
    wq = np.asarray(inputs["L0_attn_wq"], dtype=np.float32)
    wk = np.asarray(inputs["L0_attn_wk"], dtype=np.float32)
    wv = np.asarray(inputs["L0_attn_wv"], dtype=np.float32)
    wo = np.asarray(inputs["L0_attn_wo"], dtype=np.float32)
    wgu = np.asarray(inputs["L0_ffn_w_gate_up"], dtype=np.float32)
    wp = np.asarray(inputs["L0_ffn_w_proj"], dtype=np.float32)

    wqT = np.ascontiguousarray(wq.T)
    wkT = np.ascontiguousarray(wk.T)
    wvT = np.ascontiguousarray(wv.T)
    woT = np.ascontiguousarray(wo.T)
    wguT = np.ascontiguousarray(wgu.T)
    wpT = np.ascontiguousarray(wp.T)

    b2 = np.zeros((128, 2), dtype=bfloat16)
    b2[:64, 0] = 1
    b2[64:, 1] = 1
    c2 = np.ascontiguousarray(b2.T)
    ones1 = np.ones((128, 1), dtype=bfloat16)
    ones1f = np.ones((1, 128), dtype=np.float32)
    ident = np.eye(128, dtype=np.float32)

    in_maps = []
    for c in range(8):
        b, hf = c // 2, c % 2
        xt = np.ascontiguousarray(x[b].T)
        if hf == 1:
            xt = np.ascontiguousarray(
                xt.reshape(D, S // 256, 2, 128)[:, :, ::-1, :].reshape(D, S))
        qmask = np.zeros((128, MSK_TOT), dtype=np.float32)
        for i in range(8):
            grow = 256 * i + 128 * hf + np.arange(128)[:, None]
            lk = 256 * i + np.arange(MSK_W[i])[None, :]
            gk = ((lk // 128) ^ hf) * 128 + lk % 128
            qmask[:, MSK_OFF[i]:MSK_OFF[i + 1]] = np.where(gk <= grow, 0.0, NEG_BIG)
        in_maps.append(dict(
            xT=xt, wqT=wqT, wkT=wkT, wvT=wvT, woT=woT, wguT=wguT, wpT=wpT,
            qmask=qmask, b2=b2, c2=c2, ones1=ones1, ones1f=ones1f, ident=ident))
    return in_maps


def _gather(results):
    out = np.empty((B, S, D), dtype=np.float32)
    for c in range(8):
        b, hf = c // 2, c % 2
        o = results[c]["out"]
        ot = o.T
        for i in range(8):
            out[b, 256 * i + 128 * hf:256 * i + 128 * hf + 128, :] = \
                ot[i * 128:(i + 1) * 128, :]
    return out


def kernel(**inputs):
    in_maps = _host_inputs(inputs)
    nc = build_program()
    res = run_bass_kernel_spmd(nc, in_maps, core_ids=list(range(8)))
    return _gather(res.results)
